# version 01
# baseline (speedup 1.0000x reference)
import numpy as np
from contextlib import ExitStack

import concourse.bass as bass
import concourse.mybir as mybir
from concourse.bass_utils import run_bass_kernel_spmd

F32 = mybir.dt.float32
BF16 = mybir.dt.bfloat16
AF = mybir.ActivationFunctionType
ALU = mybir.AluOpType
AX = mybir.AxisListType

D = 1024
MD = 8
E = 2048
ME = E // 128
S = 128
NE = 32
II = 512
MI = II // 128
TOPK = 8
EPS = 1e-5


def build(L=16384, TOWN=4096, TB=2048, taps=False):
    assert L % 512 == 0 and TOWN % 512 == 0 and TB % 512 == 0 and TOWN % TB == 0
    nst = L // 512
    nst_own = TOWN // 512
    nqb = TOWN // 512
    G = L // 512
    ntb = TOWN // TB
    nch = TB // 512
    ntile_b = TB // 128
    NTL = L // 128
    INVL = 1.0 / L

    nc = bass.Bass("TRN2", target_bir_lowering=False)

    xb = nc.dram_tensor("xb", [L, D], F32, kind="ExternalInput").ap()
    W_in = nc.dram_tensor("L0_attn_W_in", [D, 2 * E + S], F32, kind="ExternalInput").ap()
    b_in = nc.dram_tensor("L0_attn_b_in", [2 * E + S], F32, kind="ExternalInput").ap()
    W_out = nc.dram_tensor("L0_attn_W_out", [E, D], F32, kind="ExternalInput").ap()
    b_out = nc.dram_tensor("L0_attn_b_out", [D], F32, kind="ExternalInput").ap()
    ln_g = nc.dram_tensor("L0_attn_ln_g", [D], F32, kind="ExternalInput").ap()
    ln_b = nc.dram_tensor("L0_attn_ln_b", [D], F32, kind="ExternalInput").ap()
    gq = nc.dram_tensor("L0_attn_gamma_q", [S], F32, kind="ExternalInput").ap()
    bq = nc.dram_tensor("L0_attn_beta_q", [S], F32, kind="ExternalInput").ap()
    gk = nc.dram_tensor("L0_attn_gamma_k", [S], F32, kind="ExternalInput").ap()
    bk = nc.dram_tensor("L0_attn_beta_k", [S], F32, kind="ExternalInput").ap()
    Wr = nc.dram_tensor("L0_ffn_Wr", [D, NE], F32, kind="ExternalInput").ap()
    br = nc.dram_tensor("L0_ffn_br", [NE], F32, kind="ExternalInput").ap()
    Wg_s = nc.dram_tensor("L0_ffn_Wg_s", [D, II], F32, kind="ExternalInput").ap()
    bg_s = nc.dram_tensor("L0_ffn_bg_s", [II], F32, kind="ExternalInput").ap()
    Wu_s = nc.dram_tensor("L0_ffn_Wu_s", [D, II], F32, kind="ExternalInput").ap()
    bu_s = nc.dram_tensor("L0_ffn_bu_s", [II], F32, kind="ExternalInput").ap()
    Wd_s = nc.dram_tensor("L0_ffn_Wd_s", [II, D], F32, kind="ExternalInput").ap()
    bd_s = nc.dram_tensor("L0_ffn_bd_s", [D], F32, kind="ExternalInput").ap()
    Wg = nc.dram_tensor("L0_ffn_Wg", [NE, D, II], F32, kind="ExternalInput").ap()
    bg = nc.dram_tensor("L0_ffn_bg", [NE, II], F32, kind="ExternalInput").ap()
    Wu = nc.dram_tensor("L0_ffn_Wu", [NE, D, II], F32, kind="ExternalInput").ap()
    bu = nc.dram_tensor("L0_ffn_bu", [NE, II], F32, kind="ExternalInput").ap()
    Wd = nc.dram_tensor("L0_ffn_Wd", [NE, II, D], F32, kind="ExternalInput").ap()
    bd = nc.dram_tensor("L0_ffn_bd", [NE, D], F32, kind="ExternalInput").ap()
    p_lng = nc.dram_tensor("p_lng", [128, MD], F32, kind="ExternalInput").ap()
    p_lnb = nc.dram_tensor("p_lnb", [128, MD], F32, kind="ExternalInput").ap()
    p_binu = nc.dram_tensor("p_binu", [128, ME], F32, kind="ExternalInput").ap()
    p_bgs = nc.dram_tensor("p_bgs", [128, MI], F32, kind="ExternalInput").ap()
    p_bus = nc.dram_tensor("p_bus", [128, MI], F32, kind="ExternalInput").ap()
    p_bge = nc.dram_tensor("p_bge", [128, NE, MI], F32, kind="ExternalInput").ap()
    p_bue = nc.dram_tensor("p_bue", [128, NE, MI], F32, kind="ExternalInput").ap()
    out = nc.dram_tensor("out", [TOWN, D], F32, kind="ExternalOutput").ap()
    vsp = nc.dram_tensor("vsp", [L, E], BF16, kind="Internal").ap()
    x2sp = nc.dram_tensor("x2sp", [nst_own, 128, MD, 512], BF16, kind="Internal").ap()
    ysp = nc.dram_tensor("ysp", [TOWN, D], F32, kind="Internal").ap()
    if taps:
        t_kT = nc.dram_tensor("t_kT", [128, L], BF16, kind="ExternalOutput").ap()
        t_qT = nc.dram_tensor("t_qT", [128, TOWN], BF16, kind="ExternalOutput").ap()
        t_w = nc.dram_tensor("t_w", [128, TOWN // 128, NE], F32, kind="ExternalOutput").ap()
        t_y = nc.dram_tensor("t_y", [TOWN, D], F32, kind="ExternalOutput").ap()
        t_v = nc.dram_tensor("t_v", [L, E], BF16, kind="ExternalOutput").ap()
        t_x2 = nc.dram_tensor("t_x2", [nst_own, 128, MD, 512], BF16, kind="ExternalOutput").ap()

    with ExitStack() as ctx:
        EC = ctx.enter_context

        class Sem:
            def __init__(self, name):
                self.h = EC(nc.semaphore(name))
                self.n = 0

        s_wg = Sem("s_wg")
        s_we = [Sem("s_we0"), Sem("s_we1")]
        s_gp = Sem("s_gp")
        s_x = [Sem("s_x0"), Sem("s_x1")]
        s_tr = [Sem("s_tr0"), Sem("s_tr1")]
        s_vw = [Sem("s_vw0"), Sem("s_vw1")]
        s_x2w = Sem("s_x2w")
        s_x2r = [Sem("s_x2r0"), Sem("s_x2r1")]
        s_vr = [Sem("s_vr0"), Sem("s_vr1")]
        s_yw = [Sem("s_yw0"), Sem("s_yw1")]
        s_yr = [Sem("s_yr0"), Sem("s_yr1")]
        s_x3 = [Sem("s_x30"), Sem("s_x31")]
        s_tr3 = [Sem("s_tr30"), Sem("s_tr31")]
        s_x4 = [Sem("s_x40"), Sem("s_x41")]
        s_out = [Sem("s_out0"), Sem("s_out1")]
        s_tap = Sem("s_tap")
        s_mm = Sem("s_mm")
        s_sc = Sem("s_sc")
        s_v = Sem("s_v")

        ones_row = EC(nc.sbuf_tensor("ones_row", [1, 128], BF16))
        eps_pp = EC(nc.sbuf_tensor("eps_pp", [128, 1], F32))
        invl_pp = EC(nc.sbuf_tensor("invl_pp", [128, 1], F32))
        idn = EC(nc.sbuf_tensor("idn", [128, 128], F32))
        bnsc = EC(nc.sbuf_tensor("bnsc", [128, 12], F32))
        mv_all = EC(nc.sbuf_tensor("mv_all", [128, NTL, 2], F32))
        mu_all = EC(nc.sbuf_tensor("mu_all", [128, NTL], F32))
        r_all = EC(nc.sbuf_tensor("r_all", [128, NTL], F32))
        rr_all = EC(nc.sbuf_tensor("rr_all", [128, NTL], F32))
        r1_all = EC(nc.sbuf_tensor("r1_all", [128, NTL], F32))
        mur_all = EC(nc.sbuf_tensor("mur_all", [128, NTL], F32))
        sd_all = EC(nc.sbuf_tensor("sd_all", [128, NTL], F32))
        v2_all = EC(nc.sbuf_tensor("v2_all", [128, NTL], F32))
        lng_pp = EC(nc.sbuf_tensor("lng_pp", [128, MD], F32))
        lnb_pp = EC(nc.sbuf_tensor("lnb_pp", [128, MD], F32))
        binz_pp = EC(nc.sbuf_tensor("binz_pp", [128, 1], F32))
        binu_pp = EC(nc.sbuf_tensor("binu_pp", [128, ME], F32))
        binv_row = EC(nc.sbuf_tensor("binv_row", [1, E], BF16))
        bout_row = EC(nc.sbuf_tensor("bout_row", [1, D], BF16))
        br_row = EC(nc.sbuf_tensor("br_row", [1, NE], BF16))
        gq_pp = EC(nc.sbuf_tensor("gq_pp", [128, 1], F32))
        bq_pp = EC(nc.sbuf_tensor("bq_pp", [128, 1], F32))
        gk_pp = EC(nc.sbuf_tensor("gk_pp", [128, 1], F32))
        bk_pp = EC(nc.sbuf_tensor("bk_pp", [128, 1], F32))
        bgs_pp = EC(nc.sbuf_tensor("bgs_pp", [128, MI], F32))
        bus_pp = EC(nc.sbuf_tensor("bus_pp", [128, MI], F32))
        bge_pp = EC(nc.sbuf_tensor("bge_pp", [128, NE, MI], F32))
        bue_pp = EC(nc.sbuf_tensor("bue_pp", [128, NE, MI], F32))
        bd33 = EC(nc.sbuf_tensor("bd33", [33, D], BF16))
        Wr_sb = EC(nc.sbuf_tensor("Wr_sb", [128, MD, NE], BF16))

        ops = {k: [] for k in ("sync", "gpsimd", "vector", "scalar", "tensor")}

        def go(eng, emitfn, waits=(), inc=None, k=1):
            waits = tuple((s.h, int(v)) for (s, v) in waits if v > 0)

            def run(x, emitfn=emitfn, waits=waits, inc=inc, k=k):
                for (h, v) in waits:
                    x.wait_ge(h, v)
                r = emitfn(x)
                if inc is not None:
                    r.then_inc(inc.h, k)
            ops[eng].append(run)
            if inc is not None:
                inc.n += k

        def vop(emitfn, waits=()):
            go("vector", emitfn, [(s_v, s_v.n)] + list(waits), inc=s_v)
            return s_v.n

        def sop(emitfn, waits=()):
            go("scalar", emitfn, [(s_sc, s_sc.n)] + list(waits), inc=s_sc)
            return s_sc.n

        def gop(emitfn, waits=()):
            go("gpsimd", emitfn, [(s_gp, s_gp.n)] + list(waits), inc=s_gp)
            return s_gp.n

        def flush_block():
            with nc.Block() as block:
                for name in ("sync", "gpsimd", "vector", "scalar", "tensor"):
                    lst = ops[name]
                    if not lst:
                        continue

                    def body(engine, lst=lst):
                        for f in lst:
                            f(engine)
                    getattr(block, name)(body)
            for k in ops:
                ops[k] = []

        gop(lambda g: g.memset(ones_row[:], 1.0))
        gop(lambda g: g.memset(eps_pp[:], EPS))
        gop(lambda g: g.memset(invl_pp[:], INVL))
        gop(lambda g: g.memset(idn[:], 0.0))
        gop(lambda g: g.affine_select(out=idn[:], in_=idn[:], compare_op=ALU.not_equal,
                                      fill=1.0, base=0, pattern=[[-1, 128]], channel_multiplier=1))

        def wload(dst_ap, src_ap, sem, waits=()):
            go("gpsimd", lambda g, dd=dst_ap, ss=src_ap: g.dma_start(out=dd, in_=ss),
               waits=waits, inc=sem, k=16)

        wload(lng_pp[:], p_lng[:, :], s_wg)
        wload(lnb_pp[:], p_lnb[:, :], s_wg)
        wload(binz_pp[:], b_in[2 * E:2 * E + S, None], s_wg)
        wload(binu_pp[:], p_binu[:, :], s_wg)
        wload(binv_row[:], b_in[None, E:2 * E], s_wg)
        wload(bout_row[:], b_out[None, :], s_wg)
        wload(br_row[:], br[None, :], s_wg)
        wload(gq_pp[:], gq[:, None], s_wg)
        wload(bq_pp[:], bq[:, None], s_wg)
        wload(gk_pp[:], gk[:, None], s_wg)
        wload(bk_pp[:], bk[:, None], s_wg)
        wload(bgs_pp[:], p_bgs[:, :], s_wg)
        wload(bus_pp[:], p_bus[:, :], s_wg)
        wload(bge_pp[:], p_bge[:, :, :], s_wg)
        wload(bue_pp[:], p_bue[:, :, :], s_wg)
        wload(bd33[0:32, :], bd[:, :], s_wg)
        wload(bd33[32:33, :], bd_s[None, :], s_wg)
        wload(Wr_sb[:], Wr.rearrange("(m p) e -> p m e", p=128), s_wg)
        GP0 = s_gp.n
        flush_block()

        with ExitStack() as ph12:
            E12 = ph12.enter_context
            kT = E12(nc.sbuf_tensor("kT", [128, L], BF16))
            qT = E12(nc.sbuf_tensor("qT", [128, TOWN], BF16))
            ph1 = ExitStack()
            E11 = ph1.enter_context
            Wz_sb = E11(nc.sbuf_tensor("Wz_sb", [128, MD, S], BF16))
            Wv_sb = E11(nc.sbuf_tensor("Wv_sb", [128, MD, E], BF16))
            x_sb = [E11(nc.sbuf_tensor(f"x_sb{i}", [128, 4, D], F32)) for i in range(2)]
            xn2_bf = [E11(nc.sbuf_tensor(f"xn2_bf{i}", [128, 4, D], BF16)) for i in range(2)]
            xn2T = [E11(nc.sbuf_tensor(f"xn2T{i}", [128, MD, 512], BF16)) for i in range(2)]
            xn2Ta = [E11(nc.sbuf_tensor(f"xn2Ta{i}", [128, MD, 512], BF16)) for i in range(2)]
            tmpz = E11(nc.sbuf_tensor("tmpz", [128, 512], F32))
            v_sb = [E11(nc.sbuf_tensor(f"v_sb{i}", [128, E], BF16)) for i in range(2)]
            ps_z = E11(nc.psum_tensor("ps_z", [128, 512], F32))
            ps_v = [E11(nc.psum_tensor(f"ps_v{i}", [128, 512], F32)) for i in range(2)]

            wload(Wz_sb[:], W_in[:, 2 * E:2 * E + S].rearrange("(m p) e -> p m e", p=128), s_wg)
            wload(Wv_sb[:], W_in[:, E:2 * E].rearrange("(m p) e -> p m e", p=128), s_wg)
            WG_ALL = s_wg.n

            last_bn = {}
            for st in range(nst):
                par = st % 2
                xw = [] if st < 2 else [(s_v, last_bn[st - 2])]
                go("sync", lambda y, st=st, par=par:
                   y.dma_start(out=x_sb[par][:],
                               in_=xb[st * 512:(st + 1) * 512, :].rearrange("(s p) d -> p s d", p=128)),
                   waits=xw, inc=s_x[par], k=16)
                nx = s_x[par].n
                for s in range(4):
                    t = st * 4 + s
                    vop(lambda v, par=par, s=s: v.bn_stats(out=bnsc[:, 0:6], in_=x_sb[par][:, s, 0:512]),
                        waits=[(s_x[par], nx)])
                    vop(lambda v, par=par, s=s: v.bn_stats(out=bnsc[:, 6:12], in_=x_sb[par][:, s, 512:1024]))
                    last_bn[st] = vop(lambda v, t=t: v.bn_aggr(out=mv_all[:, t, :], in_=bnsc[:]))
            _sd = sop(lambda sc: sc.activation(out=sd_all[:], in_=mv_all[:, :, 1], func=AF.Sqrt,
                                               bias=eps_pp[:]),
                      waits=[(s_v, last_bn[nst - 1]), (s_gp, GP0)])
            _r = vop(lambda v: v.reciprocal(out=r_all[:], in_=sd_all[:]), waits=[(s_sc, _sd)])
            vop(lambda v: v.tensor_tensor(out=v2_all[:], in0=mv_all[:, :, 1], in1=r_all[:], op=ALU.mult))
            vv2 = vop(lambda v: v.tensor_tensor(out=v2_all[:], in0=v2_all[:], in1=r_all[:], op=ALU.mult))
            _s2 = sop(lambda sc: sc.activation(out=v2_all[:], in_=v2_all[:], func=AF.Sqrt, bias=eps_pp[:]),
                      waits=[(s_v, vv2)])
            vop(lambda v: v.reciprocal(out=rr_all[:], in_=v2_all[:]), waits=[(s_sc, _s2)])
            vop(lambda v: v.tensor_tensor(out=rr_all[:], in0=rr_all[:], in1=r_all[:], op=ALU.mult))
            vop(lambda v: v.tensor_copy(out=mu_all[:], in_=mv_all[:, :, 0]))
            vop(lambda v: v.tensor_scalar(out=r1_all[:], in0=r_all[:], scalar1=1.0, scalar2=None, op0=ALU.add))
            STATS_DONE = vop(lambda v: v.tensor_tensor(out=mur_all[:], in0=mu_all[:], in1=r_all[:], op=ALU.mult))
            flush_block()

            lastcons = {}
            tz_cnt = {}
            mm_z = {}
            mm_v = {}
            sc_v = {}
            spill_done = {}
            aff_done = {}
            x2w_cnt = {}
            for st in range(nst):
                par = st % 2
                own = st < nst_own
                xw = [(s_v, STATS_DONE)] if st < 2 else [(s_v, lastcons[st - 2])]
                go("sync", lambda y, st=st, par=par:
                   y.dma_start(out=x_sb[par][:],
                               in_=xb[st * 512:(st + 1) * 512, :].rearrange("(s p) d -> p s d", p=128)),
                   waits=xw, inc=s_x[par], k=16)
                nx = s_x[par].n
                trw = [] if st < 2 else [(s_tr[par], 16 * 4 * (st // 2))]
                xn2cnt = {}
                for s in range(4):
                    t = st * 4 + s
                    xn2cnt[s] = vop(lambda v, par=par, s=s, t=t:
                                    v.tensor_scalar(out=xn2_bf[par][:, s, :], in0=x_sb[par][:, s, :],
                                                    scalar1=mu_all[:, t:t + 1], scalar2=rr_all[:, t:t + 1],
                                                    op0=ALU.subtract, op1=ALU.mult),
                                    waits=[(s_x[par], nx)] + trw)
                lastcons[st] = xn2cnt[3]
                afw = [] if st < 2 else [(s_v, aff_done[st - 2])]
                for s in range(4):
                    go("sync", lambda y, par=par, s=s:
                       y.dma_start(out=xn2T[par][:, :, s * 128:(s + 1) * 128],
                                   in_=xn2_bf[par][:, s, :], transpose=True),
                       waits=[(s_v, xn2cnt[s])] + afw, inc=s_tr[par], k=16)
                ntr = s_tr[par].n
                pw = [] if st < 2 else [(s_mm, mm_v[st - 2])]
                if st == 0:
                    pw.append((s_wg, WG_ALL))
                if st >= 2 and (st - 2) < nst_own:
                    pw.append((s_x2w, x2w_cnt[st - 2]))
                for m in range(MD):
                    aff_done[st] = vop(lambda v, par=par, m=m:
                                       v.tensor_scalar(out=xn2Ta[par][:, m, :], in0=xn2T[par][:, m, :],
                                                       scalar1=lng_pp[:, m:m + 1], scalar2=lnb_pp[:, m:m + 1],
                                                       op0=ALU.mult, op1=ALU.add),
                                       waits=[(s_tr[par], ntr)] + pw)
                if own:
                    go("sync", lambda y, par=par, st=st:
                       y.dma_start(out=x2sp[st, :, :, :], in_=xn2Ta[par][:]),
                       waits=[(s_v, aff_done[st])], inc=s_x2w, k=16)
                    x2w_cnt[st] = s_x2w.n
                zw = [(s_v, aff_done[st])]
                if st == 0:
                    zw.append((s_wg, WG_ALL))
                if st >= 1:
                    zw.append((s_sc, tz_cnt[st - 1]))

                def emit_z(tensor, par=par, st=st):
                    for m in range(MD):
                        r = tensor.matmul(ps_z[:], Wz_sb[:, m, :], xn2Ta[par][:, m, :],
                                          start=(m == 0), stop=(m == MD - 1))
                    return r
                go("tensor", emit_z, waits=zw, inc=s_mm)
                mm_z[st] = s_mm.n
                tz_cnt[st] = tz = sop(
                    lambda sc: sc.activation(out=tmpz[:], in_=ps_z[:], func=AF.Silu, bias=binz_pp[:]),
                    waits=[(s_mm, mm_z[st])] + ([] if st < 1 else [(s_v, lastkq[st - 1])]))
                lastkq = locals().get("lastkq", {})
                lastkq[st] = vop(lambda v, st=st:
                                 v.tensor_scalar(out=kT[:, st * 512:(st + 1) * 512], in0=tmpz[:],
                                                 scalar1=gk_pp[:], scalar2=bk_pp[:],
                                                 op0=ALU.mult, op1=ALU.add),
                                 waits=[(s_sc, tz)])
                if own:
                    lastkq[st] = vop(lambda v, st=st:
                                     v.tensor_scalar(out=qT[:, st * 512:(st + 1) * 512], in0=tmpz[:],
                                                     scalar1=gq_pp[:], scalar2=bq_pp[:],
                                                     op0=ALU.mult, op1=ALU.add))
                for s in range(4):
                    t = st * 4 + s
                    spar = t % 2
                    for ec in range(4):
                        vpar = (s * 4 + ec) % 2
                        slot = st * 16 + s * 4 + ec
                        mw = []
                        if slot >= 2:
                            mw.append((s_sc, sc_v[slot - 2]))

                        def emit_v(tensor, par=par, s=s, ec=ec, vpar=vpar):
                            for m in range(MD):
                                tensor.matmul(ps_v[vpar][:], xn2Ta[par][:, m, s * 128:(s + 1) * 128],
                                              Wv_sb[:, m, ec * 512:(ec + 1) * 512],
                                              start=(m == 0), stop=False)
                            return tensor.matmul(ps_v[vpar][:], ones_row[:],
                                                 binv_row[:, ec * 512:(ec + 1) * 512],
                                                 start=False, stop=True)
                        go("tensor", emit_v, waits=mw, inc=s_mm)
                        mm_v[st] = s_mm.n
                        sw = [(s_mm, mm_v[st])]
                        if t >= 2:
                            sw.append((s_vw[spar], 16 * (t // 2)))
                        sc_v[slot] = sop(lambda sc, vpar=vpar, spar=spar, ec=ec:
                                         sc.activation(out=v_sb[spar][:, ec * 512:(ec + 1) * 512],
                                                       in_=ps_v[vpar][:], func=AF.Silu),
                                         waits=sw)
                    go("sync", lambda y, spar=spar, t=t:
                       y.dma_start(out=vsp[t * 128:(t + 1) * 128, :], in_=v_sb[spar][:]),
                       waits=[(s_sc, sc_v[st * 16 + s * 4 + 3])], inc=s_vw[spar], k=16)
                    spill_done[t] = (spar, s_vw[spar].n)
            go("sync", lambda y: y.nop(),
               waits=[(s_vw[0], s_vw[0].n), (s_vw[1], s_vw[1].n), (s_x2w, s_x2w.n)])
            if taps:
                go("sync", lambda y: y.dma_start(out=t_kT[:, :], in_=kT[:]),
                   waits=[(s_v, s_v.n)], inc=s_tap, k=16)
                go("sync", lambda y: y.dma_start(out=t_qT[:, :], in_=qT[:]), inc=s_tap, k=16)
                go("sync", lambda y: y.dma_start(out=t_v[:, :], in_=vsp[:, :]), inc=s_tap, k=16)
                go("sync", lambda y: y.dma_start(out=t_x2[:, :, :, :], in_=x2sp[:, :, :, :]), inc=s_tap, k=16)
                go("sync", lambda y: y.nop(), waits=[(s_tap, s_tap.n)])
            flush_block()
            ph1.close()

            with ExitStack() as ph2:
                E2 = ph2.enter_context
                Wu_sb = E2(nc.sbuf_tensor("Wu_sb", [128, MD, E], BF16))
                Wout_sb = E2(nc.sbuf_tensor("Wout_sb", [128, ME, D], BF16))
                x2_sb = [E2(nc.sbuf_tensor(f"x2_sb{i}", [128, MD, 512], BF16)) for i in range(2)]
                v_buf = [E2(nc.sbuf_tensor(f"v_buf{i}", [128, 4, 512], BF16)) for i in range(2)]
                srelu = [E2(nc.sbuf_tensor(f"srelu{i}", [128, 2048], BF16)) for i in range(2)]
                ssq = [E2(nc.sbuf_tensor(f"ssq{i}", [128, 2048], BF16)) for i in range(2)]
                uT_eq = E2(nc.sbuf_tensor("uT_eq", [128, 4, 512], BF16))
                uavT = E2(nc.sbuf_tensor("uavT", [128, ME, 512], BF16))
                y_sb = [E2(nc.sbuf_tensor(f"y_sb{i}", [128, D], F32)) for i in range(2)]
                ps_s = E2(nc.psum_tensor("ps_s", [128, 2048], F32))
                ps_av = E2(nc.psum_tensor("ps_av", [128, 2048], F32))

                wload(Wu_sb[:], W_in[:, 0:E].rearrange("(m p) e -> p m e", p=128), s_wg)
                wload(Wout_sb[:], W_out.rearrange("(m p) d -> p m d", p=128), s_wg)
                WG_PH2 = s_wg.n

                relu_cnt = {}
                sq_cnt = {}
                mm1_cnt = {}
                mm2_cnt = {}
                uav_cnt = {}
                usilu_cnt = {}
                wout_cnt = {}
                ycpy_cnt = {}
                vld_cnt = {}
                yw_i = 0
                for qb in range(nqb):
                    qpar = qb % 2
                    xw = [(s_x2w, s_x2w.n)]
                    if qb >= 2:
                        xw.append((s_mm, usilu_cnt[qb - 2][1]))
                    go("sync", lambda y, qb=qb, qpar=qpar:
                       y.dma_start(out=x2_sb[qpar][:], in_=x2sp[qb, :, :, :]),
                       waits=xw, inc=s_x2r[qpar], k=16)
                    nx2 = s_x2r[qpar].n
                    git = 0
                    for eq in range(4):
                        for es in range(4):
                            eu = eq * 4 + es
                            upar = es % 2
                            uw = [(s_x2r[qpar], nx2)]
                            if qb == 0 and eq == 0 and es == 0:
                                uw.append((s_wg, WG_PH2))
                            if eq > 0 or qb > 0:
                                uw.append((s_sc, s_sc.n))
                            if es >= 2:
                                uw.append((s_sc, usil[es - 2]))

                            def emit_u(tensor, qpar=qpar, es=es, upar=upar, eu=eu):
                                for m in range(MD):
                                    r = tensor.matmul(ps_s[:, upar * 512:(upar + 1) * 512],
                                                      Wu_sb[:, m, eu * 128:(eu + 1) * 128],
                                                      x2_sb[qpar][:, m, :],
                                                      start=(m == 0), stop=(m == MD - 1))
                                return r
                            go("tensor", emit_u, waits=uw, inc=s_mm)
                            numm = s_mm.n
                            usil = locals().get("usil", {})
                            suw = [(s_mm, numm)]
                            if eq > 0:
                                suw.append((s_v, uav_cnt[(qb, eq - 1)]))
                            elif qb > 0:
                                suw.append((s_v, uav_cnt[(qb - 1, 3)]))
                            usil[es] = sop(lambda sc, upar=upar, es=es, eu=eu:
                                           sc.activation(out=uT_eq[:, es, :],
                                                         in_=ps_s[:, upar * 512:(upar + 1) * 512],
                                                         func=AF.Silu, bias=binu_pp[:, eu:eu + 1]),
                                           waits=suw)
                        usilu_cnt[qb] = (usil[3], s_mm.n)
                        for g in range(G):
                            gpar = g % 2
                            kt0 = g * 4
                            vw = []
                            if g >= 2:
                                vw.append((s_mm, mm2_cnt[(qb, eq, g - 2)]))
                            elif (qb, eq) != (0, 0):
                                vw.append((s_mm, prev_mm2_last))
                            go("sync", lambda y, g=g, eq=eq, gpar=gpar:
                               y.dma_start(out=v_buf[gpar][:],
                                           in_=vsp[g * 512:(g + 1) * 512, eq * 512:(eq + 1) * 512]
                                           .rearrange("(k p) e -> p k e", p=128)),
                               waits=vw, inc=s_vr[gpar], k=16)
                            vld_cnt[(qb, eq, g)] = (gpar, s_vr[gpar].n)
                            m1w = []
                            if g == 0:
                                m1w.append((s_sc, usil[3]))
                            else:
                                m1w.append((s_sc, relu_cnt[(qb, eq, g - 1)]))

                            def emit_m1(tensor, qb=qb, g=g):
                                for j in range(4):
                                    kt = g * 4 + j
                                    r = tensor.matmul(ps_s[:, j * 512:(j + 1) * 512],
                                                      kT[:, kt * 128:(kt + 1) * 128],
                                                      qT[:, qb * 512:(qb + 1) * 512],
                                                      start=True, stop=True)
                                return r
                            go("tensor", emit_m1, waits=m1w, inc=s_mm)
                            mm1_cnt[(qb, eq, g)] = s_mm.n
                            rw = [(s_mm, mm1_cnt[(qb, eq, g)])]
                            if g >= 2:
                                rw.append((s_v, sq_cnt[(qb, eq, g - 2)]))
                            relu_cnt[(qb, eq, g)] = sop(lambda sc, gpar=gpar:
                                                        sc.activation(out=srelu[gpar][:], in_=ps_s[:],
                                                                      func=AF.Relu, scale=invl_pp[:]),
                                                        waits=rw)
                            qw = [(s_sc, relu_cnt[(qb, eq, g)])]
                            if g >= 2:
                                qw.append((s_mm, mm2_cnt[(qb, eq, g - 2)]))
                            sq_cnt[(qb, eq, g)] = vop(lambda v, gpar=gpar:
                                                      v.tensor_tensor(out=ssq[gpar][:], in0=srelu[gpar][:],
                                                                      in1=srelu[gpar][:], op=ALU.mult),
                                                      waits=qw)
                            m2w = [(s_v, sq_cnt[(qb, eq, g)]),
                                   (s_vr[gpar], vld_cnt[(qb, eq, g)][1])]
                            if g == 0:
                                if eq > 0:
                                    m2w.append((s_v, uav_cnt[(qb, eq - 1)]))
                                elif qb > 0:
                                    m2w.append((s_v, ycpy_cnt[qb - 1]))

                            def emit_m2(tensor, g=g, gpar=gpar, eq=eq):
                                for j in range(4):
                                    for es in range(4):
                                        r = tensor.matmul(ps_av[:, es * 512:(es + 1) * 512],
                                                          v_buf[gpar][:, j, es * 128:(es + 1) * 128],
                                                          ssq[gpar][:, j * 512:(j + 1) * 512],
                                                          start=(g == 0 and j == 0),
                                                          stop=(g == G - 1 and j == 3),
                                                          skip_group_check=True)
                                return r
                            go("tensor", emit_m2, waits=m2w, inc=s_mm)
                            mm2_cnt[(qb, eq, g)] = s_mm.n
                        prev_mm2_last = mm2_cnt[(qb, eq, G - 1)]
                        for es in range(4):
                            uav_cnt[(qb, eq)] = vop(lambda v, es=es, eq=eq:
                                                    v.tensor_tensor(out=uavT[:, eq * 4 + es, :],
                                                                    in0=ps_av[:, es * 512:(es + 1) * 512],
                                                                    in1=uT_eq[:, es, :], op=ALU.mult),
                                                    waits=[(s_mm, prev_mm2_last)] +
                                                          ([] if qb == 0 else [(s_mm, wout_cnt.get(qb - 1, 0))]))
                    for qt in range(4):
                        ypar = qt % 2
                        yw_ = [(s_v, uav_cnt[(qb, 3)])]
                        if qt >= 1:
                            yw_.append((s_v, ycpy[qt - 1]))

                        def emit_y2(tensor, qb=qb, qt=qt):
                            r = None
                            for hc in range(2):
                                for m in range(ME):
                                    r = tensor.matmul(ps_av[:, hc * 512:(hc + 1) * 512],
                                                      uavT[:, m, qt * 128:(qt + 1) * 128],
                                                      Wout_sb[:, m, hc * 512:(hc + 1) * 512],
                                                      start=(m == 0), stop=False,
                                                      skip_group_check=True)
                                r = tensor.matmul(ps_av[:, hc * 512:(hc + 1) * 512], ones_row[:],
                                                  bout_row[:, hc * 512:(hc + 1) * 512],
                                                  start=False, stop=True, skip_group_check=True)
                            return r
                        go("tensor", emit_y2, waits=yw_, inc=s_mm)
                        wout_cnt[qb] = s_mm.n
                        ycpy = locals().get("ycpy", {})
                        cw = [(s_mm, wout_cnt[qb])]
                        if yw_i >= 2:
                            cw.append((s_yw[ypar], 16 * (yw_i // 2)))
                        ycpy[qt] = vop(lambda v, ypar=ypar:
                                       v.tensor_copy(out=y_sb[ypar][:], in_=ps_av[:, 0:1024]),
                                       waits=cw)
                        ycpy_cnt[qb] = ycpy[qt]
                        go("sync", lambda y, qb=qb, qt=qt, ypar=ypar:
                           y.dma_start(out=ysp[qb * 512 + qt * 128: qb * 512 + (qt + 1) * 128, :],
                                       in_=y_sb[ypar][:]),
                           waits=[(s_v, ycpy[qt])], inc=s_yw[ypar], k=16)
                        yw_i += 1
                go("sync", lambda y: y.nop(),
                   waits=[(s_yw[0], s_yw[0].n), (s_yw[1], s_yw[1].n)])
                if taps:
                    go("sync", lambda y: y.dma_start(out=t_y[:, :], in_=ysp[:, :]), inc=s_tap, k=16)
                flush_block()

        with ExitStack() as ph3:
            E3 = ph3.enter_context
            xnT_blk = E3(nc.sbuf_tensor("xnT_blk", [128, MD, TB], BF16))
            acc_sb = E3(nc.sbuf_tensor("acc_sb", [128, ntile_b, D], F32))
            x3_sb = [E3(nc.sbuf_tensor(f"x3_sb{i}", [128, 2, D], F32)) for i in range(2)]
            xn_bf = [E3(nc.sbuf_tensor(f"xn_bf{i}", [128, 2, D], BF16)) for i in range(2)]
            w_sb = E3(nc.sbuf_tensor("w_sb", [128, ntile_b, NE], F32))
            w33 = E3(nc.sbuf_tensor("w33", [33, TB], BF16))
            aff_sb = E3(nc.sbuf_tensor("aff_sb", [128, NE], F32))
            a2_sb = E3(nc.sbuf_tensor("a2_sb", [128, NE], F32))
            mx_sb = E3(nc.sbuf_tensor("mx_sb", [128, 1], F32))
            msk_sb = E3(nc.sbuf_tensor("msk_sb", [128, NE], F32))
            sum_sb = E3(nc.sbuf_tensor("sum_sb", [128, 1], F32))
            epr_sb = E3(nc.sbuf_tensor("epr_sb", [128, 1], F32))
            Wgs_sb = E3(nc.sbuf_tensor("Wgs_sb", [128, MD, II], BF16))
            Wus_sb = E3(nc.sbuf_tensor("Wus_sb", [128, MD, II], BF16))
            Wds_sb = E3(nc.sbuf_tensor("Wds_sb", [128, MI, D], BF16))
            Wge_sb = [E3(nc.sbuf_tensor(f"Wge_sb{i}", [128, MD, II], BF16)) for i in range(2)]
            Wue_sb = [E3(nc.sbuf_tensor(f"Wue_sb{i}", [128, MD, II], BF16)) for i in range(2)]
            Wde_sb = [E3(nc.sbuf_tensor(f"Wde_sb{i}", [128, MI, D], BF16)) for i in range(2)]
            g_sb = [E3(nc.sbuf_tensor(f"g_sb{i}", [128, II], BF16)) for i in range(2)]
            h_sb = [E3(nc.sbuf_tensor(f"h_sb{i}", [128, MI, II], BF16)) for i in range(2)]
            y3_sb = [E3(nc.sbuf_tensor(f"y3_sb{i}", [128, D], F32)) for i in range(2)]
            o_sb = [E3(nc.sbuf_tensor(f"o_sb{i}", [128, D], F32)) for i in range(2)]
            ps_a = E3(nc.psum_tensor("ps_a", [128, 512], F32))
            ps_b = E3(nc.psum_tensor("ps_b", [128, 512], F32))
            ps_d = [E3(nc.psum_tensor(f"ps_d{i}", [128, D], F32)) for i in range(2)]

            gop(lambda g: g.memset(epr_sb[:], 1e-9))
            GP1 = s_gp.n
            gop(lambda g: g.memset(w33[32:33, :], 1.0))
            GP2 = s_gp.n
            wload(Wgs_sb[:], Wg_s.rearrange("(m p) i -> p m i", p=128), s_wg)
            wload(Wus_sb[:], Wu_s.rearrange("(m p) i -> p m i", p=128), s_wg)
            wload(Wds_sb[:], Wd_s.rearrange("(m p) d -> p m d", p=128), s_wg)
            WG_PH3 = s_wg.n

            def eload(e, buf, waits=()):
                sem = s_we[buf]
                wload(Wge_sb[buf][:], Wg[e].rearrange("(m p) i -> p m i", p=128), sem, waits=waits)
                wload(Wue_sb[buf][:], Wu[e].rearrange("(m p) i -> p m i", p=128), sem)
                wload(Wde_sb[buf][:], Wd[e].rearrange("(m p) d -> p m d", p=128), sem)
                return (sem, sem.n)

            def eload2(e, buf, waits):
                return eload(e, buf, waits=waits)

            eweight_ready = {}
            sweep_idx = 0
            out_i = 0
            for tb in range(ntb):
                lastxn = {}
                ngrp = TB // 256
                for grp in range(ngrp):
                    gpr = grp % 2
                    t0 = tb * ntile_b + grp * 2
                    xw = [] if grp < 2 else [(s_v, lastxn[grp - 2])]
                    go("sync", lambda y, tb=tb, grp=grp, gpr=gpr:
                       y.dma_start(out=x3_sb[gpr][:],
                                   in_=xb[tb * TB + grp * 256: tb * TB + (grp + 1) * 256, :]
                                   .rearrange("(s p) d -> p s d", p=128)),
                       waits=xw, inc=s_x3[gpr], k=16)
                    nx3 = s_x3[gpr].n
                    xncnt = {}
                    trw = [] if grp < 2 else [(s_tr3[gpr], 16 * 2 * (grp // 2))]
                    for s in range(2):
                        t = t0 + s
                        xncnt[s] = vop(lambda v, gpr=gpr, s=s, t=t:
                                       v.tensor_scalar(out=xn_bf[gpr][:, s, :], in0=x3_sb[gpr][:, s, :],
                                                       scalar1=mu_all[:, t:t + 1], scalar2=r_all[:, t:t + 1],
                                                       op0=ALU.subtract, op1=ALU.mult),
                                       waits=[(s_x3[gpr], nx3)] + trw)
                    lastxn[grp] = xncnt[1]
                    for s in range(2):
                        go("sync", lambda y, gpr=gpr, grp=grp, s=s:
                           y.dma_start(out=xnT_blk[:, :, grp * 256 + s * 128: grp * 256 + (s + 1) * 128],
                                       in_=xn_bf[gpr][:, s, :], transpose=True),
                           waits=[(s_v, xncnt[s])], inc=s_tr3[gpr], k=16)
                tr3_all = (s_tr3[0].n, s_tr3[1].n)
                eweight_ready[(tb, 0)] = eload(0, 0)
                eweight_ready[(tb, 1)] = eload(1, 1)
                wdone = {}
                for tt in range(ntile_b):
                    grp = tt // 2
                    gpr = grp % 2
                    rw = [(s_tr3[gpr], 16 * 4 * (grp // 2 + 1) if False else s_tr3[gpr].n)]
                    if tb == 0 and tt == 0:
                        rw.append((s_wg, WG_PH3))
                        rw.append((s_gp, GP2))
                    if tt >= 1:
                        rw.append((s_sc, sig_cnt[tt - 1]))

                    def emit_rt(tensor, tt=tt):
                        for m in range(MD):
                            tensor.matmul(ps_a[:, 0:NE], xnT_blk[:, m, tt * 128:(tt + 1) * 128],
                                          Wr_sb[:, m, :], start=(m == 0), stop=False)
                        return tensor.matmul(ps_a[:, 0:NE], ones_row[:], br_row[:],
                                             start=False, stop=True)
                    go("tensor", emit_rt, waits=rw, inc=s_mm)
                    nrt = s_mm.n
                    sig_cnt = locals().get("sig_cnt", {})
                    sig_cnt[tt] = sop(lambda sc: sc.activation(out=aff_sb[:], in_=ps_a[:, 0:NE],
                                                               func=AF.Sigmoid),
                                      waits=[(s_mm, nrt)] + ([] if tt == 0 else [(s_v, wtk[tt - 1])]))
                    wtk = locals().get("wtk", {})
                    vop(lambda v: v.tensor_copy(out=a2_sb[:], in_=aff_sb[:]),
                        waits=[(s_sc, sig_cnt[tt]), (s_gp, GP1)])
                    for it in range(TOPK - 1):
                        vop(lambda v: v.tensor_reduce(out=mx_sb[:], in_=a2_sb[:], axis=AX.X, op=ALU.max))
                        vop(lambda v: v.tensor_scalar(out=msk_sb[:], in0=a2_sb[:], scalar1=mx_sb[:],
                                                      scalar2=None, op0=ALU.is_ge))
                        vop(lambda v: v.scalar_tensor_tensor(out=a2_sb[:], in0=msk_sb[:], scalar=-1e30,
                                                             in1=a2_sb[:], op0=ALU.mult, op1=ALU.add))
                    vop(lambda v: v.tensor_reduce(out=mx_sb[:], in_=a2_sb[:], axis=AX.X, op=ALU.max))
                    vop(lambda v: v.tensor_scalar(out=msk_sb[:], in0=aff_sb[:], scalar1=mx_sb[:],
                                                  scalar2=None, op0=ALU.is_ge))
                    vop(lambda v, tt=tt: v.tensor_tensor(out=w_sb[:, tt, :], in0=aff_sb[:],
                                                         in1=msk_sb[:], op=ALU.mult))
                    vop(lambda v, tt=tt: v.tensor_reduce(out=sum_sb[:], in_=w_sb[:, tt, :],
                                                         axis=AX.X, op=ALU.add))
                    vop(lambda v: v.tensor_tensor(out=sum_sb[:], in0=sum_sb[:], in1=epr_sb[:], op=ALU.add))
                    vop(lambda v: v.reciprocal(out=sum_sb[:], in_=sum_sb[:]))
                    wtk[tt] = vop(lambda v, tt=tt: v.tensor_scalar(out=w_sb[:, tt, :], in0=w_sb[:, tt, :],
                                                                   scalar1=sum_sb[:], scalar2=None,
                                                                   op0=ALU.mult))
                    tw = [(s_v, wtk[tt]), (s_gp, GP0)]
                    if tt >= 1:
                        tw.append((s_sc, wev[tt - 1]))
                    go("tensor", lambda tensor, tt=tt:
                       tensor.transpose(ps_b[0:NE, 0:128], w_sb[:, tt, :], idn[:]),
                       waits=tw, inc=s_mm)
                    ntw = s_mm.n
                    wev = locals().get("wev", {})
                    wev[tt] = sop(lambda sc, tt=tt: sc.copy(out=w33[0:NE, tt * 128:(tt + 1) * 128],
                                                            in_=ps_b[0:NE, 0:128]),
                                  waits=[(s_mm, ntw)])
                    wdone[tt] = (wev[tt], wtk[tt])
                if taps:
                    go("sync", lambda y, tb=tb: y.dma_start(
                        out=t_w[:, tb * ntile_b:(tb + 1) * ntile_b, :], in_=w_sb[:]),
                       waits=[(s_v, s_v.n)], inc=s_tap, k=16)
                    go("sync", lambda y: y.nop(), waits=[(s_tap, s_tap.n)])

                hstt_cnt = {}
                gsil_cnt = {}
                dacc_cnt = {}
                slot = 0
                dslot = 0
                for ei in range(-1, NE):
                    if ei >= 0:
                        sweep_idx += 1
                    ebuf = ei % 2 if ei >= 0 else 0
                    Wg_cur = Wgs_sb if ei < 0 else Wge_sb[ebuf]
                    Wu_cur = Wus_sb if ei < 0 else Wue_sb[ebuf]
                    Wd_cur = Wds_sb if ei < 0 else Wde_sb[ebuf]
                    for c in range(nch):
                        hpar = (ei * nch + c + NE * nch) % 2
                        for ic in range(MI):
                            gw = []
                            if ei >= 0:
                                gw.append(eweight_ready[(tb, ei)])
                            else:
                                gw.append((s_wg, WG_PH3))
                                gw.append((s_tr3[0], tr3_all[0]))
                                if nch > 1:
                                    gw.append((s_tr3[1], tr3_all[1]))
                            if slot >= 1:
                                gw.append((s_sc, gsil_cnt[slot - 1]))
                            else:
                                gw.append((s_sc, wev[ntile_b - 1]))

                            def emit_g(tensor, Wg_cur=Wg_cur, c=c, ic=ic):
                                for m in range(MD):
                                    r = tensor.matmul(ps_a[:], Wg_cur[:, m, ic * 128:(ic + 1) * 128],
                                                      xnT_blk[:, m, c * 512:(c + 1) * 512],
                                                      start=(m == 0), stop=(m == MD - 1))
                                return r
                            go("tensor", emit_g, waits=gw, inc=s_mm)
                            ngm = s_mm.n
                            bias_ap = (bgs_pp[:, ic:ic + 1] if ei < 0 else bge_pp[:, ei, ic:ic + 1])
                            sgw = [(s_mm, ngm)]
                            if slot >= 2:
                                sgw.append((s_v, hstt_cnt[slot - 2]))
                            gsil_cnt[slot] = sop(lambda sc, bias_ap=bias_ap, gp=slot % 2:
                                                 sc.activation(out=g_sb[gp][:], in_=ps_a[:],
                                                               func=AF.Silu, bias=bias_ap),
                                                 waits=sgw)
                            uw = []
                            if ei >= 0:
                                uw.append(eweight_ready[(tb, ei)])
                            if slot >= 1:
                                uw.append((s_v, hstt_cnt[slot - 1]))
                            else:
                                uw.append((s_sc, wev[ntile_b - 1]))

                            def emit_up(tensor, Wu_cur=Wu_cur, c=c, ic=ic):
                                for m in range(MD):
                                    r = tensor.matmul(ps_b[:], Wu_cur[:, m, ic * 128:(ic + 1) * 128],
                                                      xnT_blk[:, m, c * 512:(c + 1) * 512],
                                                      start=(m == 0), stop=(m == MD - 1))
                                return r
                            go("tensor", emit_up, waits=uw, inc=s_mm)
                            num = s_mm.n
                            bu_ap = (bus_pp[:, ic:ic + 1] if ei < 0 else bue_pp[:, ei, ic:ic + 1])
                            hw = [(s_mm, num), (s_sc, gsil_cnt[slot])]
                            key = (hpar, "lastdown")
                            if key in hstt_cnt:
                                hw.append((s_mm, hstt_cnt[key]))
                            hstt_cnt[slot] = vop(lambda v, bu_ap=bu_ap, gp=slot % 2, hpar=hpar, ic=ic:
                                                 v.scalar_tensor_tensor(out=h_sb[hpar][:, ic, :],
                                                                        in0=ps_b[:], scalar=bu_ap,
                                                                        in1=g_sb[gp][:],
                                                                        op0=ALU.add, op1=ALU.mult),
                                                 waits=hw)
                            slot += 1
                        h_ready = hstt_cnt[slot - 1]
                        for tt4 in range(4):
                            tt = c * 4 + tt4
                            dpar = dslot % 2
                            dw = [(s_v, h_ready)]
                            if dslot >= 2:
                                dw.append((s_v, dacc_cnt[dslot - 2]))
                            if ei < 0:
                                dw.append((s_sc, wdone[tt][0]))

                            def emit_d(tensor, Wd_cur=Wd_cur, hpar=hpar, tt=tt, tt4=tt4, c=c, ei=ei, tb=tb, dpar=dpar):
                                r = None
                                for hc in range(2):
                                    if ei < 0:
                                        tensor.matmul(ps_d[dpar][:, hc * 512:(hc + 1) * 512],
                                                      w33[:, tt * 128:(tt + 1) * 128],
                                                      bd33[:, hc * 512:(hc + 1) * 512],
                                                      start=True, stop=False, skip_group_check=True)
                                    for ic in range(MI):
                                        r = tensor.matmul(ps_d[dpar][:, hc * 512:(hc + 1) * 512],
                                                          h_sb[hpar][:, ic, tt4 * 128:(tt4 + 1) * 128],
                                                          Wd_cur[:, ic, hc * 512:(hc + 1) * 512],
                                                          start=(ei >= 0 and ic == 0), stop=(ic == MI - 1),
                                                          skip_group_check=True)
                                return r
                            go("tensor", emit_d, waits=dw, inc=s_mm)
                            ndm = s_mm.n
                            hstt_cnt[(hpar, "lastdown")] = ndm
                            if ei < 0:
                                dacc_cnt[dslot] = vop(lambda v, tt=tt, dpar=dpar:
                                                      v.tensor_copy(out=acc_sb[:, tt, :], in_=ps_d[dpar][:]),
                                                      waits=[(s_mm, ndm)])
                            else:
                                dacc_cnt[dslot] = vop(lambda v, tt=tt, dpar=dpar, ei=ei:
                                                      v.scalar_tensor_tensor(out=acc_sb[:, tt, :],
                                                                             in0=ps_d[dpar][:],
                                                                             scalar=w_sb[:, tt, ei:ei + 1],
                                                                             in1=acc_sb[:, tt, :],
                                                                             op0=ALU.mult, op1=ALU.add),
                                                      waits=[(s_mm, ndm)])
                            acc_last = locals().get("acc_last", {})
                            acc_last[tt] = dacc_cnt[dslot]
                            dslot += 1
                    if ei >= 0 and ei + 2 < NE:
                        eweight_ready[(tb, ei + 2)] = eload2(ei + 2, (ei + 2) % 2,
                                                             waits=[(s_mm, s_mm.n)])

                for tt in range(ntile_b):
                    tglob = tb * ntile_b + tt
                    cpar = tt % 2
                    yw_ = [(s_v, acc_last[tt])] if tt < 2 else [(s_v, comb_done[tt - 2])]
                    go("sync", lambda y, tglob=tglob, cpar=cpar:
                       y.dma_start(out=y3_sb[cpar][:], in_=ysp[tglob * 128:(tglob + 1) * 128, :]),
                       waits=yw_, inc=s_yr[cpar], k=16)
                    nyr = s_yr[cpar].n
                    go("sync", lambda y, tglob=tglob, cpar=cpar:
                       y.dma_start(out=x3_sb[cpar][:, 0, :],
                                   in_=xb[tglob * 128:(tglob + 1) * 128, :]),
                       waits=yw_, inc=s_x4[cpar], k=16)
                    nx4 = s_x4[cpar].n
                    comb_done = locals().get("comb_done", {})
                    ow = [] if out_i < 2 else [(s_out[cpar], 16 * (out_i // 2))]
                    vop(lambda v, tt=tt, cpar=cpar:
                        v.tensor_tensor(out=o_sb[cpar][:], in0=acc_sb[:, tt, :], in1=y3_sb[cpar][:],
                                        op=ALU.add),
                        waits=[(s_yr[cpar], nyr), (s_v, acc_last[tt])] + ow)
                    vop(lambda v, tt=tt, cpar=cpar, tglob=tglob:
                        v.scalar_tensor_tensor(out=o_sb[cpar][:], in0=x3_sb[cpar][:, 0, :],
                                               scalar=r1_all[:, tglob:tglob + 1], in1=o_sb[cpar][:],
                                               op0=ALU.mult, op1=ALU.add),
                        waits=[(s_x4[cpar], nx4)])
                    comb_done[tt] = vop(lambda v, tt=tt, cpar=cpar, tglob=tglob:
                                        v.tensor_scalar(out=o_sb[cpar][:], in0=o_sb[cpar][:],
                                                        scalar1=mur_all[:, tglob:tglob + 1], scalar2=None,
                                                        op0=ALU.subtract))
                    go("sync", lambda y, tglob=tglob, cpar=cpar:
                       y.dma_start(out=out[tglob * 128:(tglob + 1) * 128, :], in_=o_sb[cpar][:]),
                       waits=[(s_v, comb_done[tt])], inc=s_out[cpar], k=16)
                    out_i += 1
                flush_block()

        fw = [(s_out[0], s_out[0].n), (s_out[1], s_out[1].n)]
        if taps:
            fw.append((s_tap, s_tap.n))
        go("sync", lambda y: y.nop(), waits=fw)
        flush_block()

    return nc



def relayout_params(inputs):
    d = {}
    d["p_lng"] = np.ascontiguousarray(np.asarray(inputs["L0_attn_ln_g"]).reshape(MD, 128).T)
    d["p_lnb"] = np.ascontiguousarray(np.asarray(inputs["L0_attn_ln_b"]).reshape(MD, 128).T)
    d["p_binu"] = np.ascontiguousarray(np.asarray(inputs["L0_attn_b_in"])[0:E].reshape(ME, 128).T)
    d["p_bgs"] = np.ascontiguousarray(np.asarray(inputs["L0_ffn_bg_s"]).reshape(MI, 128).T)
    d["p_bus"] = np.ascontiguousarray(np.asarray(inputs["L0_ffn_bu_s"]).reshape(MI, 128).T)
    d["p_bge"] = np.ascontiguousarray(np.asarray(inputs["L0_ffn_bg"]).reshape(NE, MI, 128).transpose(2, 0, 1))
    d["p_bue"] = np.ascontiguousarray(np.asarray(inputs["L0_ffn_bu"]).reshape(NE, MI, 128).transpose(2, 0, 1))
    return d


def kernel(**inputs):
    L, TOWN = 16384, 4096
    x = np.asarray(inputs["L0_attn_x"])
    wnames = [k for k in inputs if k != "L0_attn_x"]
    nc = build(L=L, TOWN=TOWN, TB=1024, taps=False)
    derived = relayout_params(inputs)
    in_maps = []
    for c in range(8):
        b, q0 = c // 4, (c % 4) * TOWN
        xb_c = np.ascontiguousarray(np.concatenate([x[b, q0:], x[b, :q0]], axis=0))
        m = {"xb": xb_c}
        for k in wnames:
            m[k] = np.ascontiguousarray(np.asarray(inputs[k]))
        m.update(derived)
        in_maps.append(m)
    res = run_bass_kernel_spmd(nc, in_maps, core_ids=list(range(8)))
    outp = np.empty((2, L, D), dtype=np.float32)
    for c in range(8):
        b, q0 = c // 4, (c % 4) * TOWN
        outp[b, q0:q0 + TOWN] = res.results[c]["out"]
    return outp


# version 02
# speedup vs baseline: 1.3258x; 1.3258x over previous
import numpy as np
from contextlib import ExitStack

import concourse.bass as bass
import concourse.mybir as mybir
from concourse.bass_utils import run_bass_kernel_spmd

F32 = mybir.dt.float32
BF16 = mybir.dt.bfloat16
AF = mybir.ActivationFunctionType
ALU = mybir.AluOpType
AX = mybir.AxisListType

D = 1024
MD = 8
E = 2048
ME = E // 128
S = 128
NE = 32
II = 512
MI = II // 128
TOPK = 8
EPS = 1e-5


def build(L=16384, TOWN=4096, TB=2048, taps=False):
    assert L % 512 == 0 and TOWN % 512 == 0 and TB % 512 == 0 and TOWN % TB == 0
    nst = L // 512
    nst_own = TOWN // 512
    nqb = TOWN // 512
    G = L // 512
    ntb = TOWN // TB
    nch = TB // 512
    ntile_b = TB // 128
    NTL = L // 128
    INVL = 1.0 / L

    nc = bass.Bass("TRN2", target_bir_lowering=False)

    xb = nc.dram_tensor("xb", [L, D], F32, kind="ExternalInput").ap()
    W_in = nc.dram_tensor("L0_attn_W_in", [D, 2 * E + S], F32, kind="ExternalInput").ap()
    b_in = nc.dram_tensor("L0_attn_b_in", [2 * E + S], F32, kind="ExternalInput").ap()
    W_out = nc.dram_tensor("L0_attn_W_out", [E, D], F32, kind="ExternalInput").ap()
    b_out = nc.dram_tensor("L0_attn_b_out", [D], F32, kind="ExternalInput").ap()
    ln_g = nc.dram_tensor("L0_attn_ln_g", [D], F32, kind="ExternalInput").ap()
    ln_b = nc.dram_tensor("L0_attn_ln_b", [D], F32, kind="ExternalInput").ap()
    gq = nc.dram_tensor("L0_attn_gamma_q", [S], F32, kind="ExternalInput").ap()
    bq = nc.dram_tensor("L0_attn_beta_q", [S], F32, kind="ExternalInput").ap()
    gk = nc.dram_tensor("L0_attn_gamma_k", [S], F32, kind="ExternalInput").ap()
    bk = nc.dram_tensor("L0_attn_beta_k", [S], F32, kind="ExternalInput").ap()
    Wr = nc.dram_tensor("L0_ffn_Wr", [D, NE], F32, kind="ExternalInput").ap()
    br = nc.dram_tensor("L0_ffn_br", [NE], F32, kind="ExternalInput").ap()
    Wg_s = nc.dram_tensor("L0_ffn_Wg_s", [D, II], F32, kind="ExternalInput").ap()
    bg_s = nc.dram_tensor("L0_ffn_bg_s", [II], F32, kind="ExternalInput").ap()
    Wu_s = nc.dram_tensor("L0_ffn_Wu_s", [D, II], F32, kind="ExternalInput").ap()
    bu_s = nc.dram_tensor("L0_ffn_bu_s", [II], F32, kind="ExternalInput").ap()
    Wd_s = nc.dram_tensor("L0_ffn_Wd_s", [II, D], F32, kind="ExternalInput").ap()
    bd_s = nc.dram_tensor("L0_ffn_bd_s", [D], F32, kind="ExternalInput").ap()
    Wg = nc.dram_tensor("L0_ffn_Wg", [NE, D, II], F32, kind="ExternalInput").ap()
    bg = nc.dram_tensor("L0_ffn_bg", [NE, II], F32, kind="ExternalInput").ap()
    Wu = nc.dram_tensor("L0_ffn_Wu", [NE, D, II], F32, kind="ExternalInput").ap()
    bu = nc.dram_tensor("L0_ffn_bu", [NE, II], F32, kind="ExternalInput").ap()
    Wd = nc.dram_tensor("L0_ffn_Wd", [NE, II, D], F32, kind="ExternalInput").ap()
    bd = nc.dram_tensor("L0_ffn_bd", [NE, D], F32, kind="ExternalInput").ap()
    p_lng = nc.dram_tensor("p_lng", [128, MD], F32, kind="ExternalInput").ap()
    p_lnb = nc.dram_tensor("p_lnb", [128, MD], F32, kind="ExternalInput").ap()
    p_binu = nc.dram_tensor("p_binu", [128, ME], F32, kind="ExternalInput").ap()
    p_bgs = nc.dram_tensor("p_bgs", [128, MI], F32, kind="ExternalInput").ap()
    p_bus = nc.dram_tensor("p_bus", [128, MI], F32, kind="ExternalInput").ap()
    p_bge = nc.dram_tensor("p_bge", [128, NE, MI], F32, kind="ExternalInput").ap()
    p_bue = nc.dram_tensor("p_bue", [128, NE, MI], F32, kind="ExternalInput").ap()
    out = nc.dram_tensor("out", [TOWN, D], F32, kind="ExternalOutput").ap()
    vsp = nc.dram_tensor("vsp", [L, E], BF16, kind="Internal").ap()
    x2sp = nc.dram_tensor("x2sp", [nst_own, 128, MD, 512], BF16, kind="Internal").ap()
    ysp = nc.dram_tensor("ysp", [TOWN, D], F32, kind="Internal").ap()
    if taps:
        t_kT = nc.dram_tensor("t_kT", [128, L], BF16, kind="ExternalOutput").ap()
        t_qT = nc.dram_tensor("t_qT", [128, TOWN], BF16, kind="ExternalOutput").ap()
        t_w = nc.dram_tensor("t_w", [128, TOWN // 128, NE], F32, kind="ExternalOutput").ap()
        t_y = nc.dram_tensor("t_y", [TOWN, D], F32, kind="ExternalOutput").ap()
        t_v = nc.dram_tensor("t_v", [L, E], BF16, kind="ExternalOutput").ap()
        t_x2 = nc.dram_tensor("t_x2", [nst_own, 128, MD, 512], BF16, kind="ExternalOutput").ap()

    with ExitStack() as ctx:
        EC = ctx.enter_context

        class Sem:
            def __init__(self, name):
                self.h = EC(nc.semaphore(name))
                self.n = 0

        s_wg = Sem("s_wg")
        s_we = [Sem("s_we0"), Sem("s_we1")]
        s_gp = Sem("s_gp")
        s_x = [Sem("s_x0"), Sem("s_x1")]
        s_tr = [Sem("s_tr0"), Sem("s_tr1")]
        s_vw = [Sem("s_vw0"), Sem("s_vw1")]
        s_x2w = Sem("s_x2w")
        s_x2r = [Sem("s_x2r0"), Sem("s_x2r1")]
        s_vr = [Sem("s_vr0"), Sem("s_vr1"), Sem("s_vr2")]
        s_yw = [Sem("s_yw0"), Sem("s_yw1")]
        s_yr = [Sem("s_yr0"), Sem("s_yr1")]
        s_x3 = [Sem("s_x30"), Sem("s_x31")]
        s_tr3 = [Sem("s_tr30"), Sem("s_tr31")]
        s_x4 = [Sem("s_x40"), Sem("s_x41")]
        s_out = [Sem("s_out0"), Sem("s_out1")]
        s_tap = Sem("s_tap")
        s_mm = Sem("s_mm")
        s_sc = Sem("s_sc")
        s_v = Sem("s_v")

        ones_row = EC(nc.sbuf_tensor("ones_row", [1, 128], BF16))
        eps_pp = EC(nc.sbuf_tensor("eps_pp", [128, 1], F32))
        invl_pp = EC(nc.sbuf_tensor("invl_pp", [128, 1], F32))
        idn = EC(nc.sbuf_tensor("idn", [128, 128], F32))
        bnsc = EC(nc.sbuf_tensor("bnsc", [128, 12], F32))
        mv_all = EC(nc.sbuf_tensor("mv_all", [128, NTL, 2], F32))
        mu_all = EC(nc.sbuf_tensor("mu_all", [128, NTL], F32))
        r_all = EC(nc.sbuf_tensor("r_all", [128, NTL], F32))
        rr_all = EC(nc.sbuf_tensor("rr_all", [128, NTL], F32))
        r1_all = EC(nc.sbuf_tensor("r1_all", [128, NTL], F32))
        mur_all = EC(nc.sbuf_tensor("mur_all", [128, NTL], F32))
        sd_all = EC(nc.sbuf_tensor("sd_all", [128, NTL], F32))
        v2_all = EC(nc.sbuf_tensor("v2_all", [128, NTL], F32))
        lng_pp = EC(nc.sbuf_tensor("lng_pp", [128, MD], F32))
        lnb_pp = EC(nc.sbuf_tensor("lnb_pp", [128, MD], F32))
        binz_pp = EC(nc.sbuf_tensor("binz_pp", [128, 1], F32))
        binu_pp = EC(nc.sbuf_tensor("binu_pp", [128, ME], F32))
        binv_row = EC(nc.sbuf_tensor("binv_row", [1, E], BF16))
        bout_row = EC(nc.sbuf_tensor("bout_row", [1, D], BF16))
        br_row = EC(nc.sbuf_tensor("br_row", [1, NE], BF16))
        gq_pp = EC(nc.sbuf_tensor("gq_pp", [128, 1], F32))
        bq_pp = EC(nc.sbuf_tensor("bq_pp", [128, 1], F32))
        gk_pp = EC(nc.sbuf_tensor("gk_pp", [128, 1], F32))
        bk_pp = EC(nc.sbuf_tensor("bk_pp", [128, 1], F32))
        bgs_pp = EC(nc.sbuf_tensor("bgs_pp", [128, MI], F32))
        bus_pp = EC(nc.sbuf_tensor("bus_pp", [128, MI], F32))
        bge_pp = EC(nc.sbuf_tensor("bge_pp", [128, NE, MI], F32))
        bue_pp = EC(nc.sbuf_tensor("bue_pp", [128, NE, MI], F32))
        bd33 = EC(nc.sbuf_tensor("bd33", [33, D], BF16))
        Wr_sb = EC(nc.sbuf_tensor("Wr_sb", [128, MD, NE], BF16))

        ops = {k: [] for k in ("sync", "gpsimd", "vector", "scalar", "tensor")}

        def go(eng, emitfn, waits=(), inc=None, k=1):
            waits = tuple((s.h, int(v)) for (s, v) in waits if v > 0)

            def run(x, emitfn=emitfn, waits=waits, inc=inc, k=k):
                for (h, v) in waits:
                    x.wait_ge(h, v)
                r = emitfn(x)
                if inc is not None:
                    r.then_inc(inc.h, k)
            ops[eng].append(run)
            if inc is not None:
                inc.n += k

        def vop(emitfn, waits=()):
            go("vector", emitfn, [(s_v, s_v.n)] + list(waits), inc=s_v)
            return s_v.n

        def sop(emitfn, waits=()):
            go("scalar", emitfn, [(s_sc, s_sc.n)] + list(waits), inc=s_sc)
            return s_sc.n

        def gop(emitfn, waits=()):
            go("gpsimd", emitfn, [(s_gp, s_gp.n)] + list(waits), inc=s_gp)
            return s_gp.n

        def flush_block():
            with nc.Block() as block:
                for name in ("sync", "gpsimd", "vector", "scalar", "tensor"):
                    lst = ops[name]
                    if not lst:
                        continue

                    def body(engine, lst=lst):
                        for f in lst:
                            f(engine)
                    getattr(block, name)(body)
            for k in ops:
                ops[k] = []

        gop(lambda g: g.memset(ones_row[:], 1.0))
        gop(lambda g: g.memset(eps_pp[:], EPS))
        gop(lambda g: g.memset(invl_pp[:], INVL))
        gop(lambda g: g.memset(idn[:], 0.0))
        gop(lambda g: g.affine_select(out=idn[:], in_=idn[:], compare_op=ALU.not_equal,
                                      fill=1.0, base=0, pattern=[[-1, 128]], channel_multiplier=1))

        def wload(dst_ap, src_ap, sem, waits=()):
            go("gpsimd", lambda g, dd=dst_ap, ss=src_ap: g.dma_start(out=dd, in_=ss),
               waits=waits, inc=sem, k=16)

        wload(lng_pp[:], p_lng[:, :], s_wg)
        wload(lnb_pp[:], p_lnb[:, :], s_wg)
        wload(binz_pp[:], b_in[2 * E:2 * E + S, None], s_wg)
        wload(binu_pp[:], p_binu[:, :], s_wg)
        wload(binv_row[:], b_in[None, E:2 * E], s_wg)
        wload(bout_row[:], b_out[None, :], s_wg)
        wload(br_row[:], br[None, :], s_wg)
        wload(gq_pp[:], gq[:, None], s_wg)
        wload(bq_pp[:], bq[:, None], s_wg)
        wload(gk_pp[:], gk[:, None], s_wg)
        wload(bk_pp[:], bk[:, None], s_wg)
        wload(bgs_pp[:], p_bgs[:, :], s_wg)
        wload(bus_pp[:], p_bus[:, :], s_wg)
        wload(bge_pp[:], p_bge[:, :, :], s_wg)
        wload(bue_pp[:], p_bue[:, :, :], s_wg)
        wload(bd33[0:32, :], bd[:, :], s_wg)
        wload(bd33[32:33, :], bd_s[None, :], s_wg)
        wload(Wr_sb[:], Wr.rearrange("(m p) e -> p m e", p=128), s_wg)
        GP0 = s_gp.n
        flush_block()

        with ExitStack() as ph12:
            E12 = ph12.enter_context
            kT = E12(nc.sbuf_tensor("kT", [128, L], BF16))
            qT = E12(nc.sbuf_tensor("qT", [128, TOWN], BF16))
            ph1 = ExitStack()
            E11 = ph1.enter_context
            Wz_sb = E11(nc.sbuf_tensor("Wz_sb", [128, MD, S], BF16))
            Wv_sb = E11(nc.sbuf_tensor("Wv_sb", [128, MD, E], BF16))
            x_sb = [E11(nc.sbuf_tensor(f"x_sb{i}", [128, 4, D], F32)) for i in range(2)]
            xn2_bf = [E11(nc.sbuf_tensor(f"xn2_bf{i}", [128, 4, D], BF16)) for i in range(2)]
            xn2T = [E11(nc.sbuf_tensor(f"xn2T{i}", [128, MD, 512], BF16)) for i in range(2)]
            xn2Ta = [E11(nc.sbuf_tensor(f"xn2Ta{i}", [128, MD, 512], BF16)) for i in range(2)]
            tmpz = E11(nc.sbuf_tensor("tmpz", [128, 512], F32))
            v_sb = [E11(nc.sbuf_tensor(f"v_sb{i}", [128, E], BF16)) for i in range(2)]
            ps_z = E11(nc.psum_tensor("ps_z", [128, 512], F32))
            ps_v = [E11(nc.psum_tensor(f"ps_v{i}", [128, 512], F32)) for i in range(2)]

            wload(Wz_sb[:], W_in[:, 2 * E:2 * E + S].rearrange("(m p) e -> p m e", p=128), s_wg)
            wload(Wv_sb[:], W_in[:, E:2 * E].rearrange("(m p) e -> p m e", p=128), s_wg)
            WG_ALL = s_wg.n

            last_bn = {}
            for st in range(nst):
                par = st % 2
                xw = [] if st < 2 else [(s_v, last_bn[st - 2])]
                go("sync", lambda y, st=st, par=par:
                   y.dma_start(out=x_sb[par][:],
                               in_=xb[st * 512:(st + 1) * 512, :].rearrange("(s p) d -> p s d", p=128)),
                   waits=xw, inc=s_x[par], k=16)
                nx = s_x[par].n
                for s in range(4):
                    t = st * 4 + s
                    vop(lambda v, par=par, s=s: v.bn_stats(out=bnsc[:, 0:6], in_=x_sb[par][:, s, 0:512]),
                        waits=[(s_x[par], nx)])
                    vop(lambda v, par=par, s=s: v.bn_stats(out=bnsc[:, 6:12], in_=x_sb[par][:, s, 512:1024]))
                    last_bn[st] = vop(lambda v, t=t: v.bn_aggr(out=mv_all[:, t, :], in_=bnsc[:]))
            _sd = sop(lambda sc: sc.activation(out=sd_all[:], in_=mv_all[:, :, 1], func=AF.Sqrt,
                                               bias=eps_pp[:]),
                      waits=[(s_v, last_bn[nst - 1]), (s_gp, GP0)])
            _r = vop(lambda v: v.reciprocal(out=r_all[:], in_=sd_all[:]), waits=[(s_sc, _sd)])
            vop(lambda v: v.tensor_tensor(out=v2_all[:], in0=mv_all[:, :, 1], in1=r_all[:], op=ALU.mult))
            vv2 = vop(lambda v: v.tensor_tensor(out=v2_all[:], in0=v2_all[:], in1=r_all[:], op=ALU.mult))
            _s2 = sop(lambda sc: sc.activation(out=v2_all[:], in_=v2_all[:], func=AF.Sqrt, bias=eps_pp[:]),
                      waits=[(s_v, vv2)])
            vop(lambda v: v.reciprocal(out=rr_all[:], in_=v2_all[:]), waits=[(s_sc, _s2)])
            vop(lambda v: v.tensor_tensor(out=rr_all[:], in0=rr_all[:], in1=r_all[:], op=ALU.mult))
            vop(lambda v: v.tensor_copy(out=mu_all[:], in_=mv_all[:, :, 0]))
            vop(lambda v: v.tensor_scalar(out=r1_all[:], in0=r_all[:], scalar1=1.0, scalar2=None, op0=ALU.add))
            STATS_DONE = vop(lambda v: v.tensor_tensor(out=mur_all[:], in0=mu_all[:], in1=r_all[:], op=ALU.mult))
            flush_block()

            lastcons = {}
            tz_cnt = {}
            mm_z = {}
            mm_v = {}
            sc_v = {}
            spill_done = {}
            aff_done = {}
            x2w_cnt = {}
            for st in range(nst):
                par = st % 2
                own = st < nst_own
                xw = [(s_v, STATS_DONE)] if st < 2 else [(s_v, lastcons[st - 2])]
                go("sync", lambda y, st=st, par=par:
                   y.dma_start(out=x_sb[par][:],
                               in_=xb[st * 512:(st + 1) * 512, :].rearrange("(s p) d -> p s d", p=128)),
                   waits=xw, inc=s_x[par], k=16)
                nx = s_x[par].n
                trw = [] if st < 2 else [(s_tr[par], 16 * 4 * (st // 2))]
                xn2cnt = {}
                for s in range(4):
                    t = st * 4 + s
                    xn2cnt[s] = vop(lambda v, par=par, s=s, t=t:
                                    v.tensor_scalar(out=xn2_bf[par][:, s, :], in0=x_sb[par][:, s, :],
                                                    scalar1=mu_all[:, t:t + 1], scalar2=rr_all[:, t:t + 1],
                                                    op0=ALU.subtract, op1=ALU.mult),
                                    waits=[(s_x[par], nx)] + trw)
                lastcons[st] = xn2cnt[3]
                afw = [] if st < 2 else [(s_v, aff_done[st - 2])]
                for s in range(4):
                    go("sync", lambda y, par=par, s=s:
                       y.dma_start(out=xn2T[par][:, :, s * 128:(s + 1) * 128],
                                   in_=xn2_bf[par][:, s, :], transpose=True),
                       waits=[(s_v, xn2cnt[s])] + afw, inc=s_tr[par], k=16)
                ntr = s_tr[par].n
                pw = [] if st < 2 else [(s_mm, mm_v[st - 2])]
                if st == 0:
                    pw.append((s_wg, WG_ALL))
                if st >= 2 and (st - 2) < nst_own:
                    pw.append((s_x2w, x2w_cnt[st - 2]))
                for m in range(MD):
                    aff_done[st] = vop(lambda v, par=par, m=m:
                                       v.tensor_scalar(out=xn2Ta[par][:, m, :], in0=xn2T[par][:, m, :],
                                                       scalar1=lng_pp[:, m:m + 1], scalar2=lnb_pp[:, m:m + 1],
                                                       op0=ALU.mult, op1=ALU.add),
                                       waits=[(s_tr[par], ntr)] + pw)
                if own:
                    go("sync", lambda y, par=par, st=st:
                       y.dma_start(out=x2sp[st, :, :, :], in_=xn2Ta[par][:]),
                       waits=[(s_v, aff_done[st])], inc=s_x2w, k=16)
                    x2w_cnt[st] = s_x2w.n
                zw = [(s_v, aff_done[st])]
                if st == 0:
                    zw.append((s_wg, WG_ALL))
                if st >= 1:
                    zw.append((s_sc, tz_cnt[st - 1]))

                def emit_z(tensor, par=par, st=st):
                    for m in range(MD):
                        r = tensor.matmul(ps_z[:], Wz_sb[:, m, :], xn2Ta[par][:, m, :],
                                          start=(m == 0), stop=(m == MD - 1))
                    return r
                go("tensor", emit_z, waits=zw, inc=s_mm)
                mm_z[st] = s_mm.n
                tz_cnt[st] = tz = sop(
                    lambda sc: sc.activation(out=tmpz[:], in_=ps_z[:], func=AF.Silu, bias=binz_pp[:]),
                    waits=[(s_mm, mm_z[st])] + ([] if st < 1 else [(s_v, lastkq[st - 1])]))
                lastkq = locals().get("lastkq", {})
                lastkq[st] = vop(lambda v, st=st:
                                 v.tensor_scalar(out=kT[:, st * 512:(st + 1) * 512], in0=tmpz[:],
                                                 scalar1=gk_pp[:], scalar2=bk_pp[:],
                                                 op0=ALU.mult, op1=ALU.add),
                                 waits=[(s_sc, tz)])
                if own:
                    lastkq[st] = vop(lambda v, st=st:
                                     v.tensor_scalar(out=qT[:, st * 512:(st + 1) * 512], in0=tmpz[:],
                                                     scalar1=gq_pp[:], scalar2=bq_pp[:],
                                                     op0=ALU.mult, op1=ALU.add))
                for s in range(4):
                    t = st * 4 + s
                    spar = t % 2
                    for ec in range(4):
                        vpar = (s * 4 + ec) % 2
                        slot = st * 16 + s * 4 + ec
                        mw = []
                        if slot >= 2:
                            mw.append((s_sc, sc_v[slot - 2]))

                        def emit_v(tensor, par=par, s=s, ec=ec, vpar=vpar):
                            for m in range(MD):
                                tensor.matmul(ps_v[vpar][:], xn2Ta[par][:, m, s * 128:(s + 1) * 128],
                                              Wv_sb[:, m, ec * 512:(ec + 1) * 512],
                                              start=(m == 0), stop=False)
                            return tensor.matmul(ps_v[vpar][:], ones_row[:],
                                                 binv_row[:, ec * 512:(ec + 1) * 512],
                                                 start=False, stop=True)
                        go("tensor", emit_v, waits=mw, inc=s_mm)
                        mm_v[st] = s_mm.n
                        sw = [(s_mm, mm_v[st])]
                        if t >= 2:
                            sw.append((s_vw[spar], 16 * (t // 2)))
                        sc_v[slot] = sop(lambda sc, vpar=vpar, spar=spar, ec=ec:
                                         sc.activation(out=v_sb[spar][:, ec * 512:(ec + 1) * 512],
                                                       in_=ps_v[vpar][:], func=AF.Silu),
                                         waits=sw)
                    go("sync", lambda y, spar=spar, t=t:
                       y.dma_start(out=vsp[t * 128:(t + 1) * 128, :], in_=v_sb[spar][:]),
                       waits=[(s_sc, sc_v[st * 16 + s * 4 + 3])], inc=s_vw[spar], k=16)
                    spill_done[t] = (spar, s_vw[spar].n)
            go("sync", lambda y: y.nop(),
               waits=[(s_vw[0], s_vw[0].n), (s_vw[1], s_vw[1].n), (s_x2w, s_x2w.n)])
            if taps:
                go("sync", lambda y: y.dma_start(out=t_kT[:, :], in_=kT[:]),
                   waits=[(s_v, s_v.n)], inc=s_tap, k=16)
                go("sync", lambda y: y.dma_start(out=t_qT[:, :], in_=qT[:]), inc=s_tap, k=16)
                go("sync", lambda y: y.dma_start(out=t_v[:, :], in_=vsp[:, :]), inc=s_tap, k=16)
                go("sync", lambda y: y.dma_start(out=t_x2[:, :, :, :], in_=x2sp[:, :, :, :]), inc=s_tap, k=16)
                go("sync", lambda y: y.nop(), waits=[(s_tap, s_tap.n)])
            flush_block()
            ph1.close()

            with ExitStack() as ph2:
                E2 = ph2.enter_context
                Wu_sb = E2(nc.sbuf_tensor("Wu_sb", [128, MD, E], BF16))
                Wout_sb = E2(nc.sbuf_tensor("Wout_sb", [128, ME, D], BF16))
                x2_sb = [E2(nc.sbuf_tensor(f"x2_sb{i}", [128, MD, 512], BF16)) for i in range(2)]
                v_buf = [E2(nc.sbuf_tensor(f"v_buf{i}", [128, 4, 512], BF16)) for i in range(3)]
                srelu = [E2(nc.sbuf_tensor(f"srelu{i}", [128, 2048], BF16)) for i in range(2)]
                ssq = [E2(nc.sbuf_tensor(f"ssq{i}", [128, 2048], BF16)) for i in range(2)]
                uT_eq = E2(nc.sbuf_tensor("uT_eq", [128, 4, 512], BF16))
                uavT = E2(nc.sbuf_tensor("uavT", [128, ME, 512], BF16))
                y_sb = [E2(nc.sbuf_tensor(f"y_sb{i}", [128, D], F32)) for i in range(2)]
                ps_s = E2(nc.psum_tensor("ps_s", [128, 2048], F32))
                ps_av = E2(nc.psum_tensor("ps_av", [128, 2048], F32))

                wload(Wu_sb[:], W_in[:, 0:E].rearrange("(m p) e -> p m e", p=128), s_wg)
                wload(Wout_sb[:], W_out.rearrange("(m p) d -> p m d", p=128), s_wg)
                WG_PH2 = s_wg.n

                relu_cnt = {}
                sq_cnt = {}
                mm1_cnt = {}
                mm2_cnt = {}
                uav_cnt = {}
                usilu_cnt = {}
                wout_cnt = {}
                ycpy_cnt = {}
                vld_cnt = {}
                yw_i = 0
                for qb in range(nqb):
                    qpar = qb % 2
                    xw = [(s_x2w, s_x2w.n)]
                    if qb >= 2:
                        xw.append((s_mm, usilu_cnt[qb - 2][1]))
                    go("sync", lambda y, qb=qb, qpar=qpar:
                       y.dma_start(out=x2_sb[qpar][:], in_=x2sp[qb, :, :, :]),
                       waits=xw, inc=s_x2r[qpar], k=16)
                    nx2 = s_x2r[qpar].n
                    git = 0
                    for eq in range(4):
                        for es in range(4):
                            eu = eq * 4 + es
                            upar = es % 2
                            uw = [(s_x2r[qpar], nx2)]
                            if qb == 0 and eq == 0 and es == 0:
                                uw.append((s_wg, WG_PH2))
                            if eq > 0 or qb > 0:
                                uw.append((s_sc, s_sc.n))
                            if es >= 2:
                                uw.append((s_sc, usil[es - 2]))

                            def emit_u(tensor, qpar=qpar, es=es, upar=upar, eu=eu):
                                for m in range(MD):
                                    r = tensor.matmul(ps_s[:, upar * 512:(upar + 1) * 512],
                                                      Wu_sb[:, m, eu * 128:(eu + 1) * 128],
                                                      x2_sb[qpar][:, m, :],
                                                      start=(m == 0), stop=(m == MD - 1))
                                return r
                            go("tensor", emit_u, waits=uw, inc=s_mm)
                            numm = s_mm.n
                            usil = locals().get("usil", {})
                            suw = [(s_mm, numm)]
                            if eq > 0:
                                suw.append((s_v, uav_cnt[(qb, eq - 1)]))
                            elif qb > 0:
                                suw.append((s_v, uav_cnt[(qb - 1, 3)]))
                            usil[es] = sop(lambda sc, upar=upar, es=es, eu=eu:
                                           sc.activation(out=uT_eq[:, es, :],
                                                         in_=ps_s[:, upar * 512:(upar + 1) * 512],
                                                         func=AF.Silu, bias=binu_pp[:, eu:eu + 1]),
                                           waits=suw)
                        usilu_cnt[qb] = (usil[3], s_mm.n)
                        def emit_vload(g):
                            vp = g % 3
                            vw = []
                            if g >= 3:
                                vw.append((s_mm, mm2_cnt[(qb, eq, g - 3)]))
                            elif (qb, eq) != (0, 0):
                                vw.append((s_mm, prev_mm2_last))
                            go("sync", lambda y, g=g, eq=eq, vp=vp:
                               y.dma_start(out=v_buf[vp][:],
                                           in_=vsp[g * 512:(g + 1) * 512, eq * 512:(eq + 1) * 512]
                                           .rearrange("(k p) e -> p k e", p=128)),
                               waits=vw, inc=s_vr[vp], k=16)
                            vld_cnt[(qb, eq, g)] = (vp, s_vr[vp].n)

                        def emit_mm2_for(g):
                            vp, vn = vld_cnt[(qb, eq, g)]
                            m2w = [(s_v, sq_cnt[(qb, eq, g)]), (s_vr[vp], vn)]
                            if g == 0:
                                if eq > 0:
                                    m2w.append((s_v, uav_cnt[(qb, eq - 1)]))
                                elif qb > 0:
                                    m2w.append((s_v, ycpy_cnt[qb - 1]))

                            def emit_m2(tensor, g=g, vp=vp, eq=eq):
                                for j in range(4):
                                    for es in range(4):
                                        r = tensor.matmul(ps_av[:, es * 512:(es + 1) * 512],
                                                          v_buf[vp][:, j, es * 128:(es + 1) * 128],
                                                          ssq[(g % 2)][:, j * 512:(j + 1) * 512],
                                                          start=(g == 0 and j == 0),
                                                          stop=(g == G - 1 and j == 3),
                                                          skip_group_check=True)
                                return r
                            go("tensor", emit_m2, waits=m2w, inc=s_mm)
                            mm2_cnt[(qb, eq, g)] = s_mm.n

                        emit_vload(0)
                        for g in range(G):
                            gpar = g % 2
                            m1w = []
                            if g == 0:
                                m1w.append((s_sc, usil[3]))
                            else:
                                m1w.append((s_sc, relu_cnt[(qb, eq, g - 1)]))

                            def emit_m1(tensor, qb=qb, g=g):
                                for j in range(4):
                                    kt = g * 4 + j
                                    r = tensor.matmul(ps_s[:, j * 512:(j + 1) * 512],
                                                      kT[:, kt * 128:(kt + 1) * 128],
                                                      qT[:, qb * 512:(qb + 1) * 512],
                                                      start=True, stop=True)
                                return r
                            go("tensor", emit_m1, waits=m1w, inc=s_mm)
                            mm1_cnt[(qb, eq, g)] = s_mm.n
                            rw = [(s_mm, mm1_cnt[(qb, eq, g)])]
                            if g >= 2:
                                rw.append((s_v, sq_cnt[(qb, eq, g - 2)]))
                            relu_cnt[(qb, eq, g)] = sop(lambda sc, gpar=gpar:
                                                        sc.activation(out=srelu[gpar][:], in_=ps_s[:],
                                                                      func=AF.Relu, scale=invl_pp[:]),
                                                        waits=rw)
                            qw = [(s_sc, relu_cnt[(qb, eq, g)])]
                            if g >= 2:
                                qw.append((s_mm, mm2_cnt[(qb, eq, g - 2)]))
                            sq_cnt[(qb, eq, g)] = vop(lambda v, gpar=gpar:
                                                      v.tensor_tensor(out=ssq[gpar][:], in0=srelu[gpar][:],
                                                                      in1=srelu[gpar][:], op=ALU.mult),
                                                      waits=qw)
                            if g >= 1:
                                emit_mm2_for(g - 1)
                            if g + 1 < G:
                                emit_vload(g + 1)
                        emit_mm2_for(G - 1)
                        prev_mm2_last = mm2_cnt[(qb, eq, G - 1)]
                        for es in range(4):
                            uav_cnt[(qb, eq)] = vop(lambda v, es=es, eq=eq:
                                                    v.tensor_tensor(out=uavT[:, eq * 4 + es, :],
                                                                    in0=ps_av[:, es * 512:(es + 1) * 512],
                                                                    in1=uT_eq[:, es, :], op=ALU.mult),
                                                    waits=[(s_mm, prev_mm2_last)] +
                                                          ([] if qb == 0 else [(s_mm, wout_cnt.get(qb - 1, 0))]))
                    for qt in range(4):
                        ypar = qt % 2
                        yw_ = [(s_v, uav_cnt[(qb, 3)])]
                        if qt >= 1:
                            yw_.append((s_v, ycpy[qt - 1]))

                        def emit_y2(tensor, qb=qb, qt=qt):
                            r = None
                            for hc in range(2):
                                for m in range(ME):
                                    r = tensor.matmul(ps_av[:, hc * 512:(hc + 1) * 512],
                                                      uavT[:, m, qt * 128:(qt + 1) * 128],
                                                      Wout_sb[:, m, hc * 512:(hc + 1) * 512],
                                                      start=(m == 0), stop=False,
                                                      skip_group_check=True)
                                r = tensor.matmul(ps_av[:, hc * 512:(hc + 1) * 512], ones_row[:],
                                                  bout_row[:, hc * 512:(hc + 1) * 512],
                                                  start=False, stop=True, skip_group_check=True)
                            return r
                        go("tensor", emit_y2, waits=yw_, inc=s_mm)
                        wout_cnt[qb] = s_mm.n
                        ycpy = locals().get("ycpy", {})
                        cw = [(s_mm, wout_cnt[qb])]
                        if yw_i >= 2:
                            cw.append((s_yw[ypar], 16 * (yw_i // 2)))
                        ycpy[qt] = vop(lambda v, ypar=ypar:
                                       v.tensor_copy(out=y_sb[ypar][:], in_=ps_av[:, 0:1024]),
                                       waits=cw)
                        ycpy_cnt[qb] = ycpy[qt]
                        go("sync", lambda y, qb=qb, qt=qt, ypar=ypar:
                           y.dma_start(out=ysp[qb * 512 + qt * 128: qb * 512 + (qt + 1) * 128, :],
                                       in_=y_sb[ypar][:]),
                           waits=[(s_v, ycpy[qt])], inc=s_yw[ypar], k=16)
                        yw_i += 1
                go("sync", lambda y: y.nop(),
                   waits=[(s_yw[0], s_yw[0].n), (s_yw[1], s_yw[1].n)])
                if taps:
                    go("sync", lambda y: y.dma_start(out=t_y[:, :], in_=ysp[:, :]), inc=s_tap, k=16)
                flush_block()

        with ExitStack() as ph3:
            E3 = ph3.enter_context
            xnT_blk = E3(nc.sbuf_tensor("xnT_blk", [128, MD, TB], BF16))
            acc_sb = E3(nc.sbuf_tensor("acc_sb", [128, ntile_b, D], F32))
            x3_sb = [E3(nc.sbuf_tensor(f"x3_sb{i}", [128, 2, D], F32)) for i in range(2)]
            xn_bf = [E3(nc.sbuf_tensor(f"xn_bf{i}", [128, 2, D], BF16)) for i in range(2)]
            w_sb = E3(nc.sbuf_tensor("w_sb", [128, ntile_b, NE], F32))
            w33 = E3(nc.sbuf_tensor("w33", [33, TB], BF16))
            aff_sb = E3(nc.sbuf_tensor("aff_sb", [128, NE], F32))
            a2_sb = E3(nc.sbuf_tensor("a2_sb", [128, NE], F32))
            mx_sb = E3(nc.sbuf_tensor("mx_sb", [128, 1], F32))
            msk_sb = E3(nc.sbuf_tensor("msk_sb", [128, NE], F32))
            sum_sb = E3(nc.sbuf_tensor("sum_sb", [128, 1], F32))
            epr_sb = E3(nc.sbuf_tensor("epr_sb", [128, 1], F32))
            Wgs_sb = E3(nc.sbuf_tensor("Wgs_sb", [128, MD, II], BF16))
            Wus_sb = E3(nc.sbuf_tensor("Wus_sb", [128, MD, II], BF16))
            Wds_sb = E3(nc.sbuf_tensor("Wds_sb", [128, MI, D], BF16))
            Wge_sb = [E3(nc.sbuf_tensor(f"Wge_sb{i}", [128, MD, II], BF16)) for i in range(2)]
            Wue_sb = [E3(nc.sbuf_tensor(f"Wue_sb{i}", [128, MD, II], BF16)) for i in range(2)]
            Wde_sb = [E3(nc.sbuf_tensor(f"Wde_sb{i}", [128, MI, D], BF16)) for i in range(2)]
            g_sb = [E3(nc.sbuf_tensor(f"g_sb{i}", [128, II], BF16)) for i in range(2)]
            h_sb = [E3(nc.sbuf_tensor(f"h_sb{i}", [128, MI, II], BF16)) for i in range(2)]
            y3_sb = [E3(nc.sbuf_tensor(f"y3_sb{i}", [128, D], F32)) for i in range(2)]
            o_sb = [E3(nc.sbuf_tensor(f"o_sb{i}", [128, D], F32)) for i in range(2)]
            ps_a = E3(nc.psum_tensor("ps_a", [128, 512], F32))
            ps_b = E3(nc.psum_tensor("ps_b", [128, 512], F32))
            ps_d = [E3(nc.psum_tensor(f"ps_d{i}", [128, D], F32)) for i in range(2)]

            gop(lambda g: g.memset(epr_sb[:], 1e-9))
            GP1 = s_gp.n
            gop(lambda g: g.memset(w33[32:33, :], 1.0))
            GP2 = s_gp.n
            wload(Wgs_sb[:], Wg_s.rearrange("(m p) i -> p m i", p=128), s_wg)
            wload(Wus_sb[:], Wu_s.rearrange("(m p) i -> p m i", p=128), s_wg)
            wload(Wds_sb[:], Wd_s.rearrange("(m p) d -> p m d", p=128), s_wg)
            WG_PH3 = s_wg.n

            def eload(e, buf, waits=()):
                sem = s_we[buf]
                wload(Wge_sb[buf][:], Wg[e].rearrange("(m p) i -> p m i", p=128), sem, waits=waits)
                wload(Wue_sb[buf][:], Wu[e].rearrange("(m p) i -> p m i", p=128), sem)
                wload(Wde_sb[buf][:], Wd[e].rearrange("(m p) d -> p m d", p=128), sem)
                return (sem, sem.n)

            def eload2(e, buf, waits):
                return eload(e, buf, waits=waits)

            eweight_ready = {}
            sweep_idx = 0
            out_i = 0
            for tb in range(ntb):
                lastxn = {}
                ngrp = TB // 256
                for grp in range(ngrp):
                    gpr = grp % 2
                    t0 = tb * ntile_b + grp * 2
                    xw = [] if grp < 2 else [(s_v, lastxn[grp - 2])]
                    go("sync", lambda y, tb=tb, grp=grp, gpr=gpr:
                       y.dma_start(out=x3_sb[gpr][:],
                                   in_=xb[tb * TB + grp * 256: tb * TB + (grp + 1) * 256, :]
                                   .rearrange("(s p) d -> p s d", p=128)),
                       waits=xw, inc=s_x3[gpr], k=16)
                    nx3 = s_x3[gpr].n
                    xncnt = {}
                    trw = [] if grp < 2 else [(s_tr3[gpr], 16 * 2 * (grp // 2))]
                    for s in range(2):
                        t = t0 + s
                        xncnt[s] = vop(lambda v, gpr=gpr, s=s, t=t:
                                       v.tensor_scalar(out=xn_bf[gpr][:, s, :], in0=x3_sb[gpr][:, s, :],
                                                       scalar1=mu_all[:, t:t + 1], scalar2=r_all[:, t:t + 1],
                                                       op0=ALU.subtract, op1=ALU.mult),
                                       waits=[(s_x3[gpr], nx3)] + trw)
                    lastxn[grp] = xncnt[1]
                    for s in range(2):
                        go("sync", lambda y, gpr=gpr, grp=grp, s=s:
                           y.dma_start(out=xnT_blk[:, :, grp * 256 + s * 128: grp * 256 + (s + 1) * 128],
                                       in_=xn_bf[gpr][:, s, :], transpose=True),
                           waits=[(s_v, xncnt[s])], inc=s_tr3[gpr], k=16)
                tr3_all = (s_tr3[0].n, s_tr3[1].n)
                eweight_ready[(tb, 0)] = eload(0, 0)
                eweight_ready[(tb, 1)] = eload(1, 1)
                wdone = {}
                for tt in range(ntile_b):
                    grp = tt // 2
                    gpr = grp % 2
                    rw = [(s_tr3[gpr], 16 * 4 * (grp // 2 + 1) if False else s_tr3[gpr].n)]
                    if tb == 0 and tt == 0:
                        rw.append((s_wg, WG_PH3))
                        rw.append((s_gp, GP2))
                    if tt >= 1:
                        rw.append((s_sc, sig_cnt[tt - 1]))

                    def emit_rt(tensor, tt=tt):
                        for m in range(MD):
                            tensor.matmul(ps_a[:, 0:NE], xnT_blk[:, m, tt * 128:(tt + 1) * 128],
                                          Wr_sb[:, m, :], start=(m == 0), stop=False)
                        return tensor.matmul(ps_a[:, 0:NE], ones_row[:], br_row[:],
                                             start=False, stop=True)
                    go("tensor", emit_rt, waits=rw, inc=s_mm)
                    nrt = s_mm.n
                    sig_cnt = locals().get("sig_cnt", {})
                    sig_cnt[tt] = sop(lambda sc: sc.activation(out=aff_sb[:], in_=ps_a[:, 0:NE],
                                                               func=AF.Sigmoid),
                                      waits=[(s_mm, nrt)] + ([] if tt == 0 else [(s_v, wtk[tt - 1])]))
                    wtk = locals().get("wtk", {})
                    vop(lambda v: v.tensor_copy(out=a2_sb[:], in_=aff_sb[:]),
                        waits=[(s_sc, sig_cnt[tt]), (s_gp, GP1)])
                    for it in range(TOPK - 1):
                        vop(lambda v: v.tensor_reduce(out=mx_sb[:], in_=a2_sb[:], axis=AX.X, op=ALU.max))
                        vop(lambda v: v.tensor_scalar(out=msk_sb[:], in0=a2_sb[:], scalar1=mx_sb[:],
                                                      scalar2=None, op0=ALU.is_ge))
                        vop(lambda v: v.scalar_tensor_tensor(out=a2_sb[:], in0=msk_sb[:], scalar=-1e30,
                                                             in1=a2_sb[:], op0=ALU.mult, op1=ALU.add))
                    vop(lambda v: v.tensor_reduce(out=mx_sb[:], in_=a2_sb[:], axis=AX.X, op=ALU.max))
                    vop(lambda v: v.tensor_scalar(out=msk_sb[:], in0=aff_sb[:], scalar1=mx_sb[:],
                                                  scalar2=None, op0=ALU.is_ge))
                    vop(lambda v, tt=tt: v.tensor_tensor(out=w_sb[:, tt, :], in0=aff_sb[:],
                                                         in1=msk_sb[:], op=ALU.mult))
                    vop(lambda v, tt=tt: v.tensor_reduce(out=sum_sb[:], in_=w_sb[:, tt, :],
                                                         axis=AX.X, op=ALU.add))
                    vop(lambda v: v.tensor_tensor(out=sum_sb[:], in0=sum_sb[:], in1=epr_sb[:], op=ALU.add))
                    vop(lambda v: v.reciprocal(out=sum_sb[:], in_=sum_sb[:]))
                    wtk[tt] = vop(lambda v, tt=tt: v.tensor_scalar(out=w_sb[:, tt, :], in0=w_sb[:, tt, :],
                                                                   scalar1=sum_sb[:], scalar2=None,
                                                                   op0=ALU.mult))
                    tw = [(s_v, wtk[tt]), (s_gp, GP0)]
                    if tt >= 1:
                        tw.append((s_sc, wev[tt - 1]))
                    go("tensor", lambda tensor, tt=tt:
                       tensor.transpose(ps_b[0:NE, 0:128], w_sb[:, tt, :], idn[:]),
                       waits=tw, inc=s_mm)
                    ntw = s_mm.n
                    wev = locals().get("wev", {})
                    wev[tt] = sop(lambda sc, tt=tt: sc.copy(out=w33[0:NE, tt * 128:(tt + 1) * 128],
                                                            in_=ps_b[0:NE, 0:128]),
                                  waits=[(s_mm, ntw)])
                    wdone[tt] = (wev[tt], wtk[tt])
                if taps:
                    go("sync", lambda y, tb=tb: y.dma_start(
                        out=t_w[:, tb * ntile_b:(tb + 1) * ntile_b, :], in_=w_sb[:]),
                       waits=[(s_v, s_v.n)], inc=s_tap, k=16)
                    go("sync", lambda y: y.nop(), waits=[(s_tap, s_tap.n)])

                hstt_cnt = {}
                gsil_cnt = {}
                dacc_cnt = {}
                slot = 0
                dslot = 0
                for ei in range(-1, NE):
                    if ei >= 0:
                        sweep_idx += 1
                    ebuf = ei % 2 if ei >= 0 else 0
                    Wg_cur = Wgs_sb if ei < 0 else Wge_sb[ebuf]
                    Wu_cur = Wus_sb if ei < 0 else Wue_sb[ebuf]
                    Wd_cur = Wds_sb if ei < 0 else Wde_sb[ebuf]
                    for c in range(nch):
                        hpar = (ei * nch + c + NE * nch) % 2
                        for ic in range(MI):
                            gw = []
                            if ei >= 0:
                                gw.append(eweight_ready[(tb, ei)])
                            else:
                                gw.append((s_wg, WG_PH3))
                                gw.append((s_tr3[0], tr3_all[0]))
                                if nch > 1:
                                    gw.append((s_tr3[1], tr3_all[1]))
                            if slot >= 1:
                                gw.append((s_sc, gsil_cnt[slot - 1]))
                            else:
                                gw.append((s_sc, wev[ntile_b - 1]))

                            def emit_g(tensor, Wg_cur=Wg_cur, c=c, ic=ic):
                                for m in range(MD):
                                    r = tensor.matmul(ps_a[:], Wg_cur[:, m, ic * 128:(ic + 1) * 128],
                                                      xnT_blk[:, m, c * 512:(c + 1) * 512],
                                                      start=(m == 0), stop=(m == MD - 1))
                                return r
                            go("tensor", emit_g, waits=gw, inc=s_mm)
                            ngm = s_mm.n
                            bias_ap = (bgs_pp[:, ic:ic + 1] if ei < 0 else bge_pp[:, ei, ic:ic + 1])
                            sgw = [(s_mm, ngm)]
                            if slot >= 2:
                                sgw.append((s_v, hstt_cnt[slot - 2]))
                            gsil_cnt[slot] = sop(lambda sc, bias_ap=bias_ap, gp=slot % 2:
                                                 sc.activation(out=g_sb[gp][:], in_=ps_a[:],
                                                               func=AF.Silu, bias=bias_ap),
                                                 waits=sgw)
                            uw = []
                            if ei >= 0:
                                uw.append(eweight_ready[(tb, ei)])
                            if slot >= 1:
                                uw.append((s_v, hstt_cnt[slot - 1]))
                            else:
                                uw.append((s_sc, wev[ntile_b - 1]))

                            def emit_up(tensor, Wu_cur=Wu_cur, c=c, ic=ic):
                                for m in range(MD):
                                    r = tensor.matmul(ps_b[:], Wu_cur[:, m, ic * 128:(ic + 1) * 128],
                                                      xnT_blk[:, m, c * 512:(c + 1) * 512],
                                                      start=(m == 0), stop=(m == MD - 1))
                                return r
                            go("tensor", emit_up, waits=uw, inc=s_mm)
                            num = s_mm.n
                            bu_ap = (bus_pp[:, ic:ic + 1] if ei < 0 else bue_pp[:, ei, ic:ic + 1])
                            hw = [(s_mm, num), (s_sc, gsil_cnt[slot])]
                            key = (hpar, "lastdown")
                            if key in hstt_cnt:
                                hw.append((s_mm, hstt_cnt[key]))
                            hstt_cnt[slot] = vop(lambda v, bu_ap=bu_ap, gp=slot % 2, hpar=hpar, ic=ic:
                                                 v.scalar_tensor_tensor(out=h_sb[hpar][:, ic, :],
                                                                        in0=ps_b[:], scalar=bu_ap,
                                                                        in1=g_sb[gp][:],
                                                                        op0=ALU.add, op1=ALU.mult),
                                                 waits=hw)
                            slot += 1
                        h_ready = hstt_cnt[slot - 1]
                        for tt4 in range(4):
                            tt = c * 4 + tt4
                            dpar = dslot % 2
                            dw = [(s_v, h_ready)]
                            if dslot >= 2:
                                dw.append((s_v, dacc_cnt[dslot - 2]))
                            if ei < 0:
                                dw.append((s_sc, wdone[tt][0]))

                            def emit_d(tensor, Wd_cur=Wd_cur, hpar=hpar, tt=tt, tt4=tt4, c=c, ei=ei, tb=tb, dpar=dpar):
                                r = None
                                for hc in range(2):
                                    if ei < 0:
                                        tensor.matmul(ps_d[dpar][:, hc * 512:(hc + 1) * 512],
                                                      w33[:, tt * 128:(tt + 1) * 128],
                                                      bd33[:, hc * 512:(hc + 1) * 512],
                                                      start=True, stop=False, skip_group_check=True)
                                    for ic in range(MI):
                                        r = tensor.matmul(ps_d[dpar][:, hc * 512:(hc + 1) * 512],
                                                          h_sb[hpar][:, ic, tt4 * 128:(tt4 + 1) * 128],
                                                          Wd_cur[:, ic, hc * 512:(hc + 1) * 512],
                                                          start=(ei >= 0 and ic == 0), stop=(ic == MI - 1),
                                                          skip_group_check=True)
                                return r
                            go("tensor", emit_d, waits=dw, inc=s_mm)
                            ndm = s_mm.n
                            hstt_cnt[(hpar, "lastdown")] = ndm
                            if ei < 0:
                                dacc_cnt[dslot] = vop(lambda v, tt=tt, dpar=dpar:
                                                      v.tensor_copy(out=acc_sb[:, tt, :], in_=ps_d[dpar][:]),
                                                      waits=[(s_mm, ndm)])
                            else:
                                dacc_cnt[dslot] = vop(lambda v, tt=tt, dpar=dpar, ei=ei:
                                                      v.scalar_tensor_tensor(out=acc_sb[:, tt, :],
                                                                             in0=ps_d[dpar][:],
                                                                             scalar=w_sb[:, tt, ei:ei + 1],
                                                                             in1=acc_sb[:, tt, :],
                                                                             op0=ALU.mult, op1=ALU.add),
                                                      waits=[(s_mm, ndm)])
                            acc_last = locals().get("acc_last", {})
                            acc_last[tt] = dacc_cnt[dslot]
                            dslot += 1
                    if ei >= 0 and ei + 2 < NE:
                        eweight_ready[(tb, ei + 2)] = eload2(ei + 2, (ei + 2) % 2,
                                                             waits=[(s_mm, s_mm.n)])

                for tt in range(ntile_b):
                    tglob = tb * ntile_b + tt
                    cpar = tt % 2
                    yw_ = [(s_v, acc_last[tt])] if tt < 2 else [(s_v, comb_done[tt - 2])]
                    go("sync", lambda y, tglob=tglob, cpar=cpar:
                       y.dma_start(out=y3_sb[cpar][:], in_=ysp[tglob * 128:(tglob + 1) * 128, :]),
                       waits=yw_, inc=s_yr[cpar], k=16)
                    nyr = s_yr[cpar].n
                    go("sync", lambda y, tglob=tglob, cpar=cpar:
                       y.dma_start(out=x3_sb[cpar][:, 0, :],
                                   in_=xb[tglob * 128:(tglob + 1) * 128, :]),
                       waits=yw_, inc=s_x4[cpar], k=16)
                    nx4 = s_x4[cpar].n
                    comb_done = locals().get("comb_done", {})
                    ow = [] if out_i < 2 else [(s_out[cpar], 16 * (out_i // 2))]
                    vop(lambda v, tt=tt, cpar=cpar:
                        v.tensor_tensor(out=o_sb[cpar][:], in0=acc_sb[:, tt, :], in1=y3_sb[cpar][:],
                                        op=ALU.add),
                        waits=[(s_yr[cpar], nyr), (s_v, acc_last[tt])] + ow)
                    vop(lambda v, tt=tt, cpar=cpar, tglob=tglob:
                        v.scalar_tensor_tensor(out=o_sb[cpar][:], in0=x3_sb[cpar][:, 0, :],
                                               scalar=r1_all[:, tglob:tglob + 1], in1=o_sb[cpar][:],
                                               op0=ALU.mult, op1=ALU.add),
                        waits=[(s_x4[cpar], nx4)])
                    comb_done[tt] = vop(lambda v, tt=tt, cpar=cpar, tglob=tglob:
                                        v.tensor_scalar(out=o_sb[cpar][:], in0=o_sb[cpar][:],
                                                        scalar1=mur_all[:, tglob:tglob + 1], scalar2=None,
                                                        op0=ALU.subtract))
                    go("sync", lambda y, tglob=tglob, cpar=cpar:
                       y.dma_start(out=out[tglob * 128:(tglob + 1) * 128, :], in_=o_sb[cpar][:]),
                       waits=[(s_v, comb_done[tt])], inc=s_out[cpar], k=16)
                    out_i += 1
                flush_block()

        fw = [(s_out[0], s_out[0].n), (s_out[1], s_out[1].n)]
        if taps:
            fw.append((s_tap, s_tap.n))
        go("sync", lambda y: y.nop(), waits=fw)
        flush_block()

    return nc



def relayout_params(inputs):
    d = {}
    d["p_lng"] = np.ascontiguousarray(np.asarray(inputs["L0_attn_ln_g"]).reshape(MD, 128).T)
    d["p_lnb"] = np.ascontiguousarray(np.asarray(inputs["L0_attn_ln_b"]).reshape(MD, 128).T)
    d["p_binu"] = np.ascontiguousarray(np.asarray(inputs["L0_attn_b_in"])[0:E].reshape(ME, 128).T)
    d["p_bgs"] = np.ascontiguousarray(np.asarray(inputs["L0_ffn_bg_s"]).reshape(MI, 128).T)
    d["p_bus"] = np.ascontiguousarray(np.asarray(inputs["L0_ffn_bu_s"]).reshape(MI, 128).T)
    d["p_bge"] = np.ascontiguousarray(np.asarray(inputs["L0_ffn_bg"]).reshape(NE, MI, 128).transpose(2, 0, 1))
    d["p_bue"] = np.ascontiguousarray(np.asarray(inputs["L0_ffn_bu"]).reshape(NE, MI, 128).transpose(2, 0, 1))
    return d


def kernel(**inputs):
    L, TOWN = 16384, 4096
    x = np.asarray(inputs["L0_attn_x"])
    wnames = [k for k in inputs if k != "L0_attn_x"]
    nc = build(L=L, TOWN=TOWN, TB=1024, taps=False)
    derived = relayout_params(inputs)
    in_maps = []
    for c in range(8):
        b, q0 = c // 4, (c % 4) * TOWN
        xb_c = np.ascontiguousarray(np.concatenate([x[b, q0:], x[b, :q0]], axis=0))
        m = {"xb": xb_c}
        for k in wnames:
            m[k] = np.ascontiguousarray(np.asarray(inputs[k]))
        m.update(derived)
        in_maps.append(m)
    res = run_bass_kernel_spmd(nc, in_maps, core_ids=list(range(8)))
    outp = np.empty((2, L, D), dtype=np.float32)
    for c in range(8):
        b, q0 = c // 4, (c % 4) * TOWN
        outp[b, q0:q0 + TOWN] = res.results[c]["out"]
    return outp


# version 03
# speedup vs baseline: 1.3325x; 1.0050x over previous
import numpy as np
from contextlib import ExitStack

import concourse.bass as bass
import concourse.mybir as mybir
from concourse.bass_utils import run_bass_kernel_spmd

F32 = mybir.dt.float32
BF16 = mybir.dt.bfloat16
AF = mybir.ActivationFunctionType
ALU = mybir.AluOpType
AX = mybir.AxisListType

D = 1024
MD = 8
E = 2048
ME = E // 128
S = 128
NE = 32
II = 512
MI = II // 128
TOPK = 8
EPS = 1e-5


def build(L=16384, TOWN=4096, TB=2048, taps=False):
    assert L % 512 == 0 and TOWN % 512 == 0 and TB % 512 == 0 and TOWN % TB == 0
    nst = L // 512
    nst_own = TOWN // 512
    nqb = TOWN // 512
    G = L // 512
    ntb = TOWN // TB
    nch = TB // 512
    ntile_b = TB // 128
    NTL = L // 128
    INVL = 1.0 / L

    nc = bass.Bass("TRN2", target_bir_lowering=False)

    xb = nc.dram_tensor("xb", [L, D], F32, kind="ExternalInput").ap()
    W_in = nc.dram_tensor("L0_attn_W_in", [D, 2 * E + S], F32, kind="ExternalInput").ap()
    b_in = nc.dram_tensor("L0_attn_b_in", [2 * E + S], F32, kind="ExternalInput").ap()
    W_out = nc.dram_tensor("L0_attn_W_out", [E, D], F32, kind="ExternalInput").ap()
    b_out = nc.dram_tensor("L0_attn_b_out", [D], F32, kind="ExternalInput").ap()
    ln_g = nc.dram_tensor("L0_attn_ln_g", [D], F32, kind="ExternalInput").ap()
    ln_b = nc.dram_tensor("L0_attn_ln_b", [D], F32, kind="ExternalInput").ap()
    gq = nc.dram_tensor("L0_attn_gamma_q", [S], F32, kind="ExternalInput").ap()
    bq = nc.dram_tensor("L0_attn_beta_q", [S], F32, kind="ExternalInput").ap()
    gk = nc.dram_tensor("L0_attn_gamma_k", [S], F32, kind="ExternalInput").ap()
    bk = nc.dram_tensor("L0_attn_beta_k", [S], F32, kind="ExternalInput").ap()
    Wr = nc.dram_tensor("L0_ffn_Wr", [D, NE], F32, kind="ExternalInput").ap()
    br = nc.dram_tensor("L0_ffn_br", [NE], F32, kind="ExternalInput").ap()
    Wg_s = nc.dram_tensor("L0_ffn_Wg_s", [D, II], F32, kind="ExternalInput").ap()
    bg_s = nc.dram_tensor("L0_ffn_bg_s", [II], F32, kind="ExternalInput").ap()
    Wu_s = nc.dram_tensor("L0_ffn_Wu_s", [D, II], F32, kind="ExternalInput").ap()
    bu_s = nc.dram_tensor("L0_ffn_bu_s", [II], F32, kind="ExternalInput").ap()
    Wd_s = nc.dram_tensor("L0_ffn_Wd_s", [II, D], F32, kind="ExternalInput").ap()
    bd_s = nc.dram_tensor("L0_ffn_bd_s", [D], F32, kind="ExternalInput").ap()
    Wg = nc.dram_tensor("L0_ffn_Wg", [NE, D, II], F32, kind="ExternalInput").ap()
    bg = nc.dram_tensor("L0_ffn_bg", [NE, II], F32, kind="ExternalInput").ap()
    Wu = nc.dram_tensor("L0_ffn_Wu", [NE, D, II], F32, kind="ExternalInput").ap()
    bu = nc.dram_tensor("L0_ffn_bu", [NE, II], F32, kind="ExternalInput").ap()
    Wd = nc.dram_tensor("L0_ffn_Wd", [NE, II, D], F32, kind="ExternalInput").ap()
    bd = nc.dram_tensor("L0_ffn_bd", [NE, D], F32, kind="ExternalInput").ap()
    p_lng = nc.dram_tensor("p_lng", [128, MD], F32, kind="ExternalInput").ap()
    p_lnb = nc.dram_tensor("p_lnb", [128, MD], F32, kind="ExternalInput").ap()
    p_binu = nc.dram_tensor("p_binu", [128, ME], F32, kind="ExternalInput").ap()
    p_bgs = nc.dram_tensor("p_bgs", [128, MI], F32, kind="ExternalInput").ap()
    p_bus = nc.dram_tensor("p_bus", [128, MI], F32, kind="ExternalInput").ap()
    p_bge = nc.dram_tensor("p_bge", [128, NE, MI], F32, kind="ExternalInput").ap()
    p_bue = nc.dram_tensor("p_bue", [128, NE, MI], F32, kind="ExternalInput").ap()
    out = nc.dram_tensor("out", [TOWN, D], F32, kind="ExternalOutput").ap()
    vsp = nc.dram_tensor("vsp", [L, E], BF16, kind="Internal").ap()
    x2sp = nc.dram_tensor("x2sp", [nst_own, 128, MD, 512], BF16, kind="Internal").ap()
    ysp = nc.dram_tensor("ysp", [TOWN, D], F32, kind="Internal").ap()
    if taps:
        t_kT = nc.dram_tensor("t_kT", [128, L], BF16, kind="ExternalOutput").ap()
        t_qT = nc.dram_tensor("t_qT", [128, TOWN], BF16, kind="ExternalOutput").ap()
        t_w = nc.dram_tensor("t_w", [128, TOWN // 128, NE], F32, kind="ExternalOutput").ap()
        t_y = nc.dram_tensor("t_y", [TOWN, D], F32, kind="ExternalOutput").ap()
        t_v = nc.dram_tensor("t_v", [L, E], BF16, kind="ExternalOutput").ap()
        t_x2 = nc.dram_tensor("t_x2", [nst_own, 128, MD, 512], BF16, kind="ExternalOutput").ap()

    with ExitStack() as ctx:
        EC = ctx.enter_context

        class Sem:
            def __init__(self, name):
                self.h = EC(nc.semaphore(name))
                self.n = 0

        s_wg = Sem("s_wg")
        s_we = [Sem("s_we0"), Sem("s_we1")]
        s_gp = Sem("s_gp")
        s_x = [Sem("s_x0"), Sem("s_x1")]
        s_tr = [Sem("s_tr0"), Sem("s_tr1")]
        s_vw = [Sem("s_vw0"), Sem("s_vw1")]
        s_x2w = Sem("s_x2w")
        s_x2r = [Sem("s_x2r0"), Sem("s_x2r1")]
        s_vr = [Sem("s_vr0"), Sem("s_vr1"), Sem("s_vr2")]
        s_yw = [Sem("s_yw0"), Sem("s_yw1")]
        s_yr = [Sem("s_yr0"), Sem("s_yr1")]
        s_x3 = [Sem("s_x30"), Sem("s_x31")]
        s_tr3 = [Sem("s_tr30"), Sem("s_tr31")]
        s_x4 = [Sem("s_x40"), Sem("s_x41")]
        s_trr = [Sem("s_trr0"), Sem("s_trr1")]
        s_out = [Sem("s_out0"), Sem("s_out1")]
        s_tap = Sem("s_tap")
        s_mm = Sem("s_mm")
        s_sc = Sem("s_sc")
        s_v = Sem("s_v")

        ones_row = EC(nc.sbuf_tensor("ones_row", [1, 128], BF16))
        eps_pp = EC(nc.sbuf_tensor("eps_pp", [128, 1], F32))
        invl_pp = EC(nc.sbuf_tensor("invl_pp", [128, 1], F32))
        idn = EC(nc.sbuf_tensor("idn", [128, 128], F32))
        bnsc = EC(nc.sbuf_tensor("bnsc", [128, 12], F32))
        mv_all = EC(nc.sbuf_tensor("mv_all", [128, NTL, 2], F32))
        mu_all = EC(nc.sbuf_tensor("mu_all", [128, NTL], F32))
        r_all = EC(nc.sbuf_tensor("r_all", [128, NTL], F32))
        rr_all = EC(nc.sbuf_tensor("rr_all", [128, NTL], F32))
        r1_all = EC(nc.sbuf_tensor("r1_all", [128, NTL], F32))
        mur_all = EC(nc.sbuf_tensor("mur_all", [128, NTL], F32))
        sd_all = EC(nc.sbuf_tensor("sd_all", [128, NTL], F32))
        v2_all = EC(nc.sbuf_tensor("v2_all", [128, NTL], F32))
        lng_pp = EC(nc.sbuf_tensor("lng_pp", [128, MD], F32))
        lnb_pp = EC(nc.sbuf_tensor("lnb_pp", [128, MD], F32))
        binz_pp = EC(nc.sbuf_tensor("binz_pp", [128, 1], F32))
        binu_pp = EC(nc.sbuf_tensor("binu_pp", [128, ME], F32))
        binv_row = EC(nc.sbuf_tensor("binv_row", [1, E], BF16))
        bout_row = EC(nc.sbuf_tensor("bout_row", [1, D], BF16))
        br_row = EC(nc.sbuf_tensor("br_row", [1, NE], BF16))
        gq_pp = EC(nc.sbuf_tensor("gq_pp", [128, 1], F32))
        bq_pp = EC(nc.sbuf_tensor("bq_pp", [128, 1], F32))
        gk_pp = EC(nc.sbuf_tensor("gk_pp", [128, 1], F32))
        bk_pp = EC(nc.sbuf_tensor("bk_pp", [128, 1], F32))
        bgs_pp = EC(nc.sbuf_tensor("bgs_pp", [128, MI], F32))
        bus_pp = EC(nc.sbuf_tensor("bus_pp", [128, MI], F32))
        bge_pp = EC(nc.sbuf_tensor("bge_pp", [128, NE, MI], F32))
        bue_pp = EC(nc.sbuf_tensor("bue_pp", [128, NE, MI], F32))
        bd33 = EC(nc.sbuf_tensor("bd33", [33, D], BF16))
        Wr_sb = EC(nc.sbuf_tensor("Wr_sb", [128, MD, NE], BF16))
        NTO = TOWN // 128
        w_sb = EC(nc.sbuf_tensor("w_sb", [128, NTO, NE], F32))
        w33 = EC(nc.sbuf_tensor("w33", [33, TOWN], BF16))
        logits_all = EC(nc.sbuf_tensor("logits_all", [128, NTO, NE], F32))
        aff_sb = EC(nc.sbuf_tensor("aff_sb", [128, NE], F32))
        a2_sb = EC(nc.sbuf_tensor("a2_sb", [128, NE], F32))
        mx_sb = EC(nc.sbuf_tensor("mx_sb", [128, 1], F32))
        msk_sb = EC(nc.sbuf_tensor("msk_sb", [128, NE], F32))
        sum_sb = EC(nc.sbuf_tensor("sum_sb", [128, 1], F32))
        epr_sb = EC(nc.sbuf_tensor("epr_sb", [128, 1], F32))

        ops = {k: [] for k in ("sync", "gpsimd", "vector", "scalar", "tensor")}

        def go(eng, emitfn, waits=(), inc=None, k=1):
            waits = tuple((s.h, int(v)) for (s, v) in waits if v > 0)

            def run(x, emitfn=emitfn, waits=waits, inc=inc, k=k):
                for (h, v) in waits:
                    x.wait_ge(h, v)
                r = emitfn(x)
                if inc is not None:
                    r.then_inc(inc.h, k)
            ops[eng].append(run)
            if inc is not None:
                inc.n += k

        def vop(emitfn, waits=()):
            go("vector", emitfn, [(s_v, s_v.n)] + list(waits), inc=s_v)
            return s_v.n

        def sop(emitfn, waits=()):
            go("scalar", emitfn, [(s_sc, s_sc.n)] + list(waits), inc=s_sc)
            return s_sc.n

        def gop(emitfn, waits=()):
            go("gpsimd", emitfn, [(s_gp, s_gp.n)] + list(waits), inc=s_gp)
            return s_gp.n

        def flush_block():
            with nc.Block() as block:
                for name in ("sync", "gpsimd", "vector", "scalar", "tensor"):
                    lst = ops[name]
                    if not lst:
                        continue

                    def body(engine, lst=lst):
                        for f in lst:
                            f(engine)
                    getattr(block, name)(body)
            for k in ops:
                ops[k] = []

        gop(lambda g: g.memset(ones_row[:], 1.0))
        gop(lambda g: g.memset(eps_pp[:], EPS))
        gop(lambda g: g.memset(invl_pp[:], INVL))
        gop(lambda g: g.memset(idn[:], 0.0))
        gop(lambda g: g.memset(epr_sb[:], 1e-9))
        gop(lambda g: g.memset(w33[32:33, :], 1.0))
        gop(lambda g: g.affine_select(out=idn[:], in_=idn[:], compare_op=ALU.not_equal,
                                      fill=1.0, base=0, pattern=[[-1, 128]], channel_multiplier=1))

        def wload(dst_ap, src_ap, sem, waits=()):
            go("gpsimd", lambda g, dd=dst_ap, ss=src_ap: g.dma_start(out=dd, in_=ss),
               waits=waits, inc=sem, k=16)

        wload(lng_pp[:], p_lng[:, :], s_wg)
        wload(lnb_pp[:], p_lnb[:, :], s_wg)
        wload(binz_pp[:], b_in[2 * E:2 * E + S, None], s_wg)
        wload(binu_pp[:], p_binu[:, :], s_wg)
        wload(binv_row[:], b_in[None, E:2 * E], s_wg)
        wload(bout_row[:], b_out[None, :], s_wg)
        wload(br_row[:], br[None, :], s_wg)
        wload(gq_pp[:], gq[:, None], s_wg)
        wload(bq_pp[:], bq[:, None], s_wg)
        wload(gk_pp[:], gk[:, None], s_wg)
        wload(bk_pp[:], bk[:, None], s_wg)
        wload(bgs_pp[:], p_bgs[:, :], s_wg)
        wload(bus_pp[:], p_bus[:, :], s_wg)
        wload(bge_pp[:], p_bge[:, :, :], s_wg)
        wload(bue_pp[:], p_bue[:, :, :], s_wg)
        wload(bd33[0:32, :], bd[:, :], s_wg)
        wload(bd33[32:33, :], bd_s[None, :], s_wg)
        wload(Wr_sb[:], Wr.rearrange("(m p) e -> p m e", p=128), s_wg)
        GP0 = s_gp.n
        flush_block()

        with ExitStack() as ph12:
            E12 = ph12.enter_context
            kT = E12(nc.sbuf_tensor("kT", [128, L], BF16))
            qT = E12(nc.sbuf_tensor("qT", [128, TOWN], BF16))
            ph1 = ExitStack()
            E11 = ph1.enter_context
            Wz_sb = E11(nc.sbuf_tensor("Wz_sb", [128, MD, S], BF16))
            Wv_sb = E11(nc.sbuf_tensor("Wv_sb", [128, MD, E], BF16))
            x_sb = [E11(nc.sbuf_tensor(f"x_sb{i}", [128, 4, D], F32)) for i in range(2)]
            xn2_bf = [E11(nc.sbuf_tensor(f"xn2_bf{i}", [128, 4, D], BF16)) for i in range(2)]
            xn2T = [E11(nc.sbuf_tensor(f"xn2T{i}", [128, MD, 512], BF16)) for i in range(2)]
            xn2Ta = [E11(nc.sbuf_tensor(f"xn2Ta{i}", [128, MD, 512], BF16)) for i in range(2)]
            tmpz = E11(nc.sbuf_tensor("tmpz", [128, 512], F32))
            v_sb = [E11(nc.sbuf_tensor(f"v_sb{i}", [128, E], BF16)) for i in range(2)]
            xnb_tmp = [E11(nc.sbuf_tensor(f"xnb_tmp{i}", [128, D], BF16)) for i in range(2)]
            xnT_tmp = [E11(nc.sbuf_tensor(f"xnT_tmp{i}", [128, MD, 128], BF16)) for i in range(2)]
            ps_z = E11(nc.psum_tensor("ps_z", [128, 512], F32))
            ps_v = [E11(nc.psum_tensor(f"ps_v{i}", [128, 512], F32)) for i in range(2)]
            ps_l = E11(nc.psum_tensor("ps_l", [128, 512], F32))
            ps_wt = E11(nc.psum_tensor("ps_wt", [128, 128], F32))

            wload(Wz_sb[:], W_in[:, 2 * E:2 * E + S].rearrange("(m p) e -> p m e", p=128), s_wg)
            wload(Wv_sb[:], W_in[:, E:2 * E].rearrange("(m p) e -> p m e", p=128), s_wg)
            WG_ALL = s_wg.n

            last_bn = {}
            for st in range(nst):
                par = st % 2
                xw = [] if st < 2 else [(s_v, last_bn[st - 2])]
                go("sync", lambda y, st=st, par=par:
                   y.dma_start(out=x_sb[par][:],
                               in_=xb[st * 512:(st + 1) * 512, :].rearrange("(s p) d -> p s d", p=128)),
                   waits=xw, inc=s_x[par], k=16)
                nx = s_x[par].n
                for s in range(4):
                    t = st * 4 + s
                    vop(lambda v, par=par, s=s: v.bn_stats(out=bnsc[:, 0:6], in_=x_sb[par][:, s, 0:512]),
                        waits=[(s_x[par], nx)])
                    vop(lambda v, par=par, s=s: v.bn_stats(out=bnsc[:, 6:12], in_=x_sb[par][:, s, 512:1024]))
                    last_bn[st] = vop(lambda v, t=t: v.bn_aggr(out=mv_all[:, t, :], in_=bnsc[:]))
            _sd = sop(lambda sc: sc.activation(out=sd_all[:], in_=mv_all[:, :, 1], func=AF.Sqrt,
                                               bias=eps_pp[:]),
                      waits=[(s_v, last_bn[nst - 1]), (s_gp, GP0)])
            _r = vop(lambda v: v.reciprocal(out=r_all[:], in_=sd_all[:]), waits=[(s_sc, _sd)])
            vop(lambda v: v.tensor_tensor(out=v2_all[:], in0=mv_all[:, :, 1], in1=r_all[:], op=ALU.mult))
            vv2 = vop(lambda v: v.tensor_tensor(out=v2_all[:], in0=v2_all[:], in1=r_all[:], op=ALU.mult))
            _s2 = sop(lambda sc: sc.activation(out=v2_all[:], in_=v2_all[:], func=AF.Sqrt, bias=eps_pp[:]),
                      waits=[(s_v, vv2)])
            vop(lambda v: v.reciprocal(out=rr_all[:], in_=v2_all[:]), waits=[(s_sc, _s2)])
            vop(lambda v: v.tensor_tensor(out=rr_all[:], in0=rr_all[:], in1=r_all[:], op=ALU.mult))
            vop(lambda v: v.tensor_copy(out=mu_all[:], in_=mv_all[:, :, 0]))
            vop(lambda v: v.tensor_scalar(out=r1_all[:], in0=r_all[:], scalar1=1.0, scalar2=None, op0=ALU.add))
            STATS_DONE = vop(lambda v: v.tensor_tensor(out=mur_all[:], in0=mu_all[:], in1=r_all[:], op=ALU.mult))
            flush_block()

            lastcons = {}
            tz_cnt = {}
            mm_z = {}
            mm_v = {}
            sc_v = {}
            spill_done = {}
            aff_done = {}
            x2w_cnt = {}
            xld_cnt = {}
            lev_cnt = {}
            rmm_cnt = {}

            def emit_xload(st, waits):
                par = st % 2
                go("sync", lambda y, st=st, par=par:
                   y.dma_start(out=x_sb[par][:],
                               in_=xb[st * 512:(st + 1) * 512, :].rearrange("(s p) d -> p s d", p=128)),
                   waits=waits, inc=s_x[par], k=16)
                xld_cnt[st] = s_x[par].n

            emit_xload(0, [(s_v, STATS_DONE)])
            if nst > 1:
                emit_xload(1, [(s_v, STATS_DONE)])
            for st in range(nst):
                par = st % 2
                own = st < nst_own
                nx = xld_cnt[st]
                trw = [] if st < 2 else [(s_tr[par], 16 * 4 * (st // 2))]
                xn2cnt = {}
                for s in range(4):
                    t = st * 4 + s
                    xn2cnt[s] = vop(lambda v, par=par, s=s, t=t:
                                    v.tensor_scalar(out=xn2_bf[par][:, s, :], in0=x_sb[par][:, s, :],
                                                    scalar1=mu_all[:, t:t + 1], scalar2=rr_all[:, t:t + 1],
                                                    op0=ALU.subtract, op1=ALU.mult),
                                    waits=[(s_x[par], nx)] + trw)
                lastcons[st] = xn2cnt[3]
                afw = [] if st < 2 else [(s_v, aff_done[st - 2])]
                for s in range(4):
                    go("sync", lambda y, par=par, s=s:
                       y.dma_start(out=xn2T[par][:, :, s * 128:(s + 1) * 128],
                                   in_=xn2_bf[par][:, s, :], transpose=True),
                       waits=[(s_v, xn2cnt[s])] + afw, inc=s_tr[par], k=16)
                ntr = s_tr[par].n
                pw = [] if st < 2 else [(s_mm, mm_v[st - 2])]
                if st == 0:
                    pw.append((s_wg, WG_ALL))
                if st >= 2 and (st - 2) < nst_own:
                    pw.append((s_x2w, x2w_cnt[st - 2]))
                for m in range(MD):
                    aff_done[st] = vop(lambda v, par=par, m=m:
                                       v.tensor_scalar(out=xn2Ta[par][:, m, :], in0=xn2T[par][:, m, :],
                                                       scalar1=lng_pp[:, m:m + 1], scalar2=lnb_pp[:, m:m + 1],
                                                       op0=ALU.mult, op1=ALU.add),
                                       waits=[(s_tr[par], ntr)] + pw)
                if own:
                    go("gpsimd", lambda g, par=par, st=st:
                       g.dma_start(out=x2sp[st, :, :, :], in_=xn2Ta[par][:]),
                       waits=[(s_v, aff_done[st])], inc=s_x2w, k=16)
                    x2w_cnt[st] = s_x2w.n
                zw = [(s_v, aff_done[st])]
                if st == 0:
                    zw.append((s_wg, WG_ALL))
                if st >= 1:
                    zw.append((s_sc, tz_cnt[st - 1]))

                def emit_z(tensor, par=par, st=st):
                    for m in range(MD):
                        r = tensor.matmul(ps_z[:], Wz_sb[:, m, :], xn2Ta[par][:, m, :],
                                          start=(m == 0), stop=(m == MD - 1))
                    return r
                go("tensor", emit_z, waits=zw, inc=s_mm)
                mm_z[st] = s_mm.n
                tz_cnt[st] = tz = sop(
                    lambda sc: sc.activation(out=tmpz[:], in_=ps_z[:], func=AF.Silu, bias=binz_pp[:]),
                    waits=[(s_mm, mm_z[st])] + ([] if st < 1 else [(s_v, lastkq[st - 1])]))
                lastkq = locals().get("lastkq", {})
                lastkq[st] = vop(lambda v, st=st:
                                 v.tensor_scalar(out=kT[:, st * 512:(st + 1) * 512], in0=tmpz[:],
                                                 scalar1=gk_pp[:], scalar2=bk_pp[:],
                                                 op0=ALU.mult, op1=ALU.add),
                                 waits=[(s_sc, tz)])
                if own:
                    lastkq[st] = vop(lambda v, st=st:
                                     v.tensor_scalar(out=qT[:, st * 512:(st + 1) * 512], in0=tmpz[:],
                                                     scalar1=gq_pp[:], scalar2=bq_pp[:],
                                                     op0=ALU.mult, op1=ALU.add))
                if own:
                    for s in range(4):
                        t = st * 4 + s
                        sp = s % 2
                        xw2 = [(s_x[par], nx)]
                        if t >= 2:
                            xw2.append((s_trr[sp], 16 * (t // 2)))
                        xts = vop(lambda v, par=par, s=s, t=t, sp=sp:
                                  v.tensor_scalar(out=xnb_tmp[sp][:], in0=x_sb[par][:, s, :],
                                                  scalar1=mu_all[:, t:t + 1], scalar2=r_all[:, t:t + 1],
                                                  op0=ALU.subtract, op1=ALU.mult),
                                  waits=xw2)
                        trw2 = [(s_v, xts)]
                        if t >= 2:
                            trw2.append((s_mm, rmm_cnt[t - 2]))
                        go("sync", lambda y, sp=sp:
                           y.dma_start(out=xnT_tmp[sp][:], in_=xnb_tmp[sp][:], transpose=True),
                           waits=trw2, inc=s_trr[sp], k=16)
                        ntrr = s_trr[sp].n
                        rw_ = [(s_trr[sp], ntrr)]
                        if t >= 1:
                            rw_.append((s_v, lev_cnt[t - 1]))

                        def emit_rt(tensor, sp=sp):
                            for m in range(MD):
                                tensor.matmul(ps_l[:, 0:NE], xnT_tmp[sp][:, m, :], Wr_sb[:, m, :],
                                              start=(m == 0), stop=False)
                            return tensor.matmul(ps_l[:, 0:NE], ones_row[:], br_row[:],
                                                 start=False, stop=True)
                        go("tensor", emit_rt, waits=rw_, inc=s_mm)
                        rmm_cnt[t] = s_mm.n
                        lev_cnt[t] = vop(lambda v, t=t: v.tensor_copy(out=logits_all[:, t, :],
                                                                      in_=ps_l[:, 0:NE]),
                                         waits=[(s_mm, rmm_cnt[t])])
                if own:
                    lastcons[st] = max(lastcons[st], s_v.n)
                if st + 2 < nst:
                    emit_xload(st + 2, [(s_v, lastcons[st])])
                for s in range(4):
                    t = st * 4 + s
                    spar = t % 2
                    for ec in range(4):
                        vpar = (s * 4 + ec) % 2
                        slot = st * 16 + s * 4 + ec
                        mw = []
                        if slot >= 2:
                            mw.append((s_sc, sc_v[slot - 2]))

                        def emit_v(tensor, par=par, s=s, ec=ec, vpar=vpar):
                            for m in range(MD):
                                tensor.matmul(ps_v[vpar][:], xn2Ta[par][:, m, s * 128:(s + 1) * 128],
                                              Wv_sb[:, m, ec * 512:(ec + 1) * 512],
                                              start=(m == 0), stop=False)
                            return tensor.matmul(ps_v[vpar][:], ones_row[:],
                                                 binv_row[:, ec * 512:(ec + 1) * 512],
                                                 start=False, stop=True)
                        go("tensor", emit_v, waits=mw, inc=s_mm)
                        mm_v[st] = s_mm.n
                        sw = [(s_mm, mm_v[st])]
                        if t >= 2:
                            sw.append((s_vw[spar], 16 * (t // 2)))
                        sc_v[slot] = sop(lambda sc, vpar=vpar, spar=spar, ec=ec:
                                         sc.activation(out=v_sb[spar][:, ec * 512:(ec + 1) * 512],
                                                       in_=ps_v[vpar][:], func=AF.Silu),
                                         waits=sw)
                    go("gpsimd", lambda g, spar=spar, t=t:
                       g.dma_start(out=vsp[t * 128:(t + 1) * 128, :], in_=v_sb[spar][:]),
                       waits=[(s_sc, sc_v[st * 16 + s * 4 + 3])], inc=s_vw[spar], k=16)
                    spill_done[t] = (spar, s_vw[spar].n)
            NTO = TOWN // 128
            sig_all = sop(lambda sc: sc.activation(out=logits_all[:, :, :], in_=logits_all[:, :, :],
                                                   func=AF.Sigmoid),
                          waits=[(s_v, lev_cnt[NTO - 1])])
            wtk = {}
            wev = {}
            for t in range(NTO):
                vop(lambda v, t=t: v.tensor_copy(out=aff_sb[:], in_=logits_all[:, t, :]),
                    waits=[(s_sc, sig_all), (s_gp, GP0)])
                vop(lambda v: v.tensor_copy(out=a2_sb[:], in_=aff_sb[:]))
                for it in range(TOPK - 1):
                    vop(lambda v: v.tensor_reduce(out=mx_sb[:], in_=a2_sb[:], axis=AX.X, op=ALU.max))
                    vop(lambda v: v.tensor_scalar(out=msk_sb[:], in0=a2_sb[:], scalar1=mx_sb[:],
                                                  scalar2=None, op0=ALU.is_ge))
                    vop(lambda v: v.scalar_tensor_tensor(out=a2_sb[:], in0=msk_sb[:], scalar=-1e30,
                                                         in1=a2_sb[:], op0=ALU.mult, op1=ALU.add))
                vop(lambda v: v.tensor_reduce(out=mx_sb[:], in_=a2_sb[:], axis=AX.X, op=ALU.max))
                vop(lambda v: v.tensor_scalar(out=msk_sb[:], in0=aff_sb[:], scalar1=mx_sb[:],
                                              scalar2=None, op0=ALU.is_ge))
                vop(lambda v, t=t: v.tensor_tensor(out=w_sb[:, t, :], in0=aff_sb[:],
                                                   in1=msk_sb[:], op=ALU.mult))
                vop(lambda v, t=t: v.tensor_reduce(out=sum_sb[:], in_=w_sb[:, t, :], axis=AX.X, op=ALU.add))
                vop(lambda v: v.tensor_tensor(out=sum_sb[:], in0=sum_sb[:], in1=epr_sb[:], op=ALU.add))
                vop(lambda v: v.reciprocal(out=sum_sb[:], in_=sum_sb[:]))
                wtk[t] = vop(lambda v, t=t: v.tensor_scalar(out=w_sb[:, t, :], in0=w_sb[:, t, :],
                                                            scalar1=sum_sb[:], scalar2=None, op0=ALU.mult))
                tw_ = [(s_v, wtk[t]), (s_gp, GP0)]
                if t >= 1:
                    tw_.append((s_sc, wev[t - 1]))
                go("tensor", lambda tensor, t=t: tensor.transpose(ps_wt[0:NE, 0:128], w_sb[:, t, :], idn[:]),
                   waits=tw_, inc=s_mm)
                ntw = s_mm.n
                wev[t] = sop(lambda sc, t=t: sc.copy(out=w33[0:NE, t * 128:(t + 1) * 128],
                                                     in_=ps_wt[0:NE, 0:128]),
                             waits=[(s_mm, ntw)])
            if taps:
                go("sync", lambda y: y.dma_start(out=t_w[:, :, :], in_=w_sb[:]),
                   waits=[(s_v, s_v.n), (s_sc, s_sc.n)], inc=s_tap, k=16)
            go("sync", lambda y: y.nop(),
               waits=[(s_vw[0], s_vw[0].n), (s_vw[1], s_vw[1].n), (s_x2w, s_x2w.n)])
            if taps:
                go("sync", lambda y: y.dma_start(out=t_kT[:, :], in_=kT[:]),
                   waits=[(s_v, s_v.n)], inc=s_tap, k=16)
                go("sync", lambda y: y.dma_start(out=t_qT[:, :], in_=qT[:]), inc=s_tap, k=16)
                go("sync", lambda y: y.dma_start(out=t_v[:, :], in_=vsp[:, :]), inc=s_tap, k=16)
                go("sync", lambda y: y.dma_start(out=t_x2[:, :, :, :], in_=x2sp[:, :, :, :]), inc=s_tap, k=16)
                go("sync", lambda y: y.nop(), waits=[(s_tap, s_tap.n)])
            flush_block()
            ph1.close()

            with ExitStack() as ph2:
                E2 = ph2.enter_context
                Wu_sb = E2(nc.sbuf_tensor("Wu_sb", [128, MD, E], BF16))
                Wout_sb = E2(nc.sbuf_tensor("Wout_sb", [128, ME, D], BF16))
                x2_sb = [E2(nc.sbuf_tensor(f"x2_sb{i}", [128, MD, 512], BF16)) for i in range(2)]
                v_buf = [E2(nc.sbuf_tensor(f"v_buf{i}", [128, 4, 512], BF16)) for i in range(3)]
                srelu = [E2(nc.sbuf_tensor(f"srelu{i}", [128, 2048], BF16)) for i in range(2)]
                ssq = [E2(nc.sbuf_tensor(f"ssq{i}", [128, 2048], BF16)) for i in range(2)]
                uT_eq = E2(nc.sbuf_tensor("uT_eq", [128, 4, 512], BF16))
                uavT = E2(nc.sbuf_tensor("uavT", [128, ME, 512], BF16))
                y_sb = [E2(nc.sbuf_tensor(f"y_sb{i}", [128, D], F32)) for i in range(2)]
                ps_s = E2(nc.psum_tensor("ps_s", [128, 2048], F32))
                ps_av = E2(nc.psum_tensor("ps_av", [128, 2048], F32))

                wload(Wu_sb[:], W_in[:, 0:E].rearrange("(m p) e -> p m e", p=128), s_wg)
                wload(Wout_sb[:], W_out.rearrange("(m p) d -> p m d", p=128), s_wg)
                WG_PH2 = s_wg.n

                relu_cnt = {}
                sq_cnt = {}
                mm1_cnt = {}
                mm2_cnt = {}
                uav_cnt = {}
                usilu_cnt = {}
                wout_cnt = {}
                ycpy_cnt = {}
                vld_cnt = {}
                yw_i = 0
                for qb in range(nqb):
                    qpar = qb % 2
                    xw = [(s_x2w, s_x2w.n)]
                    if qb >= 2:
                        xw.append((s_mm, usilu_cnt[qb - 2][1]))
                    go("sync", lambda y, qb=qb, qpar=qpar:
                       y.dma_start(out=x2_sb[qpar][:], in_=x2sp[qb, :, :, :]),
                       waits=xw, inc=s_x2r[qpar], k=16)
                    nx2 = s_x2r[qpar].n
                    git = 0
                    for eq in range(4):
                        for es in range(4):
                            eu = eq * 4 + es
                            upar = es % 2
                            uw = [(s_x2r[qpar], nx2)]
                            if qb == 0 and eq == 0 and es == 0:
                                uw.append((s_wg, WG_PH2))
                            if eq > 0 or qb > 0:
                                uw.append((s_sc, s_sc.n))
                            if es >= 2:
                                uw.append((s_sc, usil[es - 2]))

                            def emit_u(tensor, qpar=qpar, es=es, upar=upar, eu=eu):
                                for m in range(MD):
                                    r = tensor.matmul(ps_s[:, upar * 512:(upar + 1) * 512],
                                                      Wu_sb[:, m, eu * 128:(eu + 1) * 128],
                                                      x2_sb[qpar][:, m, :],
                                                      start=(m == 0), stop=(m == MD - 1))
                                return r
                            go("tensor", emit_u, waits=uw, inc=s_mm)
                            numm = s_mm.n
                            usil = locals().get("usil", {})
                            suw = [(s_mm, numm)]
                            if eq > 0:
                                suw.append((s_v, uav_cnt[(qb, eq - 1)]))
                            elif qb > 0:
                                suw.append((s_v, uav_cnt[(qb - 1, 3)]))
                            usil[es] = sop(lambda sc, upar=upar, es=es, eu=eu:
                                           sc.activation(out=uT_eq[:, es, :],
                                                         in_=ps_s[:, upar * 512:(upar + 1) * 512],
                                                         func=AF.Silu, bias=binu_pp[:, eu:eu + 1]),
                                           waits=suw)
                        usilu_cnt[qb] = (usil[3], s_mm.n)
                        def emit_vload(g):
                            vp = g % 3
                            vw = []
                            if g >= 3:
                                vw.append((s_mm, mm2_cnt[(qb, eq, g - 3)]))
                            elif (qb, eq) != (0, 0):
                                vw.append((s_mm, prev_mm2_last))
                            go("sync", lambda y, g=g, eq=eq, vp=vp:
                               y.dma_start(out=v_buf[vp][:],
                                           in_=vsp[g * 512:(g + 1) * 512, eq * 512:(eq + 1) * 512]
                                           .rearrange("(k p) e -> p k e", p=128)),
                               waits=vw, inc=s_vr[vp], k=16)
                            vld_cnt[(qb, eq, g)] = (vp, s_vr[vp].n)

                        def emit_mm2_for(g):
                            vp, vn = vld_cnt[(qb, eq, g)]
                            m2w = [(s_v, sq_cnt[(qb, eq, g)]), (s_vr[vp], vn)]
                            if g == 0:
                                if eq > 0:
                                    m2w.append((s_v, uav_cnt[(qb, eq - 1)]))
                                elif qb > 0:
                                    m2w.append((s_v, ycpy_cnt[qb - 1]))

                            def emit_m2(tensor, g=g, vp=vp, eq=eq):
                                for j in range(4):
                                    for es in range(4):
                                        r = tensor.matmul(ps_av[:, es * 512:(es + 1) * 512],
                                                          v_buf[vp][:, j, es * 128:(es + 1) * 128],
                                                          ssq[(g % 2)][:, j * 512:(j + 1) * 512],
                                                          start=(g == 0 and j == 0),
                                                          stop=(g == G - 1 and j == 3),
                                                          skip_group_check=True)
                                return r
                            go("tensor", emit_m2, waits=m2w, inc=s_mm)
                            mm2_cnt[(qb, eq, g)] = s_mm.n

                        emit_vload(0)
                        for g in range(G):
                            gpar = g % 2
                            m1w = []
                            if g == 0:
                                m1w.append((s_sc, usil[3]))
                            else:
                                m1w.append((s_sc, relu_cnt[(qb, eq, g - 1)]))

                            def emit_m1(tensor, qb=qb, g=g):
                                for j in range(4):
                                    kt = g * 4 + j
                                    r = tensor.matmul(ps_s[:, j * 512:(j + 1) * 512],
                                                      kT[:, kt * 128:(kt + 1) * 128],
                                                      qT[:, qb * 512:(qb + 1) * 512],
                                                      start=True, stop=True)
                                return r
                            go("tensor", emit_m1, waits=m1w, inc=s_mm)
                            mm1_cnt[(qb, eq, g)] = s_mm.n
                            rw = [(s_mm, mm1_cnt[(qb, eq, g)])]
                            if g >= 2:
                                rw.append((s_v, sq_cnt[(qb, eq, g - 2)]))
                            relu_cnt[(qb, eq, g)] = sop(lambda sc, gpar=gpar:
                                                        sc.activation(out=srelu[gpar][:], in_=ps_s[:],
                                                                      func=AF.Relu, scale=invl_pp[:]),
                                                        waits=rw)
                            qw = [(s_sc, relu_cnt[(qb, eq, g)])]
                            if g >= 2:
                                qw.append((s_mm, mm2_cnt[(qb, eq, g - 2)]))
                            sq_cnt[(qb, eq, g)] = vop(lambda v, gpar=gpar:
                                                      v.tensor_tensor(out=ssq[gpar][:], in0=srelu[gpar][:],
                                                                      in1=srelu[gpar][:], op=ALU.mult),
                                                      waits=qw)
                            if g >= 1:
                                emit_mm2_for(g - 1)
                            if g + 1 < G:
                                emit_vload(g + 1)
                        emit_mm2_for(G - 1)
                        prev_mm2_last = mm2_cnt[(qb, eq, G - 1)]
                        for es in range(4):
                            uav_cnt[(qb, eq)] = vop(lambda v, es=es, eq=eq:
                                                    v.tensor_tensor(out=uavT[:, eq * 4 + es, :],
                                                                    in0=ps_av[:, es * 512:(es + 1) * 512],
                                                                    in1=uT_eq[:, es, :], op=ALU.mult),
                                                    waits=[(s_mm, prev_mm2_last)] +
                                                          ([] if qb == 0 else [(s_mm, wout_cnt.get(qb - 1, 0))]))
                    for qt in range(4):
                        ypar = qt % 2
                        yw_ = [(s_v, uav_cnt[(qb, 3)])]
                        if qt >= 1:
                            yw_.append((s_v, ycpy[qt - 1]))

                        def emit_y2(tensor, qb=qb, qt=qt):
                            r = None
                            for hc in range(2):
                                for m in range(ME):
                                    r = tensor.matmul(ps_av[:, hc * 512:(hc + 1) * 512],
                                                      uavT[:, m, qt * 128:(qt + 1) * 128],
                                                      Wout_sb[:, m, hc * 512:(hc + 1) * 512],
                                                      start=(m == 0), stop=False,
                                                      skip_group_check=True)
                                r = tensor.matmul(ps_av[:, hc * 512:(hc + 1) * 512], ones_row[:],
                                                  bout_row[:, hc * 512:(hc + 1) * 512],
                                                  start=False, stop=True, skip_group_check=True)
                            return r
                        go("tensor", emit_y2, waits=yw_, inc=s_mm)
                        wout_cnt[qb] = s_mm.n
                        ycpy = locals().get("ycpy", {})
                        cw = [(s_mm, wout_cnt[qb])]
                        if yw_i >= 2:
                            cw.append((s_yw[ypar], 16 * (yw_i // 2)))
                        ycpy[qt] = vop(lambda v, ypar=ypar:
                                       v.tensor_copy(out=y_sb[ypar][:], in_=ps_av[:, 0:1024]),
                                       waits=cw)
                        ycpy_cnt[qb] = ycpy[qt]
                        go("sync", lambda y, qb=qb, qt=qt, ypar=ypar:
                           y.dma_start(out=ysp[qb * 512 + qt * 128: qb * 512 + (qt + 1) * 128, :],
                                       in_=y_sb[ypar][:]),
                           waits=[(s_v, ycpy[qt])], inc=s_yw[ypar], k=16)
                        yw_i += 1
                go("sync", lambda y: y.nop(),
                   waits=[(s_yw[0], s_yw[0].n), (s_yw[1], s_yw[1].n)])
                if taps:
                    go("sync", lambda y: y.dma_start(out=t_y[:, :], in_=ysp[:, :]), inc=s_tap, k=16)
                flush_block()

        with ExitStack() as ph3:
            E3 = ph3.enter_context
            xnT_blk = E3(nc.sbuf_tensor("xnT_blk", [128, MD, TB], BF16))
            acc_sb = E3(nc.sbuf_tensor("acc_sb", [128, ntile_b, D], F32))
            x3_sb = [E3(nc.sbuf_tensor(f"x3_sb{i}", [128, 2, D], F32)) for i in range(2)]
            xn_bf = [E3(nc.sbuf_tensor(f"xn_bf{i}", [128, 2, D], BF16)) for i in range(2)]
            Wgs_sb = E3(nc.sbuf_tensor("Wgs_sb", [128, MD, II], BF16))
            Wus_sb = E3(nc.sbuf_tensor("Wus_sb", [128, MD, II], BF16))
            Wds_sb = E3(nc.sbuf_tensor("Wds_sb", [128, MI, D], BF16))
            Wge_sb = [E3(nc.sbuf_tensor(f"Wge_sb{i}", [128, MD, II], BF16)) for i in range(2)]
            Wue_sb = [E3(nc.sbuf_tensor(f"Wue_sb{i}", [128, MD, II], BF16)) for i in range(2)]
            Wde_sb = [E3(nc.sbuf_tensor(f"Wde_sb{i}", [128, MI, D], BF16)) for i in range(2)]
            g_sb = [E3(nc.sbuf_tensor(f"g_sb{i}", [128, II], BF16)) for i in range(2)]
            h_sb = [E3(nc.sbuf_tensor(f"h_sb{i}", [128, MI, II], BF16)) for i in range(2)]
            y3_sb = [E3(nc.sbuf_tensor(f"y3_sb{i}", [128, D], F32)) for i in range(2)]
            o_sb = [E3(nc.sbuf_tensor(f"o_sb{i}", [128, D], F32)) for i in range(2)]
            ps_a = E3(nc.psum_tensor("ps_a", [128, 512], F32))
            ps_b = E3(nc.psum_tensor("ps_b", [128, 512], F32))
            ps_d = [E3(nc.psum_tensor(f"ps_d{i}", [128, D], F32)) for i in range(2)]

            wload(Wgs_sb[:], Wg_s.rearrange("(m p) i -> p m i", p=128), s_wg)
            wload(Wus_sb[:], Wu_s.rearrange("(m p) i -> p m i", p=128), s_wg)
            wload(Wds_sb[:], Wd_s.rearrange("(m p) d -> p m d", p=128), s_wg)
            WG_PH3 = s_wg.n

            def eload(e, buf, waits=()):
                sem = s_we[buf]
                wload(Wge_sb[buf][:], Wg[e].rearrange("(m p) i -> p m i", p=128), sem, waits=waits)
                wload(Wue_sb[buf][:], Wu[e].rearrange("(m p) i -> p m i", p=128), sem)
                wload(Wde_sb[buf][:], Wd[e].rearrange("(m p) d -> p m d", p=128), sem)
                return (sem, sem.n)

            def eload2(e, buf, waits):
                return eload(e, buf, waits=waits)

            eweight_ready = {}
            sweep_idx = 0
            out_i = 0
            for tb in range(ntb):
                lastxn = {}
                ngrp = TB // 256
                for grp in range(ngrp):
                    gpr = grp % 2
                    t0 = tb * ntile_b + grp * 2
                    xw = [] if grp < 2 else [(s_v, lastxn[grp - 2])]
                    go("sync", lambda y, tb=tb, grp=grp, gpr=gpr:
                       y.dma_start(out=x3_sb[gpr][:],
                                   in_=xb[tb * TB + grp * 256: tb * TB + (grp + 1) * 256, :]
                                   .rearrange("(s p) d -> p s d", p=128)),
                       waits=xw, inc=s_x3[gpr], k=16)
                    nx3 = s_x3[gpr].n
                    xncnt = {}
                    trw = [] if grp < 2 else [(s_tr3[gpr], 16 * 2 * (grp // 2))]
                    for s in range(2):
                        t = t0 + s
                        xncnt[s] = vop(lambda v, gpr=gpr, s=s, t=t:
                                       v.tensor_scalar(out=xn_bf[gpr][:, s, :], in0=x3_sb[gpr][:, s, :],
                                                       scalar1=mu_all[:, t:t + 1], scalar2=r_all[:, t:t + 1],
                                                       op0=ALU.subtract, op1=ALU.mult),
                                       waits=[(s_x3[gpr], nx3)] + trw)
                    lastxn[grp] = xncnt[1]
                    for s in range(2):
                        go("sync", lambda y, gpr=gpr, grp=grp, s=s:
                           y.dma_start(out=xnT_blk[:, :, grp * 256 + s * 128: grp * 256 + (s + 1) * 128],
                                       in_=xn_bf[gpr][:, s, :], transpose=True),
                           waits=[(s_v, xncnt[s])], inc=s_tr3[gpr], k=16)
                tr3_all = (s_tr3[0].n, s_tr3[1].n)
                eweight_ready[(tb, 0)] = eload(0, 0)
                eweight_ready[(tb, 1)] = eload(1, 1)

                hstt_cnt = {}
                gsil_cnt = {}
                dacc_cnt = {}
                slot = 0
                dslot = 0
                for ei in range(-1, NE):
                    if ei >= 0:
                        sweep_idx += 1
                    ebuf = ei % 2 if ei >= 0 else 0
                    Wg_cur = Wgs_sb if ei < 0 else Wge_sb[ebuf]
                    Wu_cur = Wus_sb if ei < 0 else Wue_sb[ebuf]
                    Wd_cur = Wds_sb if ei < 0 else Wde_sb[ebuf]
                    for c in range(nch):
                        hpar = (ei * nch + c + NE * nch) % 2
                        for ic in range(MI):
                            gw = []
                            if ei >= 0:
                                gw.append(eweight_ready[(tb, ei)])
                            else:
                                gw.append((s_wg, WG_PH3))
                                gw.append((s_tr3[0], tr3_all[0]))
                                gw.append((s_tr3[1], tr3_all[1]))
                            if slot >= 1:
                                gw.append((s_sc, gsil_cnt[slot - 1]))

                            def emit_g(tensor, Wg_cur=Wg_cur, c=c, ic=ic):
                                for m in range(MD):
                                    r = tensor.matmul(ps_a[:], Wg_cur[:, m, ic * 128:(ic + 1) * 128],
                                                      xnT_blk[:, m, c * 512:(c + 1) * 512],
                                                      start=(m == 0), stop=(m == MD - 1))
                                return r
                            go("tensor", emit_g, waits=gw, inc=s_mm)
                            ngm = s_mm.n
                            bias_ap = (bgs_pp[:, ic:ic + 1] if ei < 0 else bge_pp[:, ei, ic:ic + 1])
                            sgw = [(s_mm, ngm)]
                            if slot >= 2:
                                sgw.append((s_v, hstt_cnt[slot - 2]))
                            gsil_cnt[slot] = sop(lambda sc, bias_ap=bias_ap, gp=slot % 2:
                                                 sc.activation(out=g_sb[gp][:], in_=ps_a[:],
                                                               func=AF.Silu, bias=bias_ap),
                                                 waits=sgw)
                            uw = []
                            if ei >= 0:
                                uw.append(eweight_ready[(tb, ei)])
                            if slot >= 1:
                                uw.append((s_v, hstt_cnt[slot - 1]))

                            def emit_up(tensor, Wu_cur=Wu_cur, c=c, ic=ic):
                                for m in range(MD):
                                    r = tensor.matmul(ps_b[:], Wu_cur[:, m, ic * 128:(ic + 1) * 128],
                                                      xnT_blk[:, m, c * 512:(c + 1) * 512],
                                                      start=(m == 0), stop=(m == MD - 1))
                                return r
                            go("tensor", emit_up, waits=uw, inc=s_mm)
                            num = s_mm.n
                            bu_ap = (bus_pp[:, ic:ic + 1] if ei < 0 else bue_pp[:, ei, ic:ic + 1])
                            hw = [(s_mm, num), (s_sc, gsil_cnt[slot])]
                            key = (hpar, "lastdown")
                            if key in hstt_cnt:
                                hw.append((s_mm, hstt_cnt[key]))
                            hstt_cnt[slot] = vop(lambda v, bu_ap=bu_ap, gp=slot % 2, hpar=hpar, ic=ic:
                                                 v.scalar_tensor_tensor(out=h_sb[hpar][:, ic, :],
                                                                        in0=ps_b[:], scalar=bu_ap,
                                                                        in1=g_sb[gp][:],
                                                                        op0=ALU.add, op1=ALU.mult),
                                                 waits=hw)
                            slot += 1
                        h_ready = hstt_cnt[slot - 1]
                        for tt4 in range(4):
                            tt = c * 4 + tt4
                            dpar = dslot % 2
                            dw = [(s_v, h_ready)]
                            if dslot >= 2:
                                dw.append((s_v, dacc_cnt[dslot - 2]))

                            def emit_d(tensor, Wd_cur=Wd_cur, hpar=hpar, tt=tt, tt4=tt4, c=c, ei=ei, tb=tb, dpar=dpar):
                                r = None
                                for hc in range(2):
                                    if ei < 0:
                                        tg = tb * ntile_b + tt
                                        tensor.matmul(ps_d[dpar][:, hc * 512:(hc + 1) * 512],
                                                      w33[:, tg * 128:(tg + 1) * 128],
                                                      bd33[:, hc * 512:(hc + 1) * 512],
                                                      start=True, stop=False, skip_group_check=True)
                                    for ic in range(MI):
                                        r = tensor.matmul(ps_d[dpar][:, hc * 512:(hc + 1) * 512],
                                                          h_sb[hpar][:, ic, tt4 * 128:(tt4 + 1) * 128],
                                                          Wd_cur[:, ic, hc * 512:(hc + 1) * 512],
                                                          start=(ei >= 0 and ic == 0), stop=(ic == MI - 1),
                                                          skip_group_check=True)
                                return r
                            go("tensor", emit_d, waits=dw, inc=s_mm)
                            ndm = s_mm.n
                            hstt_cnt[(hpar, "lastdown")] = ndm
                            if ei < 0:
                                dacc_cnt[dslot] = vop(lambda v, tt=tt, dpar=dpar:
                                                      v.tensor_copy(out=acc_sb[:, tt, :], in_=ps_d[dpar][:]),
                                                      waits=[(s_mm, ndm)])
                            else:
                                dacc_cnt[dslot] = vop(lambda v, tt=tt, dpar=dpar, ei=ei, tg=tb * ntile_b + tt:
                                                      v.scalar_tensor_tensor(out=acc_sb[:, tt, :],
                                                                             in0=ps_d[dpar][:],
                                                                             scalar=w_sb[:, tg, ei:ei + 1],
                                                                             in1=acc_sb[:, tt, :],
                                                                             op0=ALU.mult, op1=ALU.add),
                                                      waits=[(s_mm, ndm)])
                            acc_last = locals().get("acc_last", {})
                            acc_last[tt] = dacc_cnt[dslot]
                            dslot += 1
                    if ei >= 0 and ei + 2 < NE:
                        eweight_ready[(tb, ei + 2)] = eload2(ei + 2, (ei + 2) % 2,
                                                             waits=[(s_mm, s_mm.n)])

                for tt in range(ntile_b):
                    tglob = tb * ntile_b + tt
                    cpar = tt % 2
                    yw_ = [(s_v, acc_last[tt])] if tt < 2 else [(s_v, comb_done[tt - 2])]
                    go("sync", lambda y, tglob=tglob, cpar=cpar:
                       y.dma_start(out=y3_sb[cpar][:], in_=ysp[tglob * 128:(tglob + 1) * 128, :]),
                       waits=yw_, inc=s_yr[cpar], k=16)
                    nyr = s_yr[cpar].n
                    go("sync", lambda y, tglob=tglob, cpar=cpar:
                       y.dma_start(out=x3_sb[cpar][:, 0, :],
                                   in_=xb[tglob * 128:(tglob + 1) * 128, :]),
                       waits=yw_, inc=s_x4[cpar], k=16)
                    nx4 = s_x4[cpar].n
                    comb_done = locals().get("comb_done", {})
                    ow = [] if out_i < 2 else [(s_out[cpar], 16 * (out_i // 2))]
                    vop(lambda v, tt=tt, cpar=cpar:
                        v.tensor_tensor(out=o_sb[cpar][:], in0=acc_sb[:, tt, :], in1=y3_sb[cpar][:],
                                        op=ALU.add),
                        waits=[(s_yr[cpar], nyr), (s_v, acc_last[tt])] + ow)
                    vop(lambda v, tt=tt, cpar=cpar, tglob=tglob:
                        v.scalar_tensor_tensor(out=o_sb[cpar][:], in0=x3_sb[cpar][:, 0, :],
                                               scalar=r1_all[:, tglob:tglob + 1], in1=o_sb[cpar][:],
                                               op0=ALU.mult, op1=ALU.add),
                        waits=[(s_x4[cpar], nx4)])
                    comb_done[tt] = vop(lambda v, tt=tt, cpar=cpar, tglob=tglob:
                                        v.tensor_scalar(out=o_sb[cpar][:], in0=o_sb[cpar][:],
                                                        scalar1=mur_all[:, tglob:tglob + 1], scalar2=None,
                                                        op0=ALU.subtract))
                    go("sync", lambda y, tglob=tglob, cpar=cpar:
                       y.dma_start(out=out[tglob * 128:(tglob + 1) * 128, :], in_=o_sb[cpar][:]),
                       waits=[(s_v, comb_done[tt])], inc=s_out[cpar], k=16)
                    out_i += 1
                flush_block()

        fw = [(s_out[0], s_out[0].n), (s_out[1], s_out[1].n)]
        if taps:
            fw.append((s_tap, s_tap.n))
        go("sync", lambda y: y.nop(), waits=fw)
        flush_block()

    return nc



def relayout_params(inputs):
    d = {}
    d["p_lng"] = np.ascontiguousarray(np.asarray(inputs["L0_attn_ln_g"]).reshape(MD, 128).T)
    d["p_lnb"] = np.ascontiguousarray(np.asarray(inputs["L0_attn_ln_b"]).reshape(MD, 128).T)
    d["p_binu"] = np.ascontiguousarray(np.asarray(inputs["L0_attn_b_in"])[0:E].reshape(ME, 128).T)
    d["p_bgs"] = np.ascontiguousarray(np.asarray(inputs["L0_ffn_bg_s"]).reshape(MI, 128).T)
    d["p_bus"] = np.ascontiguousarray(np.asarray(inputs["L0_ffn_bu_s"]).reshape(MI, 128).T)
    d["p_bge"] = np.ascontiguousarray(np.asarray(inputs["L0_ffn_bg"]).reshape(NE, MI, 128).transpose(2, 0, 1))
    d["p_bue"] = np.ascontiguousarray(np.asarray(inputs["L0_ffn_bu"]).reshape(NE, MI, 128).transpose(2, 0, 1))
    return d


def kernel(**inputs):
    L, TOWN = 16384, 4096
    x = np.asarray(inputs["L0_attn_x"])
    wnames = [k for k in inputs if k != "L0_attn_x"]
    nc = build(L=L, TOWN=TOWN, TB=1024, taps=False)
    derived = relayout_params(inputs)
    in_maps = []
    for c in range(8):
        b, q0 = c // 4, (c % 4) * TOWN
        xb_c = np.ascontiguousarray(np.concatenate([x[b, q0:], x[b, :q0]], axis=0))
        m = {"xb": xb_c}
        for k in wnames:
            m[k] = np.ascontiguousarray(np.asarray(inputs[k]))
        m.update(derived)
        in_maps.append(m)
    res = run_bass_kernel_spmd(nc, in_maps, core_ids=list(range(8)))
    outp = np.empty((2, L, D), dtype=np.float32)
    for c in range(8):
        b, q0 = c // 4, (c % 4) * TOWN
        outp[b, q0:q0 + TOWN] = res.results[c]["out"]
    return outp


# version 04
# speedup vs baseline: 1.5390x; 1.1550x over previous
import numpy as np
from contextlib import ExitStack

import concourse.bass as bass
import concourse.mybir as mybir
from concourse.bass_utils import run_bass_kernel_spmd

F32 = mybir.dt.float32
BF16 = mybir.dt.bfloat16
AF = mybir.ActivationFunctionType
ALU = mybir.AluOpType
AX = mybir.AxisListType

D = 1024
MD = 8
E = 2048
ME = E // 128
S = 128
NE = 32
II = 512
MI = II // 128
TOPK = 8
EPS = 1e-5


def build(L=16384, TOWN=4096, TB=2048, taps=False):
    assert L % 512 == 0 and TOWN % 512 == 0 and TB % 512 == 0 and TOWN % TB == 0
    nst = L // 512
    nst_own = TOWN // 512
    nqb = TOWN // 512
    G = L // 512
    ntb = TOWN // TB
    nch = TB // 512
    ntile_b = TB // 128
    NTL = L // 128
    INVL = 1.0 / L

    nc = bass.Bass("TRN2", target_bir_lowering=False)

    xb = nc.dram_tensor("xb", [L, D], F32, kind="ExternalInput").ap()
    W_in = nc.dram_tensor("L0_attn_W_in", [D, 2 * E + S], F32, kind="ExternalInput").ap()
    b_in = nc.dram_tensor("L0_attn_b_in", [2 * E + S], F32, kind="ExternalInput").ap()
    W_out = nc.dram_tensor("L0_attn_W_out", [E, D], F32, kind="ExternalInput").ap()
    b_out = nc.dram_tensor("L0_attn_b_out", [D], F32, kind="ExternalInput").ap()
    ln_g = nc.dram_tensor("L0_attn_ln_g", [D], F32, kind="ExternalInput").ap()
    ln_b = nc.dram_tensor("L0_attn_ln_b", [D], F32, kind="ExternalInput").ap()
    gq = nc.dram_tensor("L0_attn_gamma_q", [S], F32, kind="ExternalInput").ap()
    bq = nc.dram_tensor("L0_attn_beta_q", [S], F32, kind="ExternalInput").ap()
    gk = nc.dram_tensor("L0_attn_gamma_k", [S], F32, kind="ExternalInput").ap()
    bk = nc.dram_tensor("L0_attn_beta_k", [S], F32, kind="ExternalInput").ap()
    Wr = nc.dram_tensor("L0_ffn_Wr", [D, NE], F32, kind="ExternalInput").ap()
    br = nc.dram_tensor("L0_ffn_br", [NE], F32, kind="ExternalInput").ap()
    Wg_s = nc.dram_tensor("L0_ffn_Wg_s", [D, II], F32, kind="ExternalInput").ap()
    bg_s = nc.dram_tensor("L0_ffn_bg_s", [II], F32, kind="ExternalInput").ap()
    Wu_s = nc.dram_tensor("L0_ffn_Wu_s", [D, II], F32, kind="ExternalInput").ap()
    bu_s = nc.dram_tensor("L0_ffn_bu_s", [II], F32, kind="ExternalInput").ap()
    Wd_s = nc.dram_tensor("L0_ffn_Wd_s", [II, D], F32, kind="ExternalInput").ap()
    bd_s = nc.dram_tensor("L0_ffn_bd_s", [D], F32, kind="ExternalInput").ap()
    Wg = nc.dram_tensor("L0_ffn_Wg", [NE, D, II], F32, kind="ExternalInput").ap()
    bg = nc.dram_tensor("L0_ffn_bg", [NE, II], F32, kind="ExternalInput").ap()
    Wu = nc.dram_tensor("L0_ffn_Wu", [NE, D, II], F32, kind="ExternalInput").ap()
    bu = nc.dram_tensor("L0_ffn_bu", [NE, II], F32, kind="ExternalInput").ap()
    Wd = nc.dram_tensor("L0_ffn_Wd", [NE, II, D], F32, kind="ExternalInput").ap()
    bd = nc.dram_tensor("L0_ffn_bd", [NE, D], F32, kind="ExternalInput").ap()
    p_lng = nc.dram_tensor("p_lng", [128, MD], F32, kind="ExternalInput").ap()
    p_lnb = nc.dram_tensor("p_lnb", [128, MD], F32, kind="ExternalInput").ap()
    p_binu = nc.dram_tensor("p_binu", [128, ME], F32, kind="ExternalInput").ap()
    p_bgs = nc.dram_tensor("p_bgs", [128, MI], F32, kind="ExternalInput").ap()
    p_bus = nc.dram_tensor("p_bus", [128, MI], F32, kind="ExternalInput").ap()
    p_bge = nc.dram_tensor("p_bge", [128, NE, MI], F32, kind="ExternalInput").ap()
    p_bue = nc.dram_tensor("p_bue", [128, NE, MI], F32, kind="ExternalInput").ap()
    out = nc.dram_tensor("out", [TOWN, D], F32, kind="ExternalOutput").ap()
    vsp = nc.dram_tensor("vsp", [L, E], BF16, kind="Internal").ap()
    x2sp = nc.dram_tensor("x2sp", [nst_own, 128, MD, 512], BF16, kind="Internal").ap()
    ysp = nc.dram_tensor("ysp", [TOWN, D], F32, kind="Internal").ap()
    if taps:
        t_kT = nc.dram_tensor("t_kT", [128, L], BF16, kind="ExternalOutput").ap()
        t_qT = nc.dram_tensor("t_qT", [128, TOWN], BF16, kind="ExternalOutput").ap()
        t_w = nc.dram_tensor("t_w", [128, TOWN // 128, NE], F32, kind="ExternalOutput").ap()
        t_y = nc.dram_tensor("t_y", [TOWN, D], F32, kind="ExternalOutput").ap()
        t_v = nc.dram_tensor("t_v", [L, E], BF16, kind="ExternalOutput").ap()
        t_x2 = nc.dram_tensor("t_x2", [nst_own, 128, MD, 512], BF16, kind="ExternalOutput").ap()

    with ExitStack() as ctx:
        EC = ctx.enter_context

        class Sem:
            def __init__(self, name):
                self.h = EC(nc.semaphore(name))
                self.n = 0

        s_wg = Sem("s_wg")
        s_we = [Sem("s_we0"), Sem("s_we1")]
        s_gp = Sem("s_gp")
        s_x = [Sem("s_x0"), Sem("s_x1")]
        s_tr = [Sem("s_tr0"), Sem("s_tr1")]
        s_vw = [Sem("s_vw0"), Sem("s_vw1")]
        s_x2w = Sem("s_x2w")
        s_x2r = [Sem("s_x2r0"), Sem("s_x2r1")]
        s_vr = [Sem("s_vr0"), Sem("s_vr1"), Sem("s_vr2")]
        s_yw = [Sem("s_yw0"), Sem("s_yw1")]
        s_yr = [Sem("s_yr0"), Sem("s_yr1")]
        s_x3 = [Sem("s_x30"), Sem("s_x31")]
        s_tr3 = [Sem("s_tr30"), Sem("s_tr31")]
        s_x4 = [Sem("s_x40"), Sem("s_x41")]
        s_trr = [Sem("s_trr0"), Sem("s_trr1")]
        s_out = [Sem("s_out0"), Sem("s_out1")]
        s_tap = Sem("s_tap")
        s_mm = Sem("s_mm")
        s_sc = Sem("s_sc")
        s_v = Sem("s_v")

        ones_row = EC(nc.sbuf_tensor("ones_row", [1, 128], BF16))
        eps_pp = EC(nc.sbuf_tensor("eps_pp", [128, 1], F32))
        invl_pp = EC(nc.sbuf_tensor("invl_pp", [128, 1], F32))
        idn = EC(nc.sbuf_tensor("idn", [128, 128], F32))
        bnsc = EC(nc.sbuf_tensor("bnsc", [128, 12], F32))
        mv_all = EC(nc.sbuf_tensor("mv_all", [128, NTL, 2], F32))
        mu_all = EC(nc.sbuf_tensor("mu_all", [128, NTL], F32))
        r_all = EC(nc.sbuf_tensor("r_all", [128, NTL], F32))
        rr_all = EC(nc.sbuf_tensor("rr_all", [128, NTL], F32))
        r1_all = EC(nc.sbuf_tensor("r1_all", [128, NTL], F32))
        mur_all = EC(nc.sbuf_tensor("mur_all", [128, NTL], F32))
        sd_all = EC(nc.sbuf_tensor("sd_all", [128, NTL], F32))
        v2_all = EC(nc.sbuf_tensor("v2_all", [128, NTL], F32))
        lng_pp = EC(nc.sbuf_tensor("lng_pp", [128, MD], F32))
        lnb_pp = EC(nc.sbuf_tensor("lnb_pp", [128, MD], F32))
        binz_pp = EC(nc.sbuf_tensor("binz_pp", [128, 1], F32))
        binu_pp = EC(nc.sbuf_tensor("binu_pp", [128, ME], F32))
        binv_row = EC(nc.sbuf_tensor("binv_row", [1, E], BF16))
        bout_row = EC(nc.sbuf_tensor("bout_row", [1, D], BF16))
        br_row = EC(nc.sbuf_tensor("br_row", [1, NE], BF16))
        gq_pp = EC(nc.sbuf_tensor("gq_pp", [128, 1], F32))
        bq_pp = EC(nc.sbuf_tensor("bq_pp", [128, 1], F32))
        gk_pp = EC(nc.sbuf_tensor("gk_pp", [128, 1], F32))
        bk_pp = EC(nc.sbuf_tensor("bk_pp", [128, 1], F32))
        bgs_pp = EC(nc.sbuf_tensor("bgs_pp", [128, MI], F32))
        bus_pp = EC(nc.sbuf_tensor("bus_pp", [128, MI], F32))
        bge_pp = EC(nc.sbuf_tensor("bge_pp", [128, NE, MI], F32))
        bue_pp = EC(nc.sbuf_tensor("bue_pp", [128, NE, MI], F32))
        bd33 = EC(nc.sbuf_tensor("bd33", [33, D], BF16))
        Wr_sb = EC(nc.sbuf_tensor("Wr_sb", [128, MD, NE], BF16))
        NTO = TOWN // 128
        w_sb = EC(nc.sbuf_tensor("w_sb", [128, NTO, NE], F32))
        w33 = EC(nc.sbuf_tensor("w33", [33, TOWN], BF16))
        logits_all = EC(nc.sbuf_tensor("logits_all", [128, NTO, NE], F32))
        aff_sb = EC(nc.sbuf_tensor("aff_sb", [128, NE], F32))
        a2_sb = EC(nc.sbuf_tensor("a2_sb", [128, NE], F32))
        mx_sb = EC(nc.sbuf_tensor("mx_sb", [128, 1], F32))
        msk_sb = EC(nc.sbuf_tensor("msk_sb", [128, NE], F32))
        sum_sb = EC(nc.sbuf_tensor("sum_sb", [128, 1], F32))
        epr_sb = EC(nc.sbuf_tensor("epr_sb", [128, 1], F32))

        ops = {k: [] for k in ("sync", "gpsimd", "vector", "scalar", "tensor")}

        def go(eng, emitfn, waits=(), inc=None, k=1):
            waits = tuple((s.h, int(v)) for (s, v) in waits if v > 0)

            def run(x, emitfn=emitfn, waits=waits, inc=inc, k=k):
                for (h, v) in waits:
                    x.wait_ge(h, v)
                r = emitfn(x)
                if inc is not None:
                    r.then_inc(inc.h, k)
            ops[eng].append(run)
            if inc is not None:
                inc.n += k

        def vop(emitfn, waits=()):
            go("vector", emitfn, [(s_v, s_v.n)] + list(waits), inc=s_v)
            return s_v.n

        def sop(emitfn, waits=()):
            go("scalar", emitfn, [(s_sc, s_sc.n)] + list(waits), inc=s_sc)
            return s_sc.n

        def gop(emitfn, waits=()):
            go("gpsimd", emitfn, [(s_gp, s_gp.n)] + list(waits), inc=s_gp)
            return s_gp.n

        def flush_block():
            with nc.Block() as block:
                for name in ("sync", "gpsimd", "vector", "scalar", "tensor"):
                    lst = ops[name]
                    if not lst:
                        continue

                    def body(engine, lst=lst):
                        for f in lst:
                            f(engine)
                    getattr(block, name)(body)
            for k in ops:
                ops[k] = []

        gop(lambda g: g.memset(ones_row[:], 1.0))
        gop(lambda g: g.memset(eps_pp[:], EPS))
        gop(lambda g: g.memset(invl_pp[:], INVL))
        gop(lambda g: g.memset(idn[:], 0.0))
        gop(lambda g: g.memset(epr_sb[:], 1e-9))
        gop(lambda g: g.memset(w33[32:33, :], 1.0))
        gop(lambda g: g.affine_select(out=idn[:], in_=idn[:], compare_op=ALU.not_equal,
                                      fill=1.0, base=0, pattern=[[-1, 128]], channel_multiplier=1))

        def wload(dst_ap, src_ap, sem, waits=()):
            go("gpsimd", lambda g, dd=dst_ap, ss=src_ap: g.dma_start(out=dd, in_=ss),
               waits=waits, inc=sem, k=16)

        wload(lng_pp[:], p_lng[:, :], s_wg)
        wload(lnb_pp[:], p_lnb[:, :], s_wg)
        wload(binz_pp[:], b_in[2 * E:2 * E + S, None], s_wg)
        wload(binu_pp[:], p_binu[:, :], s_wg)
        wload(binv_row[:], b_in[None, E:2 * E], s_wg)
        wload(bout_row[:], b_out[None, :], s_wg)
        wload(br_row[:], br[None, :], s_wg)
        wload(gq_pp[:], gq[:, None], s_wg)
        wload(bq_pp[:], bq[:, None], s_wg)
        wload(gk_pp[:], gk[:, None], s_wg)
        wload(bk_pp[:], bk[:, None], s_wg)
        wload(bgs_pp[:], p_bgs[:, :], s_wg)
        wload(bus_pp[:], p_bus[:, :], s_wg)
        wload(bge_pp[:], p_bge[:, :, :], s_wg)
        wload(bue_pp[:], p_bue[:, :, :], s_wg)
        wload(bd33[0:32, :], bd[:, :], s_wg)
        wload(bd33[32:33, :], bd_s[None, :], s_wg)
        wload(Wr_sb[:], Wr.rearrange("(m p) e -> p m e", p=128), s_wg)
        GP0 = s_gp.n
        flush_block()

        with ExitStack() as ph12:
            E12 = ph12.enter_context
            kT = E12(nc.sbuf_tensor("kT", [128, L], BF16))
            qT = E12(nc.sbuf_tensor("qT", [128, TOWN], BF16))
            ph1 = ExitStack()
            E11 = ph1.enter_context
            Wz_sb = E11(nc.sbuf_tensor("Wz_sb", [128, MD, S], BF16))
            Wv_sb = E11(nc.sbuf_tensor("Wv_sb", [128, MD, E], BF16))
            x_sb = [E11(nc.sbuf_tensor(f"x_sb{i}", [128, 4, D], F32)) for i in range(2)]
            xn2_bf = [E11(nc.sbuf_tensor(f"xn2_bf{i}", [128, 4, D], BF16)) for i in range(2)]
            xn2T = [E11(nc.sbuf_tensor(f"xn2T{i}", [128, MD, 512], BF16)) for i in range(2)]
            xn2Ta = [E11(nc.sbuf_tensor(f"xn2Ta{i}", [128, MD, 512], BF16)) for i in range(2)]
            tmpz = E11(nc.sbuf_tensor("tmpz", [128, 512], F32))
            v_sb = [E11(nc.sbuf_tensor(f"v_sb{i}", [128, E], BF16)) for i in range(2)]
            xnb_tmp = [E11(nc.sbuf_tensor(f"xnb_tmp{i}", [128, D], BF16)) for i in range(2)]
            xnT_tmp = [E11(nc.sbuf_tensor(f"xnT_tmp{i}", [128, MD, 128], BF16)) for i in range(2)]
            ps_z = E11(nc.psum_tensor("ps_z", [128, 512], F32))
            ps_v = [E11(nc.psum_tensor(f"ps_v{i}", [128, 512], F32)) for i in range(2)]
            ps_l = E11(nc.psum_tensor("ps_l", [128, 512], F32))

            wload(Wz_sb[:], W_in[:, 2 * E:2 * E + S].rearrange("(m p) e -> p m e", p=128), s_wg)
            wload(Wv_sb[:], W_in[:, E:2 * E].rearrange("(m p) e -> p m e", p=128), s_wg)
            WG_ALL = s_wg.n

            last_bn = {}
            for st in range(nst):
                par = st % 2
                xw = [] if st < 2 else [(s_v, last_bn[st - 2])]
                go("sync", lambda y, st=st, par=par:
                   y.dma_start(out=x_sb[par][:],
                               in_=xb[st * 512:(st + 1) * 512, :].rearrange("(s p) d -> p s d", p=128)),
                   waits=xw, inc=s_x[par], k=16)
                nx = s_x[par].n
                for s in range(4):
                    t = st * 4 + s
                    vop(lambda v, par=par, s=s: v.bn_stats(out=bnsc[:, 0:6], in_=x_sb[par][:, s, 0:512]),
                        waits=[(s_x[par], nx)])
                    vop(lambda v, par=par, s=s: v.bn_stats(out=bnsc[:, 6:12], in_=x_sb[par][:, s, 512:1024]))
                    last_bn[st] = vop(lambda v, t=t: v.bn_aggr(out=mv_all[:, t, :], in_=bnsc[:]))
            _sd = sop(lambda sc: sc.activation(out=sd_all[:], in_=mv_all[:, :, 1], func=AF.Sqrt,
                                               bias=eps_pp[:]),
                      waits=[(s_v, last_bn[nst - 1]), (s_gp, GP0)])
            _r = vop(lambda v: v.reciprocal(out=r_all[:], in_=sd_all[:]), waits=[(s_sc, _sd)])
            vop(lambda v: v.tensor_tensor(out=v2_all[:], in0=mv_all[:, :, 1], in1=r_all[:], op=ALU.mult))
            vv2 = vop(lambda v: v.tensor_tensor(out=v2_all[:], in0=v2_all[:], in1=r_all[:], op=ALU.mult))
            _s2 = sop(lambda sc: sc.activation(out=v2_all[:], in_=v2_all[:], func=AF.Sqrt, bias=eps_pp[:]),
                      waits=[(s_v, vv2)])
            vop(lambda v: v.reciprocal(out=rr_all[:], in_=v2_all[:]), waits=[(s_sc, _s2)])
            vop(lambda v: v.tensor_tensor(out=rr_all[:], in0=rr_all[:], in1=r_all[:], op=ALU.mult))
            vop(lambda v: v.tensor_copy(out=mu_all[:], in_=mv_all[:, :, 0]))
            vop(lambda v: v.tensor_scalar(out=r1_all[:], in0=r_all[:], scalar1=1.0, scalar2=None, op0=ALU.add))
            STATS_DONE = vop(lambda v: v.tensor_tensor(out=mur_all[:], in0=mu_all[:], in1=r_all[:], op=ALU.mult))
            flush_block()

            lastcons = {}
            tz_cnt = {}
            mm_z = {}
            mm_v = {}
            sc_v = {}
            spill_done = {}
            aff_done = {}
            x2w_cnt = {}
            xld_cnt = {}
            lev_cnt = {}
            rmm_cnt = {}

            def emit_xload(st, waits):
                par = st % 2
                go("sync", lambda y, st=st, par=par:
                   y.dma_start(out=x_sb[par][:],
                               in_=xb[st * 512:(st + 1) * 512, :].rearrange("(s p) d -> p s d", p=128)),
                   waits=waits, inc=s_x[par], k=16)
                xld_cnt[st] = s_x[par].n

            emit_xload(0, [(s_v, STATS_DONE)])
            if nst > 1:
                emit_xload(1, [(s_v, STATS_DONE)])
            for st in range(nst):
                par = st % 2
                own = st < nst_own
                nx = xld_cnt[st]
                trw = [] if st < 2 else [(s_tr[par], 16 * 4 * (st // 2))]
                xn2cnt = {}
                for s in range(4):
                    t = st * 4 + s
                    xn2cnt[s] = vop(lambda v, par=par, s=s, t=t:
                                    v.tensor_scalar(out=xn2_bf[par][:, s, :], in0=x_sb[par][:, s, :],
                                                    scalar1=mu_all[:, t:t + 1], scalar2=rr_all[:, t:t + 1],
                                                    op0=ALU.subtract, op1=ALU.mult),
                                    waits=[(s_x[par], nx)] + trw)
                lastcons[st] = xn2cnt[3]
                afw = [] if st < 2 else [(s_v, aff_done[st - 2])]
                for s in range(4):
                    go("sync", lambda y, par=par, s=s:
                       y.dma_start(out=xn2T[par][:, :, s * 128:(s + 1) * 128],
                                   in_=xn2_bf[par][:, s, :], transpose=True),
                       waits=[(s_v, xn2cnt[s])] + afw, inc=s_tr[par], k=16)
                ntr = s_tr[par].n
                pw = [] if st < 2 else [(s_mm, mm_v[st - 2])]
                if st == 0:
                    pw.append((s_wg, WG_ALL))
                if st >= 2 and (st - 2) < nst_own:
                    pw.append((s_x2w, x2w_cnt[st - 2]))
                for m in range(MD):
                    aff_done[st] = vop(lambda v, par=par, m=m:
                                       v.tensor_scalar(out=xn2Ta[par][:, m, :], in0=xn2T[par][:, m, :],
                                                       scalar1=lng_pp[:, m:m + 1], scalar2=lnb_pp[:, m:m + 1],
                                                       op0=ALU.mult, op1=ALU.add),
                                       waits=[(s_tr[par], ntr)] + pw)
                if own:
                    go("gpsimd", lambda g, par=par, st=st:
                       g.dma_start(out=x2sp[st, :, :, :], in_=xn2Ta[par][:]),
                       waits=[(s_v, aff_done[st])], inc=s_x2w, k=16)
                    x2w_cnt[st] = s_x2w.n
                zw = [(s_v, aff_done[st])]
                if st == 0:
                    zw.append((s_wg, WG_ALL))
                if st >= 1:
                    zw.append((s_sc, tz_cnt[st - 1]))

                def emit_z(tensor, par=par, st=st):
                    for m in range(MD):
                        r = tensor.matmul(ps_z[:], Wz_sb[:, m, :], xn2Ta[par][:, m, :],
                                          start=(m == 0), stop=(m == MD - 1))
                    return r
                go("tensor", emit_z, waits=zw, inc=s_mm)
                mm_z[st] = s_mm.n
                tz_cnt[st] = tz = sop(
                    lambda sc: sc.activation(out=tmpz[:], in_=ps_z[:], func=AF.Silu, bias=binz_pp[:]),
                    waits=[(s_mm, mm_z[st])] + ([] if st < 1 else [(s_v, lastkq[st - 1])]))
                lastkq = locals().get("lastkq", {})
                lastkq[st] = vop(lambda v, st=st:
                                 v.tensor_scalar(out=kT[:, st * 512:(st + 1) * 512], in0=tmpz[:],
                                                 scalar1=gk_pp[:], scalar2=bk_pp[:],
                                                 op0=ALU.mult, op1=ALU.add),
                                 waits=[(s_sc, tz)])
                if own:
                    lastkq[st] = vop(lambda v, st=st:
                                     v.tensor_scalar(out=qT[:, st * 512:(st + 1) * 512], in0=tmpz[:],
                                                     scalar1=gq_pp[:], scalar2=bq_pp[:],
                                                     op0=ALU.mult, op1=ALU.add))
                if own:
                    for s in range(4):
                        t = st * 4 + s
                        sp = s % 2
                        xw2 = [(s_x[par], nx)]
                        if t >= 2:
                            xw2.append((s_trr[sp], 16 * (t // 2)))
                        xts = vop(lambda v, par=par, s=s, t=t, sp=sp:
                                  v.tensor_scalar(out=xnb_tmp[sp][:], in0=x_sb[par][:, s, :],
                                                  scalar1=mu_all[:, t:t + 1], scalar2=r_all[:, t:t + 1],
                                                  op0=ALU.subtract, op1=ALU.mult),
                                  waits=xw2)
                        trw2 = [(s_v, xts)]
                        if t >= 2:
                            trw2.append((s_mm, rmm_cnt[t - 2]))
                        go("sync", lambda y, sp=sp:
                           y.dma_start(out=xnT_tmp[sp][:], in_=xnb_tmp[sp][:], transpose=True),
                           waits=trw2, inc=s_trr[sp], k=16)
                        ntrr = s_trr[sp].n
                        rw_ = [(s_trr[sp], ntrr)]
                        if t >= 1:
                            rw_.append((s_v, lev_cnt[t - 1]))

                        def emit_rt(tensor, sp=sp):
                            for m in range(MD):
                                tensor.matmul(ps_l[:, 0:NE], xnT_tmp[sp][:, m, :], Wr_sb[:, m, :],
                                              start=(m == 0), stop=False)
                            return tensor.matmul(ps_l[:, 0:NE], ones_row[:], br_row[:],
                                                 start=False, stop=True)
                        go("tensor", emit_rt, waits=rw_, inc=s_mm)
                        rmm_cnt[t] = s_mm.n
                        lev_cnt[t] = vop(lambda v, t=t: v.tensor_copy(out=logits_all[:, t, :],
                                                                      in_=ps_l[:, 0:NE]),
                                         waits=[(s_mm, rmm_cnt[t])])
                if own:
                    lastcons[st] = max(lastcons[st], s_v.n)
                if st + 2 < nst:
                    emit_xload(st + 2, [(s_v, lastcons[st])])
                for s in range(4):
                    t = st * 4 + s
                    spar = t % 2
                    for ec in range(4):
                        vpar = (s * 4 + ec) % 2
                        slot = st * 16 + s * 4 + ec
                        mw = []
                        if slot >= 2:
                            mw.append((s_sc, sc_v[slot - 2]))

                        def emit_v(tensor, par=par, s=s, ec=ec, vpar=vpar):
                            for m in range(MD):
                                tensor.matmul(ps_v[vpar][:], xn2Ta[par][:, m, s * 128:(s + 1) * 128],
                                              Wv_sb[:, m, ec * 512:(ec + 1) * 512],
                                              start=(m == 0), stop=False)
                            return tensor.matmul(ps_v[vpar][:], ones_row[:],
                                                 binv_row[:, ec * 512:(ec + 1) * 512],
                                                 start=False, stop=True)
                        go("tensor", emit_v, waits=mw, inc=s_mm)
                        mm_v[st] = s_mm.n
                        sw = [(s_mm, mm_v[st])]
                        if t >= 2:
                            sw.append((s_vw[spar], 16 * (t // 2)))
                        sc_v[slot] = sop(lambda sc, vpar=vpar, spar=spar, ec=ec:
                                         sc.activation(out=v_sb[spar][:, ec * 512:(ec + 1) * 512],
                                                       in_=ps_v[vpar][:], func=AF.Silu),
                                         waits=sw)
                    go("gpsimd", lambda g, spar=spar, t=t:
                       g.dma_start(out=vsp[t * 128:(t + 1) * 128, :], in_=v_sb[spar][:]),
                       waits=[(s_sc, sc_v[st * 16 + s * 4 + 3])], inc=s_vw[spar], k=16)
                    spill_done[t] = (spar, s_vw[spar].n)
            NTO = TOWN // 128
            sig_all = sop(lambda sc: sc.activation(out=logits_all[:, :, :], in_=logits_all[:, :, :],
                                                   func=AF.Sigmoid),
                          waits=[(s_v, lev_cnt[NTO - 1])])
            wtk = {}
            pending_topk = []

            def _mk_topk(t):
                ops_t = []
                ops_t.append((lambda v, t=t: v.tensor_copy(out=aff_sb[:], in_=logits_all[:, t, :]),
                              [(s_sc, sig_all), (s_gp, GP0)]))
                ops_t.append((lambda v: v.tensor_copy(out=a2_sb[:], in_=aff_sb[:]), []))
                for it in range(TOPK - 1):
                    ops_t.append((lambda v: v.tensor_reduce(out=mx_sb[:], in_=a2_sb[:], axis=AX.X, op=ALU.max), []))
                    ops_t.append((lambda v: v.tensor_scalar(out=msk_sb[:], in0=a2_sb[:], scalar1=mx_sb[:],
                                                            scalar2=None, op0=ALU.is_ge), []))
                    ops_t.append((lambda v: v.scalar_tensor_tensor(out=a2_sb[:], in0=msk_sb[:], scalar=-1e30,
                                                                   in1=a2_sb[:], op0=ALU.mult, op1=ALU.add), []))
                ops_t.append((lambda v: v.tensor_reduce(out=mx_sb[:], in_=a2_sb[:], axis=AX.X, op=ALU.max), []))
                ops_t.append((lambda v: v.tensor_scalar(out=msk_sb[:], in0=aff_sb[:], scalar1=mx_sb[:],
                                                        scalar2=None, op0=ALU.is_ge), []))
                ops_t.append((lambda v, t=t: v.tensor_tensor(out=w_sb[:, t, :], in0=aff_sb[:],
                                                             in1=msk_sb[:], op=ALU.mult), []))
                ops_t.append((lambda v, t=t: v.tensor_reduce(out=sum_sb[:], in_=w_sb[:, t, :],
                                                             axis=AX.X, op=ALU.add), []))
                ops_t.append((lambda v: v.tensor_tensor(out=sum_sb[:], in0=sum_sb[:], in1=epr_sb[:], op=ALU.add), []))
                ops_t.append((lambda v: v.reciprocal(out=sum_sb[:], in_=sum_sb[:]), []))
                ops_t.append((lambda v, t=t: v.tensor_scalar(out=w_sb[:, t, :], in0=w_sb[:, t, :],
                                                             scalar1=sum_sb[:], scalar2=None, op0=ALU.mult),
                              [], t))
                return ops_t

            for t in range(NTO):
                pending_topk.extend(_mk_topk(t))

            def drain_topk(n=1):
                for _ in range(n):
                    if not pending_topk:
                        return
                    item = pending_topk.pop(0)
                    if len(item) == 3:
                        fn, w_, t = item
                        wtk[t] = vop(fn, waits=w_)
                    else:
                        fn, w_ = item
                        vop(fn, waits=w_)

            go("sync", lambda y: y.nop(),
               waits=[(s_vw[0], s_vw[0].n), (s_vw[1], s_vw[1].n), (s_x2w, s_x2w.n)])
            if taps:
                go("sync", lambda y: y.dma_start(out=t_kT[:, :], in_=kT[:]),
                   waits=[(s_v, s_v.n)], inc=s_tap, k=16)
                go("sync", lambda y: y.dma_start(out=t_qT[:, :], in_=qT[:]), inc=s_tap, k=16)
                go("sync", lambda y: y.dma_start(out=t_v[:, :], in_=vsp[:, :]), inc=s_tap, k=16)
                go("sync", lambda y: y.dma_start(out=t_x2[:, :, :, :], in_=x2sp[:, :, :, :]), inc=s_tap, k=16)
                go("sync", lambda y: y.nop(), waits=[(s_tap, s_tap.n)])
            flush_block()
            ph1.close()

            with ExitStack() as ph2:
                E2 = ph2.enter_context
                Wu_sb = E2(nc.sbuf_tensor("Wu_sb", [128, MD, E], BF16))
                Wout_sb = E2(nc.sbuf_tensor("Wout_sb", [128, ME, D], BF16))
                x2_sb = [E2(nc.sbuf_tensor(f"x2_sb{i}", [128, MD, 512], BF16)) for i in range(2)]
                v_buf = [E2(nc.sbuf_tensor(f"v_buf{i}", [128, 4, 512], BF16)) for i in range(3)]
                srelu = [E2(nc.sbuf_tensor(f"srelu{i}", [128, 2048], BF16)) for i in range(2)]
                ssq = [E2(nc.sbuf_tensor(f"ssq{i}", [128, 2048], BF16)) for i in range(2)]
                uT_eq = E2(nc.sbuf_tensor("uT_eq", [128, 4, 512], BF16))
                uavT = E2(nc.sbuf_tensor("uavT", [128, ME, 512], BF16))
                y_sb = [E2(nc.sbuf_tensor(f"y_sb{i}", [128, D], F32)) for i in range(2)]
                ps_s = E2(nc.psum_tensor("ps_s", [128, 2048], F32))
                ps_av = E2(nc.psum_tensor("ps_av", [128, 2048], F32))

                wload(Wu_sb[:], W_in[:, 0:E].rearrange("(m p) e -> p m e", p=128), s_wg)
                wload(Wout_sb[:], W_out.rearrange("(m p) d -> p m d", p=128), s_wg)
                WG_PH2 = s_wg.n

                relu_cnt = {}
                sq_cnt = {}
                mm1_cnt = {}
                mm2_cnt = {}
                uav_cnt = {}
                usilu_cnt = {}
                wout_cnt = {}
                ycpy_cnt = {}
                vld_cnt = {}
                yw_i = 0
                for qb in range(nqb):
                    qpar = qb % 2
                    xw = [(s_x2w, s_x2w.n)]
                    if qb >= 2:
                        xw.append((s_mm, usilu_cnt[qb - 2][1]))
                    go("sync", lambda y, qb=qb, qpar=qpar:
                       y.dma_start(out=x2_sb[qpar][:], in_=x2sp[qb, :, :, :]),
                       waits=xw, inc=s_x2r[qpar], k=16)
                    nx2 = s_x2r[qpar].n
                    git = 0
                    for eq in range(4):
                        for es in range(4):
                            eu = eq * 4 + es
                            upar = es % 2
                            uw = [(s_x2r[qpar], nx2)]
                            if qb == 0 and eq == 0 and es == 0:
                                uw.append((s_wg, WG_PH2))
                            if eq > 0 or qb > 0:
                                uw.append((s_sc, s_sc.n))
                            if es >= 2:
                                uw.append((s_sc, usil[es - 2]))

                            def emit_u(tensor, qpar=qpar, es=es, upar=upar, eu=eu):
                                for m in range(MD):
                                    r = tensor.matmul(ps_s[:, upar * 512:(upar + 1) * 512],
                                                      Wu_sb[:, m, eu * 128:(eu + 1) * 128],
                                                      x2_sb[qpar][:, m, :],
                                                      start=(m == 0), stop=(m == MD - 1))
                                return r
                            go("tensor", emit_u, waits=uw, inc=s_mm)
                            numm = s_mm.n
                            usil = locals().get("usil", {})
                            suw = [(s_mm, numm)]
                            if eq > 0:
                                suw.append((s_v, uav_cnt[(qb, eq - 1)]))
                            elif qb > 0:
                                suw.append((s_v, uav_cnt[(qb - 1, 3)]))
                            usil[es] = sop(lambda sc, upar=upar, es=es, eu=eu:
                                           sc.activation(out=uT_eq[:, es, :],
                                                         in_=ps_s[:, upar * 512:(upar + 1) * 512],
                                                         func=AF.Silu, bias=binu_pp[:, eu:eu + 1]),
                                           waits=suw)
                        usilu_cnt[qb] = (usil[3], s_mm.n)
                        def emit_vload(g):
                            vp = g % 3
                            vw = []
                            if g >= 3:
                                vw.append((s_mm, mm2_cnt[(qb, eq, g - 3)]))
                            elif (qb, eq) != (0, 0):
                                vw.append((s_mm, prev_mm2_last))
                            go("sync", lambda y, g=g, eq=eq, vp=vp:
                               y.dma_start(out=v_buf[vp][:],
                                           in_=vsp[g * 512:(g + 1) * 512, eq * 512:(eq + 1) * 512]
                                           .rearrange("(k p) e -> p k e", p=128)),
                               waits=vw, inc=s_vr[vp], k=16)
                            vld_cnt[(qb, eq, g)] = (vp, s_vr[vp].n)

                        def emit_mm2_for(g):
                            vp, vn = vld_cnt[(qb, eq, g)]
                            m2w = [(s_v, sq_cnt[(qb, eq, g)]), (s_vr[vp], vn)]
                            if g == 0:
                                if eq > 0:
                                    m2w.append((s_v, uav_cnt[(qb, eq - 1)]))
                                elif qb > 0:
                                    m2w.append((s_v, ycpy_cnt[qb - 1]))

                            def emit_m2(tensor, g=g, vp=vp, eq=eq):
                                for j in range(4):
                                    for es in range(4):
                                        r = tensor.matmul(ps_av[:, es * 512:(es + 1) * 512],
                                                          v_buf[vp][:, j, es * 128:(es + 1) * 128],
                                                          ssq[(g % 2)][:, j * 512:(j + 1) * 512],
                                                          start=(g == 0 and j == 0),
                                                          stop=(g == G - 1 and j == 3),
                                                          skip_group_check=True)
                                return r
                            go("tensor", emit_m2, waits=m2w, inc=s_mm)
                            mm2_cnt[(qb, eq, g)] = s_mm.n

                        emit_vload(0)
                        for g in range(G):
                            gpar = g % 2
                            m1w = []
                            if g == 0:
                                m1w.append((s_sc, usil[3]))
                            else:
                                m1w.append((s_sc, relu_cnt[(qb, eq, g - 1)]))

                            def emit_m1(tensor, qb=qb, g=g):
                                for j in range(4):
                                    kt = g * 4 + j
                                    r = tensor.matmul(ps_s[:, j * 512:(j + 1) * 512],
                                                      kT[:, kt * 128:(kt + 1) * 128],
                                                      qT[:, qb * 512:(qb + 1) * 512],
                                                      start=True, stop=True)
                                return r
                            go("tensor", emit_m1, waits=m1w, inc=s_mm)
                            mm1_cnt[(qb, eq, g)] = s_mm.n
                            rw = [(s_mm, mm1_cnt[(qb, eq, g)])]
                            if g >= 2:
                                rw.append((s_v, sq_cnt[(qb, eq, g - 2)]))
                            relu_cnt[(qb, eq, g)] = sop(lambda sc, gpar=gpar:
                                                        sc.activation(out=srelu[gpar][:], in_=ps_s[:],
                                                                      func=AF.Relu, scale=invl_pp[:]),
                                                        waits=rw)
                            qw = [(s_sc, relu_cnt[(qb, eq, g)])]
                            if g >= 2:
                                qw.append((s_mm, mm2_cnt[(qb, eq, g - 2)]))
                            sq_cnt[(qb, eq, g)] = vop(lambda v, gpar=gpar:
                                                      v.tensor_tensor(out=ssq[gpar][:], in0=srelu[gpar][:],
                                                                      in1=srelu[gpar][:], op=ALU.mult),
                                                      waits=qw)
                            drain_topk(1)
                            if g >= 1:
                                emit_mm2_for(g - 1)
                            if g + 1 < G:
                                emit_vload(g + 1)
                        emit_mm2_for(G - 1)
                        prev_mm2_last = mm2_cnt[(qb, eq, G - 1)]
                        for es in range(4):
                            uav_cnt[(qb, eq)] = vop(lambda v, es=es, eq=eq:
                                                    v.tensor_tensor(out=uavT[:, eq * 4 + es, :],
                                                                    in0=ps_av[:, es * 512:(es + 1) * 512],
                                                                    in1=uT_eq[:, es, :], op=ALU.mult),
                                                    waits=[(s_mm, prev_mm2_last)] +
                                                          ([] if qb == 0 else [(s_mm, wout_cnt.get(qb - 1, 0))]))
                    for qt in range(4):
                        ypar = qt % 2
                        yw_ = [(s_v, uav_cnt[(qb, 3)])]
                        if qt >= 1:
                            yw_.append((s_v, ycpy[qt - 1]))

                        def emit_y2(tensor, qb=qb, qt=qt):
                            r = None
                            for hc in range(2):
                                for m in range(ME):
                                    r = tensor.matmul(ps_av[:, hc * 512:(hc + 1) * 512],
                                                      uavT[:, m, qt * 128:(qt + 1) * 128],
                                                      Wout_sb[:, m, hc * 512:(hc + 1) * 512],
                                                      start=(m == 0), stop=False,
                                                      skip_group_check=True)
                                r = tensor.matmul(ps_av[:, hc * 512:(hc + 1) * 512], ones_row[:],
                                                  bout_row[:, hc * 512:(hc + 1) * 512],
                                                  start=False, stop=True, skip_group_check=True)
                            return r
                        go("tensor", emit_y2, waits=yw_, inc=s_mm)
                        wout_cnt[qb] = s_mm.n
                        ycpy = locals().get("ycpy", {})
                        cw = [(s_mm, wout_cnt[qb])]
                        if yw_i >= 2:
                            cw.append((s_yw[ypar], 16 * (yw_i // 2)))
                        ycpy[qt] = vop(lambda v, ypar=ypar:
                                       v.tensor_copy(out=y_sb[ypar][:], in_=ps_av[:, 0:1024]),
                                       waits=cw)
                        ycpy_cnt[qb] = ycpy[qt]
                        go("sync", lambda y, qb=qb, qt=qt, ypar=ypar:
                           y.dma_start(out=ysp[qb * 512 + qt * 128: qb * 512 + (qt + 1) * 128, :],
                                       in_=y_sb[ypar][:]),
                           waits=[(s_v, ycpy[qt])], inc=s_yw[ypar], k=16)
                        yw_i += 1
                drain_topk(10**9)
                go("sync", lambda y: y.nop(),
                   waits=[(s_yw[0], s_yw[0].n), (s_yw[1], s_yw[1].n)])
                if taps:
                    go("sync", lambda y: y.dma_start(out=t_y[:, :], in_=ysp[:, :]), inc=s_tap, k=16)
                    go("sync", lambda y: y.dma_start(out=t_w[:, :, :], in_=w_sb[:]),
                       waits=[(s_v, s_v.n)], inc=s_tap, k=16)
                    go("sync", lambda y: y.nop(), waits=[(s_tap, s_tap.n)])
                flush_block()

        with ExitStack() as ph3:
            E3 = ph3.enter_context
            xnT_blk = E3(nc.sbuf_tensor("xnT_blk", [128, MD, TB], BF16))
            acc_sb = E3(nc.sbuf_tensor("acc_sb", [128, ntile_b, D], F32))
            x3_sb = [E3(nc.sbuf_tensor(f"x3_sb{i}", [128, 2, D], F32)) for i in range(2)]
            xn_bf = [E3(nc.sbuf_tensor(f"xn_bf{i}", [128, 2, D], BF16)) for i in range(2)]
            Wgs_sb = E3(nc.sbuf_tensor("Wgs_sb", [128, MD, II], BF16))
            Wus_sb = E3(nc.sbuf_tensor("Wus_sb", [128, MD, II], BF16))
            Wds_sb = E3(nc.sbuf_tensor("Wds_sb", [128, MI, D], BF16))
            Wge_sb = [E3(nc.sbuf_tensor(f"Wge_sb{i}", [128, MD, II], BF16)) for i in range(2)]
            Wue_sb = [E3(nc.sbuf_tensor(f"Wue_sb{i}", [128, MD, II], BF16)) for i in range(2)]
            Wde_sb = [E3(nc.sbuf_tensor(f"Wde_sb{i}", [128, MI, D], BF16)) for i in range(2)]
            g_sb = [E3(nc.sbuf_tensor(f"g_sb{i}", [128, II], BF16)) for i in range(2)]
            h_sb = [E3(nc.sbuf_tensor(f"h_sb{i}", [128, MI, II], BF16)) for i in range(2)]
            y3_sb = [E3(nc.sbuf_tensor(f"y3_sb{i}", [128, D], F32)) for i in range(2)]
            o_sb = [E3(nc.sbuf_tensor(f"o_sb{i}", [128, D], F32)) for i in range(2)]
            ps_wt = E3(nc.psum_tensor("ps_wt", [128, 128], F32))
            ps_a = E3(nc.psum_tensor("ps_a", [128, 512], F32))
            ps_b = E3(nc.psum_tensor("ps_b", [128, 512], F32))
            ps_d = [E3(nc.psum_tensor(f"ps_d{i}", [128, D], F32)) for i in range(2)]

            wload(Wgs_sb[:], Wg_s.rearrange("(m p) i -> p m i", p=128), s_wg)
            wload(Wus_sb[:], Wu_s.rearrange("(m p) i -> p m i", p=128), s_wg)
            wload(Wds_sb[:], Wd_s.rearrange("(m p) d -> p m d", p=128), s_wg)
            WG_PH3 = s_wg.n

            def eload(e, buf, waits=()):
                sem = s_we[buf]
                wload(Wge_sb[buf][:], Wg[e].rearrange("(m p) i -> p m i", p=128), sem, waits=waits)
                wload(Wue_sb[buf][:], Wu[e].rearrange("(m p) i -> p m i", p=128), sem)
                wload(Wde_sb[buf][:], Wd[e].rearrange("(m p) d -> p m d", p=128), sem)
                return (sem, sem.n)

            def eload2(e, buf, waits):
                return eload(e, buf, waits=waits)

            eweight_ready = {}
            sweep_idx = 0
            out_i = 0
            wev = {}
            for t in range(TOWN // 128):
                tw_ = [(s_v, wtk[t]), (s_gp, GP0)]
                if t >= 1:
                    tw_.append((s_sc, wev[t - 1]))
                go("tensor", lambda tensor, t=t: tensor.transpose(ps_wt[0:NE, 0:128], w_sb[:, t, :], idn[:]),
                   waits=tw_, inc=s_mm)
                ntw = s_mm.n
                wev[t] = sop(lambda sc, t=t: sc.copy(out=w33[0:NE, t * 128:(t + 1) * 128],
                                                     in_=ps_wt[0:NE, 0:128]),
                             waits=[(s_mm, ntw)])
            W33_DONE = wev[TOWN // 128 - 1]
            for tb in range(ntb):
                lastxn = {}
                ngrp = TB // 256
                for grp in range(ngrp):
                    gpr = grp % 2
                    t0 = tb * ntile_b + grp * 2
                    xw = [] if grp < 2 else [(s_v, lastxn[grp - 2])]
                    go("sync", lambda y, tb=tb, grp=grp, gpr=gpr:
                       y.dma_start(out=x3_sb[gpr][:],
                                   in_=xb[tb * TB + grp * 256: tb * TB + (grp + 1) * 256, :]
                                   .rearrange("(s p) d -> p s d", p=128)),
                       waits=xw, inc=s_x3[gpr], k=16)
                    nx3 = s_x3[gpr].n
                    xncnt = {}
                    trw = [] if grp < 2 else [(s_tr3[gpr], 16 * 2 * (grp // 2))]
                    for s in range(2):
                        t = t0 + s
                        xncnt[s] = vop(lambda v, gpr=gpr, s=s, t=t:
                                       v.tensor_scalar(out=xn_bf[gpr][:, s, :], in0=x3_sb[gpr][:, s, :],
                                                       scalar1=mu_all[:, t:t + 1], scalar2=r_all[:, t:t + 1],
                                                       op0=ALU.subtract, op1=ALU.mult),
                                       waits=[(s_x3[gpr], nx3)] + trw)
                    lastxn[grp] = xncnt[1]
                    for s in range(2):
                        go("sync", lambda y, gpr=gpr, grp=grp, s=s:
                           y.dma_start(out=xnT_blk[:, :, grp * 256 + s * 128: grp * 256 + (s + 1) * 128],
                                       in_=xn_bf[gpr][:, s, :], transpose=True),
                           waits=[(s_v, xncnt[s])], inc=s_tr3[gpr], k=16)
                tr3_all = (s_tr3[0].n, s_tr3[1].n)
                eweight_ready[(tb, 0)] = eload(0, 0)
                eweight_ready[(tb, 1)] = eload(1, 1)

                hstt_cnt = {}
                gsil_cnt = {}
                dacc_cnt = {}
                slot = 0
                dslot = 0
                for ei in range(-1, NE):
                    if ei >= 0:
                        sweep_idx += 1
                    ebuf = ei % 2 if ei >= 0 else 0
                    Wg_cur = Wgs_sb if ei < 0 else Wge_sb[ebuf]
                    Wu_cur = Wus_sb if ei < 0 else Wue_sb[ebuf]
                    Wd_cur = Wds_sb if ei < 0 else Wde_sb[ebuf]
                    for c in range(nch):
                        hpar = (ei * nch + c + NE * nch) % 2
                        for ic in range(MI):
                            gw = []
                            if ei >= 0:
                                gw.append(eweight_ready[(tb, ei)])
                            else:
                                gw.append((s_wg, WG_PH3))
                                gw.append((s_tr3[0], tr3_all[0]))
                                gw.append((s_tr3[1], tr3_all[1]))
                            if slot >= 1:
                                gw.append((s_sc, gsil_cnt[slot - 1]))

                            def emit_g(tensor, Wg_cur=Wg_cur, c=c, ic=ic):
                                for m in range(MD):
                                    r = tensor.matmul(ps_a[:], Wg_cur[:, m, ic * 128:(ic + 1) * 128],
                                                      xnT_blk[:, m, c * 512:(c + 1) * 512],
                                                      start=(m == 0), stop=(m == MD - 1))
                                return r
                            go("tensor", emit_g, waits=gw, inc=s_mm)
                            ngm = s_mm.n
                            bias_ap = (bgs_pp[:, ic:ic + 1] if ei < 0 else bge_pp[:, ei, ic:ic + 1])
                            sgw = [(s_mm, ngm)]
                            if slot >= 2:
                                sgw.append((s_v, hstt_cnt[slot - 2]))
                            gsil_cnt[slot] = sop(lambda sc, bias_ap=bias_ap, gp=slot % 2:
                                                 sc.activation(out=g_sb[gp][:], in_=ps_a[:],
                                                               func=AF.Silu, bias=bias_ap),
                                                 waits=sgw)
                            uw = []
                            if ei >= 0:
                                uw.append(eweight_ready[(tb, ei)])
                            if slot >= 1:
                                uw.append((s_v, hstt_cnt[slot - 1]))

                            def emit_up(tensor, Wu_cur=Wu_cur, c=c, ic=ic):
                                for m in range(MD):
                                    r = tensor.matmul(ps_b[:], Wu_cur[:, m, ic * 128:(ic + 1) * 128],
                                                      xnT_blk[:, m, c * 512:(c + 1) * 512],
                                                      start=(m == 0), stop=(m == MD - 1))
                                return r
                            go("tensor", emit_up, waits=uw, inc=s_mm)
                            num = s_mm.n
                            bu_ap = (bus_pp[:, ic:ic + 1] if ei < 0 else bue_pp[:, ei, ic:ic + 1])
                            hw = [(s_mm, num), (s_sc, gsil_cnt[slot])]
                            key = (hpar, "lastdown")
                            if key in hstt_cnt:
                                hw.append((s_mm, hstt_cnt[key]))
                            hstt_cnt[slot] = vop(lambda v, bu_ap=bu_ap, gp=slot % 2, hpar=hpar, ic=ic:
                                                 v.scalar_tensor_tensor(out=h_sb[hpar][:, ic, :],
                                                                        in0=ps_b[:], scalar=bu_ap,
                                                                        in1=g_sb[gp][:],
                                                                        op0=ALU.add, op1=ALU.mult),
                                                 waits=hw)
                            slot += 1
                        h_ready = hstt_cnt[slot - 1]
                        for tt4 in range(4):
                            tt = c * 4 + tt4
                            dpar = dslot % 2
                            dw = [(s_v, h_ready)]
                            if dslot >= 2:
                                dw.append((s_v, dacc_cnt[dslot - 2]))
                            if ei < 0:
                                dw.append((s_sc, W33_DONE))

                            def emit_d(tensor, Wd_cur=Wd_cur, hpar=hpar, tt=tt, tt4=tt4, c=c, ei=ei, tb=tb, dpar=dpar):
                                r = None
                                for hc in range(2):
                                    if ei < 0:
                                        tg = tb * ntile_b + tt
                                        tensor.matmul(ps_d[dpar][:, hc * 512:(hc + 1) * 512],
                                                      w33[:, tg * 128:(tg + 1) * 128],
                                                      bd33[:, hc * 512:(hc + 1) * 512],
                                                      start=True, stop=False, skip_group_check=True)
                                    for ic in range(MI):
                                        r = tensor.matmul(ps_d[dpar][:, hc * 512:(hc + 1) * 512],
                                                          h_sb[hpar][:, ic, tt4 * 128:(tt4 + 1) * 128],
                                                          Wd_cur[:, ic, hc * 512:(hc + 1) * 512],
                                                          start=(ei >= 0 and ic == 0), stop=(ic == MI - 1),
                                                          skip_group_check=True)
                                return r
                            go("tensor", emit_d, waits=dw, inc=s_mm)
                            ndm = s_mm.n
                            hstt_cnt[(hpar, "lastdown")] = ndm
                            if ei < 0:
                                dacc_cnt[dslot] = vop(lambda v, tt=tt, dpar=dpar:
                                                      v.tensor_copy(out=acc_sb[:, tt, :], in_=ps_d[dpar][:]),
                                                      waits=[(s_mm, ndm)])
                            else:
                                dacc_cnt[dslot] = vop(lambda v, tt=tt, dpar=dpar, ei=ei, tg=tb * ntile_b + tt:
                                                      v.scalar_tensor_tensor(out=acc_sb[:, tt, :],
                                                                             in0=ps_d[dpar][:],
                                                                             scalar=w_sb[:, tg, ei:ei + 1],
                                                                             in1=acc_sb[:, tt, :],
                                                                             op0=ALU.mult, op1=ALU.add),
                                                      waits=[(s_mm, ndm)])
                            acc_last = locals().get("acc_last", {})
                            acc_last[tt] = dacc_cnt[dslot]
                            dslot += 1
                    if ei >= 0 and ei + 2 < NE:
                        eweight_ready[(tb, ei + 2)] = eload2(ei + 2, (ei + 2) % 2,
                                                             waits=[(s_mm, s_mm.n)])

                for tt in range(ntile_b):
                    tglob = tb * ntile_b + tt
                    cpar = tt % 2
                    yw_ = [(s_v, acc_last[tt])] if tt < 2 else [(s_v, comb_done[tt - 2])]
                    go("sync", lambda y, tglob=tglob, cpar=cpar:
                       y.dma_start(out=y3_sb[cpar][:], in_=ysp[tglob * 128:(tglob + 1) * 128, :]),
                       waits=yw_, inc=s_yr[cpar], k=16)
                    nyr = s_yr[cpar].n
                    go("sync", lambda y, tglob=tglob, cpar=cpar:
                       y.dma_start(out=x3_sb[cpar][:, 0, :],
                                   in_=xb[tglob * 128:(tglob + 1) * 128, :]),
                       waits=yw_, inc=s_x4[cpar], k=16)
                    nx4 = s_x4[cpar].n
                    comb_done = locals().get("comb_done", {})
                    ow = [] if out_i < 2 else [(s_out[cpar], 16 * (out_i // 2))]
                    vop(lambda v, tt=tt, cpar=cpar:
                        v.tensor_tensor(out=o_sb[cpar][:], in0=acc_sb[:, tt, :], in1=y3_sb[cpar][:],
                                        op=ALU.add),
                        waits=[(s_yr[cpar], nyr), (s_v, acc_last[tt])] + ow)
                    vop(lambda v, tt=tt, cpar=cpar, tglob=tglob:
                        v.scalar_tensor_tensor(out=o_sb[cpar][:], in0=x3_sb[cpar][:, 0, :],
                                               scalar=r1_all[:, tglob:tglob + 1], in1=o_sb[cpar][:],
                                               op0=ALU.mult, op1=ALU.add),
                        waits=[(s_x4[cpar], nx4)])
                    comb_done[tt] = vop(lambda v, tt=tt, cpar=cpar, tglob=tglob:
                                        v.tensor_scalar(out=o_sb[cpar][:], in0=o_sb[cpar][:],
                                                        scalar1=mur_all[:, tglob:tglob + 1], scalar2=None,
                                                        op0=ALU.subtract))
                    go("sync", lambda y, tglob=tglob, cpar=cpar:
                       y.dma_start(out=out[tglob * 128:(tglob + 1) * 128, :], in_=o_sb[cpar][:]),
                       waits=[(s_v, comb_done[tt])], inc=s_out[cpar], k=16)
                    out_i += 1
                flush_block()

        fw = [(s_out[0], s_out[0].n), (s_out[1], s_out[1].n)]
        if taps:
            fw.append((s_tap, s_tap.n))
        go("sync", lambda y: y.nop(), waits=fw)
        flush_block()

    return nc



def relayout_params(inputs):
    d = {}
    d["p_lng"] = np.ascontiguousarray(np.asarray(inputs["L0_attn_ln_g"]).reshape(MD, 128).T)
    d["p_lnb"] = np.ascontiguousarray(np.asarray(inputs["L0_attn_ln_b"]).reshape(MD, 128).T)
    d["p_binu"] = np.ascontiguousarray(np.asarray(inputs["L0_attn_b_in"])[0:E].reshape(ME, 128).T)
    d["p_bgs"] = np.ascontiguousarray(np.asarray(inputs["L0_ffn_bg_s"]).reshape(MI, 128).T)
    d["p_bus"] = np.ascontiguousarray(np.asarray(inputs["L0_ffn_bu_s"]).reshape(MI, 128).T)
    d["p_bge"] = np.ascontiguousarray(np.asarray(inputs["L0_ffn_bg"]).reshape(NE, MI, 128).transpose(2, 0, 1))
    d["p_bue"] = np.ascontiguousarray(np.asarray(inputs["L0_ffn_bu"]).reshape(NE, MI, 128).transpose(2, 0, 1))
    return d


def kernel(**inputs):
    L, TOWN = 16384, 4096
    x = np.asarray(inputs["L0_attn_x"])
    wnames = [k for k in inputs if k != "L0_attn_x"]
    nc = build(L=L, TOWN=TOWN, TB=1024, taps=False)
    derived = relayout_params(inputs)
    in_maps = []
    for c in range(8):
        b, q0 = c // 4, (c % 4) * TOWN
        xb_c = np.ascontiguousarray(np.concatenate([x[b, q0:], x[b, :q0]], axis=0))
        m = {"xb": xb_c}
        for k in wnames:
            m[k] = np.ascontiguousarray(np.asarray(inputs[k]))
        m.update(derived)
        in_maps.append(m)
    res = run_bass_kernel_spmd(nc, in_maps, core_ids=list(range(8)))
    outp = np.empty((2, L, D), dtype=np.float32)
    for c in range(8):
        b, q0 = c // 4, (c % 4) * TOWN
        outp[b, q0:q0 + TOWN] = res.results[c]["out"]
    return outp
